# version 11
# baseline (speedup 1.0000x reference)
import contextlib
import numpy as np
import concourse.bass as bass
import concourse.mybir as mybir
from concourse.bass_utils import run_bass_kernel_spmd

F32 = mybir.dt.float32
BF16 = mybir.dt.bfloat16
I32 = mybir.dt.int32
AF = mybir.ActivationFunctionType
ALU = mybir.AluOpType
AX = mybir.AxisListType.X

T = 4096
D = 1024
NTT = T // 128
NHT = D // 128
TC = 512
NTC = T // TC
HEADS = 16
DH = 64
NE = 8
FF = 4096
FH = FF // 2
NFH = FH // 128
ED = 2048
NET = ED // 128
NST = 16
KC = 4
I1 = 2816
NIT = I1 // 128
NEG = -1.0e30
NJT = 16
TPAD = NJT * 512

IN, OUT, INT = "ExternalInput", "ExternalOutput", "Internal"

DBG_REQ = set()
L1_STOP = ""
L1_SCANMODE = ""


class Ctx:
    def __init__(self, nc, es):
        self.nc = nc
        self.es = es
        self.io = {}
        self.sems = {}
        self.cnt = {}
        for name in ["qL", "qW", "qS", "qS2", "qA", "tP", "tP2", "tE", "tE2",
                     "tS", "tS2", "tV", "tX", "gate", "tCv", "tCs"]:
            self.sems[name] = es.enter_context(nc.semaphore(name))
            self.cnt[name] = 0

    def sem(self, name):
        return self.sems[name]

    def inc(self, name, ins, k=1):
        ins.then_inc(self.sems[name], k)
        self.cnt[name] += k
        return self.cnt[name]

    def bump(self, name, k=1):
        self.cnt[name] += k
        return self.cnt[name]

    def wait(self, eng, name, val):
        eng.wait_ge(self.sems[name], val)

    def chain(self, eng, name, ins):
        v = self.inc(name, ins)
        eng.wait_ge(self.sems[name], v)
        return ins

    def vchain(self, eng, ins):
        return self.chain(eng, "tCv", ins)

    def schain(self, eng, ins):
        return self.chain(eng, "tCs", ins)

    def snap(self, *names):
        if len(names) == 1:
            return self.cnt[names[0]]
        return tuple(self.cnt[n] for n in names)


class DSlots:
    def __init__(self, c, name, n):
        self.c = c
        self.n = n
        self.sems = [c.es.enter_context(c.nc.semaphore(f"{name}{i}")) for i in range(n)]
        self.cnt = [0] * n

    def issue(self, ins, slot, units=16):
        ins.then_inc(self.sems[slot % self.n], units)
        self.cnt[slot % self.n] += units
        return self.cnt[slot % self.n]

    def wait_units(self, eng, slot, units):
        eng.wait_ge(self.sems[slot % self.n], units)

    def wait_all(self, eng):
        for i in range(self.n):
            eng.wait_ge(self.sems[i], self.cnt[i])

    def snap(self):
        return list(self.cnt)


def dram(c, name, shape, dtype, kind):
    if kind == INT and name in DBG_REQ:
        kind = OUT
    ap = c.nc.dram_tensor(name, shape, dtype, kind=kind).ap()
    c.io[name] = ap
    return ap


def sbuf(c, stack, name, shape, dtype):
    return stack.enter_context(c.nc.sbuf_tensor(name, shape, dtype))


def psum(c, stack, name, shape, dtype=F32):
    return stack.enter_context(c.nc.psum_tensor(name, shape, dtype))


def build(do_l0=True, do_l1=True):
    nc = bass.Bass("TRN2", target_bir_lowering=False, detect_race_conditions=False)
    es = contextlib.ExitStack()
    c = Ctx(nc, es)

    if do_l0:
        dram(c, "x", [T, D], F32, IN)
        for w in ["wq", "wk", "wv", "wo"]:
            dram(c, w, [D, D], F32, IN)
        dram(c, "bq_t", [128, NHT], F32, IN)
        dram(c, "bk_t", [128, NHT], F32, IN)
        for r in ["bv_r", "bo_r", "gamma_r", "beta_r"]:
            dram(c, r, [1, D], F32, IN)
        dram(c, "gate_w", [D, NE], F32, IN)
        dram(c, "gate_b_r", [1, NE], F32, IN)
        dram(c, "w1", [NE * 2 * 128, NHT * FH], F32, IN)
        dram(c, "b1r", [NE * 128, 32], F32, IN)
        dram(c, "w2", [NE * 128, (FF // 128) * D], F32, IN)
        dram(c, "b2", [NE, D], F32, IN)
        dram(c, "xn0fm_h", [D, T], BF16, INT)
        dram(c, "xn0tm_h", [T + 1, D], BF16, INT)
        dram(c, "moe2_h", [T + 128, D], F32, INT)
        dram(c, "invh", [TPAD, 1], I32, INT)
        dram(c, "wslh", [TPAD, 1], F32, INT)
        dram(c, "qT_h", [D, T], BF16, INT)
        dram(c, "kT_h", [D, T], BF16, INT)
        dram(c, "v_h", [T, D], BF16, INT)
        dram(c, "attn_h", [T, D], BF16, INT)
        dram(c, "moe_h", [T, D], F32, INT)
    x1_kind = INT if (do_l0 and do_l1) else (OUT if do_l0 else IN)
    if x1_kind == INT and "x1" in DBG_REQ:
        x1_kind = OUT
    c.io["x1"] = nc.dram_tensor("x1", [T, D], F32, kind=x1_kind).ap()
    if do_l1:
        dram(c, "inproj_t", [D, 2 * ED], F32, IN)
        dram(c, "convw_t", [128, NET * KC], F32, IN)
        dram(c, "convb_t", [128, NET], F32, IN)
        dram(c, "xproj_pad", [ED, 128], F32, IN)
        dram(c, "dtw_t", [64, ED], F32, IN)
        dram(c, "dtb_t", [128, NET], F32, IN)
        dram(c, "alog_t", [128, NET * NST], F32, IN)
        dram(c, "dvec_t", [128, NET], F32, IN)
        dram(c, "outproj_t", [ED, D], F32, IN)
        dram(c, "wgate", [I1, D], F32, IN)
        dram(c, "wup", [I1, D], F32, IN)
        dram(c, "wdown", [D, I1], F32, IN)
        dram(c, "out", [T, D], F32, OUT)
        dram(c, "xn1_h", [T, D], F32, INT)
        dram(c, "xn1fm_h", [D, T], BF16, INT)
        dram(c, "hsr_h", [ED, T], BF16, INT)
        dram(c, "hs2_h", [ED, T], BF16, INT)
        dram(c, "y2_h", [ED, T], BF16, INT)
        dram(c, "gate_h", [ED, T], BF16, INT)
        dram(c, "mam_h", [T, D], F32, INT)
        dram(c, "hqT_h", [I1, T], BF16, INT)
        dram(c, "xq_h", [D, T], BF16, INT)
        dram(c, "wgq_h", [I1, D], BF16, INT)
        dram(c, "wuq_h", [I1, D], BF16, INT)
        dram(c, "wdq_h", [D, I1], BF16, INT)

    c.ldA = DSlots(c, "ldA", 2)
    c.ldB = DSlots(c, "ldB", 3)
    c.stA = DSlots(c, "stA", 4)
    c.wLa = DSlots(c, "wLa", 2)
    c.w2s = DSlots(c, "w2s", 2)
    c.b2s = DSlots(c, "b2s", 1)
    c.grpA = DSlots(c, "grpA", 1)
    c.grpB = DSlots(c, "grpB", 1)
    c.qAs = DSlots(c, "qAs", 8)
    c.pwld = DSlots(c, "pwld", 2)
    c.stG = DSlots(c, "stG", 2)
    c.grpC = DSlots(c, "grpC", 1)
    c.ldC = DSlots(c, "ldC", 2)

    idn = sbuf(c, es, "idn", [128, 128], BF16)
    iot = sbuf(c, es, "iot", [128, 512], I32)
    ones128 = sbuf(c, es, "ones128", [1, 128], BF16)
    onescol = sbuf(c, es, "onescol", [64, 1], BF16)
    ones128f = sbuf(c, es, "ones128f", [1, 128], F32)
    cpool = sbuf(c, es, "cpool", [128, 8], F32)
    CONSTS = [1e-5, -0.5, 0.125, -1.0, 0.25]
    for j, vv in enumerate(CONSTS):
        nc.const_aps.register(F32, vv, cpool[:, j:j + 1])

    with nc.Block("init") as blk:
        @blk.gpsimd
        def _(g):
            g.iota(iot[:], [[1, 512]], channel_multiplier=-1)
            g.memset(ones128[:], 1.0)
            g.memset(ones128f[:], 1.0)
            for j, vv in enumerate(CONSTS):
                g.memset(cpool[:, j:j + 1], vv)
            c.inc("tX", g.memset(onescol[:], 1.0))

        @blk.vector
        def _(v):
            c.wait(v, "tX", c.cnt["tX"])
            c.inc("tX", v.tensor_single_scalar(idn[:], iot[:, 0:128], 0, ALU.is_equal))

    if do_l0:
        build_l0(c, idn, iot, ones128, ones128f, onescol)
    if do_l1:
        build_l1(c, idn, iot, ones128, ones128f, onescol)

    es.close()
    return nc, c


def emit_ln_stats_vec(c, v, src, sq, stats):
    c.vchain(v, v.reduce_sum(stats[:, 0:1], src, axis=AX, negate=True))
    c.vchain(v, v.reduce_sum(stats[:, 1:2], sq, axis=AX))
    c.vchain(v, v.tensor_scalar(stats[:, 2:3], stats[:, 0:1], 1.0 / D, None, ALU.mult))
    c.vchain(v, v.tensor_scalar(stats[:, 3:4], stats[:, 1:2], 1.0 / D, None, ALU.mult))
    c.vchain(v, v.tensor_tensor(stats[:, 4:5], stats[:, 2:3], stats[:, 2:3], ALU.mult))
    c.vchain(v, v.tensor_tensor(stats[:, 4:5], stats[:, 3:4], stats[:, 4:5], ALU.subtract))
    return c.inc("tV", v.tensor_copy(stats[:, 5:6], stats[:, 4:5]))


def emit_ln_scalar(c, s, stats):
    c.schain(s, s.activation(stats[:, 6:7], stats[:, 5:6], AF.Ln, bias=1e-5))
    return c.inc("tS2", s.activation(stats[:, 6:7], stats[:, 6:7], AF.Exp, scale=-0.5))


def build_l0(c, idn, iot, ones128, ones128f, onescol):
    nc = c.nc
    io = c.io

    with contextlib.ExitStack() as ph:
        xn0fm = sbuf(c, ph, "xn0fm", [128, NHT * T], BF16)

        with contextlib.ExitStack() as p1s:
            xbuf = sbuf(c, p1s, "p1_xbuf", [128, 2 * D], F32)
            sq = sbuf(c, p1s, "p1_sq", [128, 2 * D], F32)
            stats = sbuf(c, p1s, "p1_stats", [128, 2 * 8], F32)
            xn_tm = sbuf(c, p1s, "p1_xn", [128, 2 * D], BF16)
            zrow = sbuf(c, p1s, "p1_zrow", [1, D], BF16)
            pst = [psum(c, p1s, f"p1_pst{i}", [128, 512], BF16) for i in range(2)]

            with nc.Block("p1") as blk:
                qL0, tS0, tS20, tV0, tX0, tP0, tE0 = c.snap(
                    "qL", "tS", "tS2", "tV", "tX", "tP", "tE")

                ldA0 = c.ldA.snap()
                stG0 = c.stG.snap()

                @blk.sync
                def _(sy):
                    sy.wait_ge(c.sem("tX"), 2)
                    for i in range(NTT):
                        if i >= 2:
                            c.wait(sy, "tX", tX0 + (i - 2) + 1)
                        c.ldA.issue(sy.dma_start(out=xbuf[:, (i % 2) * D:(i % 2 + 1) * D],
                                                 in_=io["x"][i * 128:(i + 1) * 128, :]), i)

                @blk.scalar
                def _(s):
                    for i in range(NTT):
                        b = i % 2
                        c.ldA.wait_units(s, i, ldA0[i % 2] + 16 * (i // 2 + 1))
                        c.inc("tS", s.activation(sq[:, b * D:(b + 1) * D],
                                                 xbuf[:, b * D:(b + 1) * D], AF.Square))
                        c.wait(s, "tV", tV0 + i + 1)
                        emit_ln_scalar(c, s, stats[:, b * 8:(b + 1) * 8])

                def emit_evac(v, i):
                    for half in range(2):
                        k = i * 2 + half
                        c.wait(v, "tP", tP0 + k + 1)
                        for q in range(4):
                            hh = half * 4 + q
                            ins = v.tensor_copy(
                                xn0fm[:, hh * T + i * 128:hh * T + (i + 1) * 128],
                                pst[half][:, q * 128:(q + 1) * 128])
                        c.inc("tE", ins)

                @blk.vector
                def _(v):
                    for i in range(NTT):
                        b = i % 2
                        st_ = stats[:, b * 8:(b + 1) * 8]
                        c.wait(v, "tS", tS0 + i + 1)
                        emit_ln_stats_vec(c, v, xbuf[:, b * D:(b + 1) * D],
                                          sq[:, b * D:(b + 1) * D], st_)
                        c.wait(v, "tS2", tS20 + i + 1)
                        if i >= 2:
                            c.wait(v, "tP", tP0 + 2 * (i - 1))
                            c.stG.wait_units(v, i, stG0[i % 2] + 16 * (i // 2))
                        c.inc("tX", v.tensor_scalar(xn_tm[:, b * D:(b + 1) * D],
                                                    xbuf[:, b * D:(b + 1) * D],
                                                    st_[:, 2:3], st_[:, 6:7],
                                                    ALU.add, ALU.mult))
                        if i >= 1:
                            emit_evac(v, i - 1)
                    emit_evac(v, NTT - 1)

                @blk.tensor
                def _(t):
                    t.wait_ge(c.sem("tX"), 2)
                    for i in range(NTT):
                        c.wait(t, "tX", tX0 + i + 1)
                        for half in range(2):
                            k = i * 2 + half
                            if k >= 2:
                                c.wait(t, "tE", tE0 + k - 1)
                            for q in range(4):
                                hh = half * 4 + q
                                ins = t.transpose(
                                    pst[half][:, q * 128:(q + 1) * 128],
                                    xn_tm[:, (i % 2) * D + hh * 128:(i % 2) * D + (hh + 1) * 128],
                                    idn[:])
                            c.inc("tP", ins)

                @blk.gpsimd
                def _(g):
                    g.memset(zrow[:], 0.0)
                    for i in range(NTT):
                        g.wait_ge(c.sem("tX"), tX0 + i + 1)
                        c.stG.issue(g.dma_start(
                            out=io["xn0tm_h"][i * 128:(i + 1) * 128, :],
                            in_=xn_tm[:, (i % 2) * D:(i % 2 + 1) * D]), i)
                    c.grpB.issue(g.dma_start(out=io["xn0tm_h"][T:T + 1, :],
                                             in_=zrow[:]), 0)
                    g.wait_ge(c.sem("tE"), tE0 + 2 * NTT)
                    for hh in range(NHT):
                        c.grpB.issue(g.dma_start(out=io["xn0fm_h"][hh * 128:(hh + 1) * 128, :],
                                                 in_=xn0fm[:, hh * T:(hh + 1) * T]), 0)
                    c.stG.wait_all(g)
                    c.grpB.wait_all(g)
                    g.sem_inc(c.sem("gate"), 1)
                    c.bump("gate")

        with contextlib.ExitStack() as p2:
            wq_sb = sbuf(c, p2, "wq_sb", [128, NHT * D], BF16)
            wk_sb = sbuf(c, p2, "wk_sb", [128, NHT * D], BF16)
            wv_sb = sbuf(c, p2, "wv_sb", [128, NHT * D], BF16)
            bq_sb = sbuf(c, p2, "bq_sb", [128, NHT], F32)
            bk_sb = sbuf(c, p2, "bk_sb", [128, NHT], F32)
            bvbc = sbuf(c, p2, "bvbc", [128, D], F32)
            bv_row = sbuf(c, p2, "bv_row", [1, D], F32)
            stage = sbuf(c, p2, "p2a_stage", [128, 4 * 512], BF16)
            psq = [psum(c, p2, f"psq{i}", [128, 512]) for i in range(4)]

            NQK = 2 * NHT * NTC

            with nc.Block("p2a") as blk:
                qW0, tP0, tP20, tE0, tE20, qS0, tX0 = c.snap(
                    "qW", "tP", "tP2", "tE", "tE2", "qS", "tX")
                stA0 = c.stA.snap()
                gate_p1 = c.cnt["gate"]

                @blk.gpsimd
                def _(g):
                    for name, buf in [("wq", wq_sb), ("wk", wk_sb), ("wv", wv_sb)]:
                        for hh in range(NHT):
                            c.grpA.issue(g.dma_start(out=buf[:, hh * D:(hh + 1) * D],
                                                     in_=io[name][hh * 128:(hh + 1) * 128, :]), 0)
                    c.grpA.issue(g.dma_start(out=bq_sb[:], in_=io["bq_t"]), 0)
                    c.grpA.issue(g.dma_start(out=bk_sb[:], in_=io["bk_t"]), 0)
                    c.grpA.issue(g.dma_start(out=bv_row[:], in_=io["bv_r"]), 0)

                @blk.tensor
                def _(t):
                    c.grpA.wait_all(t)
                    for dch in range(2):
                        c.inc("tP2", t.matmul(psq[dch][:], ones128f[:],
                                              bv_row[:, dch * 512:(dch + 1) * 512],
                                              start=True, stop=True))
                    c.wait(t, "tX", tX0 + 1)
                    k = 0
                    for wsb in [wq_sb, wk_sb]:
                        for dt_ in range(NHT):
                            for tcc in range(NTC):
                                if k >= 4:
                                    c.wait(t, "tE", tE0 + (k - 4) + 1)
                                for hh in range(NHT):
                                    ins = t.matmul(
                                        psq[k % 4][:],
                                        wsb[:, hh * D + dt_ * 128:hh * D + (dt_ + 1) * 128],
                                        xn0fm[:, hh * T + tcc * TC:hh * T + (tcc + 1) * TC],
                                        start=(hh == 0), stop=(hh == NHT - 1))
                                c.inc("tP", ins)
                                k += 1
                    for i in range(NTT):
                        for dch in range(2):
                            if k >= 4:
                                kprev = k - 4
                                if kprev < NQK:
                                    c.wait(t, "tE", tE0 + kprev + 1)
                                else:
                                    c.wait(t, "tE2", tE20 + (kprev - NQK) + 1)
                            for hh in range(NHT):
                                ins = t.matmul(
                                    psq[k % 4][:],
                                    xn0fm[:, hh * T + i * 128:hh * T + (i + 1) * 128],
                                    wv_sb[:, hh * D + dch * 512:hh * D + (dch + 1) * 512],
                                    start=(hh == 0), stop=(hh == NHT - 1))
                            c.inc("tP", ins)
                            k += 1

                @blk.scalar
                def _(s):
                    k = 0
                    for bsb in [bq_sb, bk_sb]:
                        for dt_ in range(NHT):
                            for tcc in range(NTC):
                                s.wait_ge(c.sem("tP"), tP0 + k + 1)
                                if k >= 4:
                                    c.stA.wait_units(s, k, stA0[k % 4] + 16 * (k // 4))
                                c.inc("tE", s.activation(
                                    stage[:, (k % 4) * 512:(k % 4 + 1) * 512],
                                    psq[k % 4][:], AF.Identity, bias=bsb[:, dt_:dt_ + 1]))
                                k += 1

                @blk.vector
                def _(v):
                    c.wait(v, "tP2", tP20 + 2)
                    v.tensor_copy(bvbc[:, 0:512], psq[0][:])
                    c.inc("tX", v.tensor_copy(bvbc[:, 512:1024], psq[1][:]))
                    v.wait_ge(c.sem("tE"), tE0 + NQK)
                    for i in range(NTT):
                        for dch in range(2):
                            kk = NQK + i * 2 + dch
                            v.wait_ge(c.sem("tP"), tP0 + kk + 1)
                            if kk >= 4:
                                c.stA.wait_units(v, kk, stA0[kk % 4] + 16 * (kk // 4))
                            c.inc("tE2", v.tensor_tensor(
                                stage[:, (kk % 4) * 512:(kk % 4 + 1) * 512],
                                psq[kk % 4][:], bvbc[:, dch * 512:(dch + 1) * 512], ALU.add))

                @blk.sync
                def _(sy):
                    k = 0
                    for hbm in [io["qT_h"], io["kT_h"]]:
                        for dt_ in range(NHT):
                            for tcc in range(NTC):
                                sy.wait_ge(c.sem("tE"), tE0 + k + 1)
                                c.stA.issue(sy.dma_start(
                                    out=hbm[dt_ * 128:(dt_ + 1) * 128, tcc * TC:(tcc + 1) * TC],
                                    in_=stage[:, (k % 4) * 512:(k % 4 + 1) * 512]), k)
                                k += 1
                    for i in range(NTT):
                        for dch in range(2):
                            sy.wait_ge(c.sem("tE2"), tE20 + i * 2 + dch + 1)
                            c.stA.issue(sy.dma_start(
                                out=io["v_h"][i * 128:(i + 1) * 128, dch * 512:(dch + 1) * 512],
                                in_=stage[:, (k % 4) * 512:(k % 4 + 1) * 512]), k)
                            k += 1
                    c.stA.wait_all(sy)
                    sy.sem_inc(c.sem("gate"), 1)
                    c.bump("gate")

    gate_p2a = c.cnt["gate"]
    with contextlib.ExitStack() as p2:
        q_aug = sbuf(c, p2, "q_aug", [65, T], BF16)
        k_aug = sbuf(c, p2, "k_aug", [65, T], BF16)
        v_sb = sbuf(c, p2, "v_sb", [128, NTT * 65], BF16)
        q2 = sbuf(c, p2, "q2", [64, T], BF16)
        nrm = sbuf(c, p2, "nrm", [1, 2 * T + 8], BF16)
        arow = None
        ptb = sbuf(c, p2, "ptb", [128, 4 * 512], BF16)
        rlb = sbuf(c, p2, "rlb", [1, 512], F32)
        rlbc = sbuf(c, p2, "rlbc", [64, 512], F32)
        maskT = sbuf(c, p2, "maskT", [128, 4 * 512], F32)
        oT_sb = sbuf(c, p2, "oT_sb", [128, NHT * T], BF16)
        ones64f = sbuf(c, p2, "ones64f", [1, 64], F32)
        p2ps = contextlib.ExitStack()
        psS = [psum(c, p2ps, f"psS{i}", [128, 512]) for i in range(4)]
        psO = [psum(c, p2ps, f"psO{i}", [65, 512]) for i in range(2)]
        psN = psum(c, p2ps, "psN", [64, 512])

        with nc.Block("p2b_init") as blk:
            @blk.vector
            def _(v):
                v.wait_ge(c.sem("gate"), gate_p2a)
                for j in range(4):
                    ins = v.tensor_scalar(maskT[:, j * 512:(j + 1) * 512], iot[:],
                                          j * 128, NEG, ALU.is_lt, ALU.mult)
                c.inc("tX", ins)

            @blk.gpsimd
            def _(g):
                g.wait_ge(c.sem("gate"), gate_p2a)
                g.memset(k_aug[64:65, :], 1.0)
                g.memset(ones64f[:], 1.0)
                c.inc("tX", g.memset(v_sb[:], 1.0))

        for h in range(HEADS):
            attn_head(c, h, q_aug, k_aug, v_sb, q2, nrm, arow, ptb, rlb, rlbc,
                      maskT, oT_sb, ones64f, psS, psO, psN, ones128, onescol)
        p2ps.close()

        with contextlib.ExitStack() as pc:
            wo_sb = sbuf(c, pc, "wo_sb", [128, NHT * D], BF16)
            rows = sbuf(c, pc, "p2c_rows", [1, 3 * D], F32)
            bobc = sbuf(c, pc, "bobc", [128, D], F32)
            gbc = sbuf(c, pc, "gbc", [128, D], F32)
            bbc = sbuf(c, pc, "bbc", [128, D], F32)
            ybuf = sbuf(c, pc, "ybuf", [128, 2 * D], F32)
            sq = sbuf(c, pc, "p2c_sq", [128, 2 * D], F32)
            stats = sbuf(c, pc, "p2c_stats", [128, 2 * 8], F32)
            aout = sbuf(c, pc, "aout", [128, 2 * D], BF16)
            psY = [psum(c, pc, f"psY{i}", [128, 512]) for i in range(4)]

            oT_done = c.cnt["tE2"]

            with nc.Block("p2c") as blk:
                qW0, tP0, tP20, tE0, tV0, tS0, tS20, tX0, qS0, tE20 = c.snap(
                    "qW", "tP", "tP2", "tE", "tV", "tS", "tS2", "tX", "qS", "tE2")
                stA0 = c.stA.snap()

                @blk.gpsimd
                def _(g):
                    for hh in range(NHT):
                        c.grpA.issue(g.dma_start(out=wo_sb[:, hh * D:(hh + 1) * D],
                                                 in_=io["wo"][hh * 128:(hh + 1) * 128, :]), 0)
                    c.grpA.issue(g.dma_start(out=rows[:, 0:D], in_=io["bo_r"]), 0)
                    c.grpA.issue(g.dma_start(out=rows[:, D:2 * D], in_=io["gamma_r"]), 0)
                    c.grpA.issue(g.dma_start(out=rows[:, 2 * D:3 * D], in_=io["beta_r"]), 0)

                @blk.tensor
                def _(t):
                    c.grpA.wait_all(t)
                    t.wait_ge(c.sem("tE2"), oT_done)
                    for j in range(3):
                        if j >= 2:
                            c.wait(t, "tX", tX0 + j - 1)
                        for dch in range(2):
                            ins = t.matmul(psY[(j * 2 + dch) % 4][:], ones128f[:],
                                           rows[:, j * D + dch * 512:j * D + (dch + 1) * 512],
                                           start=True, stop=True)
                        c.inc("tP2", ins)
                    c.wait(t, "tX", tX0 + 3)
                    for i in range(NTT):
                        for dch in range(2):
                            k = i * 2 + dch
                            if k >= 4:
                                c.wait(t, "tE", tE0 + (k - 4) + 1)
                            for hh in range(NHT):
                                ins = t.matmul(
                                    psY[k % 4][:],
                                    oT_sb[:, hh * T + i * 128:hh * T + (i + 1) * 128],
                                    wo_sb[:, hh * D + dch * 512:hh * D + (dch + 1) * 512],
                                    start=(hh == 0), stop=(hh == NHT - 1))
                            c.inc("tP", ins)

                @blk.vector
                def _(v):
                    for j, dst in enumerate([bobc, gbc, bbc]):
                        c.wait(v, "tP2", tP20 + j + 1)
                        v.tensor_copy(dst[:, 0:512], psY[(j * 2) % 4][:])
                        c.inc("tX", v.tensor_copy(dst[:, 512:1024], psY[(j * 2 + 1) % 4][:]))
                    for i in range(NTT):
                        b = i % 2
                        st_ = stats[:, b * 8:(b + 1) * 8]
                        for dch in range(2):
                            k = i * 2 + dch
                            c.wait(v, "tP", tP0 + k + 1)
                            ins = v.tensor_tensor(
                                ybuf[:, b * D + dch * 512:b * D + (dch + 1) * 512],
                                psY[k % 4][:], bobc[:, dch * 512:(dch + 1) * 512], ALU.add)
                            c.vchain(v, ins)
                            v.sem_inc(c.sem("tE"), 1)
                            c.bump("tE")
                        c.wait(v, "tS", tS0 + i + 1)
                        emit_ln_stats_vec(c, v, ybuf[:, b * D:(b + 1) * D],
                                          sq[:, b * D:(b + 1) * D], st_)
                        c.wait(v, "tS2", tS20 + i + 1)
                        c.vchain(v, v.tensor_scalar(sq[:, b * D:(b + 1) * D], ybuf[:, b * D:(b + 1) * D],
                                        st_[:, 2:3], st_[:, 6:7], ALU.add, ALU.mult))
                        c.vchain(v, v.tensor_tensor(sq[:, b * D:(b + 1) * D], sq[:, b * D:(b + 1) * D],
                                        gbc[:], ALU.mult))
                        if i >= 2:
                            c.stA.wait_units(v, i % 2, stA0[i % 2] + 16 * (i // 2))
                        c.inc("tE2", v.tensor_tensor(aout[:, b * D:(b + 1) * D],
                                                     sq[:, b * D:(b + 1) * D], bbc[:], ALU.add))

                @blk.scalar
                def _(s):
                    for i in range(NTT):
                        b = i % 2
                        s.wait_ge(c.sem("tE"), tE0 + 2 * (i + 1))
                        c.inc("tS", s.activation(sq[:, b * D:(b + 1) * D],
                                                 ybuf[:, b * D:(b + 1) * D], AF.Square))
                        c.wait(s, "tV", tV0 + i + 1)
                        emit_ln_scalar(c, s, stats[:, b * 8:(b + 1) * 8])

                @blk.sync
                def _(sy):
                    for i in range(NTT):
                        sy.wait_ge(c.sem("tE2"), tE20 + i + 1)
                        c.stA.issue(sy.dma_start(out=io["attn_h"][i * 128:(i + 1) * 128, :],
                                                 in_=aout[:, (i % 2) * D:(i % 2 + 1) * D]), i % 2)
                    c.stA.wait_all(sy)
                    sy.sem_inc(c.sem("gate"), 1)
                    c.bump("gate")

    moe_p3(c, ones128, ones128f, idn, iot)

    gate_p3 = c.cnt["gate"]
    with contextlib.ExitStack() as ph:
        xb = sbuf(c, ph, "p4_x", [128, 2 * D], F32)
        ab = sbuf(c, ph, "p4_a", [128, 2 * D], BF16)
        mb = sbuf(c, ph, "p4_m", [128, 2 * D], F32)
        x1b = sbuf(c, ph, "p4_o", [128, 2 * D], F32)

        with nc.Block("p4") as blk:
            qL0, tV0, qS0 = c.snap("qL", "tV", "qS")

            ldA0 = c.ldA.snap()
            stG0 = c.stG.snap()

            @blk.sync
            def _(sy):
                sy.wait_ge(c.sem("gate"), gate_p3)
                for i in range(NTT):
                    b = i % 2
                    if i >= 2:
                        c.wait(sy, "tV", tV0 + (i - 2) + 1)
                    c.ldA.issue(sy.dma_start(out=xb[:, b * D:(b + 1) * D],
                                             in_=io["x"][i * 128:(i + 1) * 128, :]), i)
                    c.ldA.issue(sy.dma_start(out=ab[:, b * D:(b + 1) * D],
                                             in_=io["attn_h"][i * 128:(i + 1) * 128, :]), i)
                    c.ldA.issue(sy.dma_start(out=mb[:, b * D:(b + 1) * D],
                                             in_=io["moe2_h"][i * 128:(i + 1) * 128, :]), i)

            @blk.vector
            def _(v):
                for i in range(NTT):
                    b = i % 2
                    c.ldA.wait_units(v, i, ldA0[i % 2] + 48 * (i // 2 + 1))
                    if i >= 2:
                        c.stG.wait_units(v, i % 2, stG0[i % 2] + 16 * (i // 2))
                    c.vchain(v, v.tensor_tensor(x1b[:, b * D:(b + 1) * D], xb[:, b * D:(b + 1) * D],
                                    mb[:, b * D:(b + 1) * D], ALU.add))
                    c.inc("tV", v.tensor_tensor(x1b[:, b * D:(b + 1) * D],
                                                x1b[:, b * D:(b + 1) * D],
                                                ab[:, b * D:(b + 1) * D], ALU.add))

            @blk.gpsimd
            def _(g):
                for i in range(NTT):
                    g.wait_ge(c.sem("tV"), tV0 + i + 1)
                    c.stG.issue(g.dma_start(out=io["x1"][i * 128:(i + 1) * 128, :],
                                            in_=x1b[:, (i % 2) * D:(i % 2 + 1) * D]), i % 2)
                c.stG.wait_all(g)
                g.sem_inc(c.sem("gate"), 1)
                c.bump("gate")

    gfin = c.cnt["gate"]
    with nc.Block("l0_fin") as blk:
        @blk.sync
        def _(sy):
            sy.wait_ge(c.sem("gate"), gfin)


def attn_head(c, h, q_aug, k_aug, v_sb, q2, nrm, arow, ptb, rlb, rlbc,
              maskT, oT_sb, ones64f, psS, psO, psN, ones128, onescol):
    nc = c.nc
    io = c.io
    ht, hp = h // 2, (h % 2) * 64

    with nc.Block(f"h{h}a") as blk:
        qL0, tS0, tP20, tV0 = c.snap("qL", "tS", "tP2", "tV")
        prev_done = c.cnt["tE2"]
        initX = c.cnt["tX"]

        @blk.sync
        def _(sy):
            sy.wait_ge(c.sem("tX"), initX)
            if h > 0:
                sy.wait_ge(c.sem("tE2"), prev_done)
            c.ldB.issue(sy.dma_start(out=q_aug[0:64, :],
                                     in_=io["qT_h"][h * DH:(h + 1) * DH, :]), 0)
            c.ldB.issue(sy.dma_start(out=k_aug[0:64, :],
                                     in_=io["kT_h"][h * DH:(h + 1) * DH, :]), 1)
            c.ldB.issue(sy.dma_start(
                out=v_sb[:].rearrange("p (a f) -> p a f", a=NTT)[:, :, 0:64],
                in_=io["v_h"].rearrange("(a p) d -> p a d", p=128)[:, :, h * DH:(h + 1) * DH]), 2)

        @blk.scalar
        def _(s):
            c.ldB.wait_units(s, 0, 16 * (h + 1))
            c.inc("tS", s.activation(q2[:], q_aug[0:64, :], AF.Square))
            for cc in range(NTC):
                s.wait_ge(c.sem("tP2"), tP20 + cc + 1)
                c.inc("tS", s.activation(nrm[0:1, cc * 512:(cc + 1) * 512],
                                         psN[0:1, :], AF.Sqrt))
            c.ldB.wait_units(s, 1, 16 * (h + 1))
            c.inc("tS", s.activation(q2[:], k_aug[0:64, :], AF.Square))
            for cc in range(NTC):
                s.wait_ge(c.sem("tP2"), tP20 + NTC + cc + 1)
                c.inc("tS", s.activation(nrm[0:1, T + cc * 512:T + (cc + 1) * 512],
                                         psN[0:1, :], AF.Sqrt))

        @blk.tensor
        def _(t):
            c.wait(t, "tS", tS0 + 1)
            for cc in range(NTC):
                if cc > 0:
                    c.wait(t, "tS", tS0 + 1 + cc)
                c.inc("tP2", t.matmul(psN[0:1, :], onescol[:],
                                      q2[:, cc * 512:(cc + 1) * 512],
                                      start=True, stop=True))
            c.wait(t, "tS", tS0 + NTC + 2)
            for cc in range(NTC):
                if cc > 0:
                    c.wait(t, "tS", tS0 + NTC + 2 + cc)
                c.inc("tP2", t.matmul(psN[0:1, :], onescol[:],
                                      q2[:, cc * 512:(cc + 1) * 512],
                                      start=True, stop=True))

        @blk.vector
        def _(v):
            c.wait(v, "tS", tS0 + 2 * NTC + 2)
            c.vchain(v, v.reduce_max(rlb[0:1, 0:1], nrm[0:1, T:2 * T], axis=AX))
            c.inc("tV", v.tensor_scalar(q_aug[64:65, :], nrm[0:1, 0:T],
                                        rlb[0:1, 0:1], -1.0,
                                        ALU.mult, ALU.mult))

    kt_list = []
    for g in range(NTC):
        for kt in range(4 * g + 4):
            j = kt - 4 * g
            kt_list.append((g, kt, j if 0 <= j <= 3 else None))
    NK = len(kt_list)
    g_first = {}
    idx = 0
    for g in range(NTC):
        g_first[g] = idx
        idx += 4 * g + 4

    def av_incs(n_tiles):
        return n_tiles

    with nc.Block(f"h{h}b") as blk:
        sS0, sM0, sE0, sA0, sV0, sP0, sO0 = c.snap(
            "tP", "tX", "tS", "tP2", "tV", "tS2", "tE2")

        def emit_av(t, li):
            g, kt, j = kt_list[li]
            first = (kt == 0)
            last = (kt == 4 * g + 3)
            if first and g >= 2:
                c.wait(t, "tE2", sO0 + (g - 2) + 1)
            c.wait(t, "tS", sE0 + li + 1)
            c.inc("tP2", t.matmul(psO[g % 2][:],
                                  v_sb[:, kt * 65:(kt + 1) * 65],
                                  ptb[:, (li % 4) * 512:(li % 4 + 1) * 512],
                                  start=first, stop=last))
            if last:
                c.wait(t, "tV", sV0 + g + 1)
                c.inc("tS2", t.matmul(psN[:], ones64f[:], rlb[:],
                                      start=True, stop=True))

        @blk.tensor
        def _(t):
            c.wait(t, "tV", sV0)
            c.ldB.wait_units(t, 2, 16 * (h + 1))
            for li in range(NK):
                g, kt, j = kt_list[li]
                if li >= 4:
                    c.wait(t, "tS", sE0 + (li - 4) + 1)
                c.inc("tP", t.matmul(psS[li % 4][:],
                                     k_aug[:, kt * 128:(kt + 1) * 128],
                                     q_aug[:, g * 512:(g + 1) * 512],
                                     start=True, stop=True))
                if li - 2 >= 0:
                    emit_av(t, li - 2)
            emit_av(t, NK - 2)
            emit_av(t, NK - 1)

        @blk.vector
        def _(v):
            nm = 0
            for g in range(NTC):
                for jj in range(4):
                    li = g_first[g] + 4 * g + jj
                    v.wait_ge(c.sem("tP"), sS0 + li + 1)
                    nm += 1
                    c.inc("tX", v.tensor_tensor(psS[li % 4][:], psS[li % 4][:],
                                                maskT[:, jj * 512:(jj + 1) * 512],
                                                ALU.add))
                v.wait_ge(c.sem("tP2"), sA0 + g_first[g] + 4 * g + 4)
                c.inc("tV", v.reciprocal(rlb[:], psO[g % 2][64:65, :]))
                c.wait(v, "tS2", sP0 + g + 1)
                c.vchain(v, v.tensor_copy(rlbc[:], psN[:]))
                c.inc("tE2", v.tensor_tensor(
                    oT_sb[hp:hp + 64, ht * T + g * 512:ht * T + (g + 1) * 512],
                    psO[g % 2][0:64, :], rlbc[:], ALU.mult))

        @blk.scalar
        def _(s):
            nmask = 0
            for li in range(NK):
                g, kt, j = kt_list[li]
                s.wait_ge(c.sem("tP"), sS0 + li + 1)
                if j is not None:
                    nmask += 1
                    s.wait_ge(c.sem("tX"), sM0 + nmask)
                if li >= 4:
                    s.wait_ge(c.sem("tP2"), sA0 + av_incs(li - 4 + 1))
                c.inc("tS", s.activation(ptb[:, (li % 4) * 512:(li % 4 + 1) * 512],
                                         psS[li % 4][:], AF.Exp, scale=0.125))


def moe_p3(c, ones128, ones128f, idn, iot):
    nc = c.nc
    io = c.io
    gate_prev = c.cnt["gate"]

    with contextlib.ExitStack() as ph:
        invs = sbuf(c, ph, "sm_invs", [128, 64], I32)
        wsl = sbuf(c, ph, "sm_wsl", [128, 64], F32)
        iw1 = sbuf(c, ph, "sm_iw1", [128, 2 * NJT], I32)
        iw2 = sbuf(c, ph, "sm_iw2", [128, NJT], I32)
        iwb = sbuf(c, ph, "sm_iwb", [128, NJT], I32)
        ejint = sbuf(c, ph, "sm_ejint", [1, NJT], I32)

        with contextlib.ExitStack() as rt:
            gw_sb = sbuf(c, rt, "gw_sb", [128, NHT * NE], BF16)
            gb_row = sbuf(c, rt, "gb_row", [1, NE], F32)
            gbbc = sbuf(c, rt, "gbbc", [128, NE], F32)
            OHall = sbuf(c, rt, "sm_oh", [128, NTT * NE], F32)
            PFXall = sbuf(c, rt, "sm_pfx", [128, NTT * NE], F32)
            mxall = sbuf(c, rt, "sm_mx", [128, NTT], F32)
            desti = sbuf(c, rt, "sm_dst", [128, NTT], I32)
            tokid = sbuf(c, rt, "sm_tok", [128, NTT], I32)
            iw1f = sbuf(c, rt, "sm_iw1f", [128, 2 * NJT], F32)
            iw2f = sbuf(c, rt, "sm_iw2f", [128, NJT], F32)
            iota8i = sbuf(c, rt, "sm_io8i", [128, NE], I32)
            iota8bc = sbuf(c, rt, "sm_io8", [128, NE], F32)
            m99 = sbuf(c, rt, "sm_m99", [128, NE], F32)
            iotapm = sbuf(c, rt, "sm_iopm", [128, 1], F32)
            iotapf = sbuf(c, rt, "sm_iop", [128, 1], F32)
            pfh1 = sbuf(c, rt, "sm_pfh1", [128, 1], F32)
            LTs = sbuf(c, rt, "sm_lts", [128, 128], F32)
            o128c = sbuf(c, rt, "sm_o128c", [128, 1], F32)
            o32c = sbuf(c, rt, "sm_o32c", [32, 1], F32)
            sel32it = sbuf(c, rt, "sm_s32i", [32, 32 * 128], I32)
            SEL32 = sbuf(c, rt, "sm_s32", [32, 32 * 128], F32)
            valTf = sbuf(c, rt, "sm_vtf", [128, 64], F32)
            valT = sbuf(c, rt, "sm_vt", [128, 64], I32)
            cntT = sbuf(c, rt, "sm_cntT", [NE, NTT], F32)
            cqA = sbuf(c, rt, "sm_cqA", [NE, NTT], F32)
            cqB = sbuf(c, rt, "sm_cqB", [NE, NTT], F32)
            combo = sbuf(c, rt, "sm_cmb", [NE, NTT + 1], F32)
            EBs = sbuf(c, rt, "sm_eb", [32, NE], F32)
            BASE = sbuf(c, rt, "sm_base", [32, NE], F32)
            ejbcf = sbuf(c, rt, "sm_ejbc", [128, NJT], F32)
            rowp = sbuf(c, rt, "sm_row", [1, 64], F32)
            rowi = sbuf(c, rt, "sm_rowi", [1, 16], I32)
            dsti = sbuf(c, rt, "sm_dsti", [128, 4], I32)
            ejrow = sbuf(c, rt, "sm_ejr", [1, NJT], F32)
            ejsc = sbuf(c, rt, "sm_ejs", [1, NE], F32)
            idn8f = sbuf(c, rt, "sm_id8", [NE, NE], F32)
            gsc = sbuf(c, rt, "p3_gsc", [128, 40], F32)
            destf = sbuf(c, rt, "sm_dstf", [128, 4], F32)
            xg = sbuf(c, rt, "p3_xg", [128, 2 * NHT * 128], BF16)

            psG = [psum(c, rt, f"psG{i}", [128, 64]) for i in range(2)]
            psR = [psum(c, rt, f"psR{i}", [128, 16]) for i in range(2)]
            psCT = psum(c, rt, "psCT", [NE, NTT])

            rts = DSlots(c, "rts", 1)

            with nc.Block("p3a") as blk:
                tP0, tP20, tV0, tX0, tS0, tS20, tE0 = c.snap(
                    "tP", "tP2", "tV", "tX", "tS", "tS2", "tE")
                ldA0 = c.ldA.snap()

                def pos_gate(i):
                    return tP0 + i + 1

                def pos_pfx(i):
                    return tP20 + 1 + 2 * i + 1

                def pos_cnt(i):
                    return tP20 + 1 + 2 * i + 2

                pos_trans, pos_baseb, pos_ejbc = (
                    tP20 + 66, tP20 + 67, tP20 + 68)

                def pos_bb(i):
                    return tP20 + 68 + i + 1

                pos_valt = tX0 + 6

                def pos_oh(i):
                    return tX0 + 10 + i + 1

                pos_cnts, pos_combo, pos_pbx, pos_ej, pos_iw = (
                    tX0 + 43, tX0 + 44, tX0 + 45, tX0 + 46, tX0 + 47)

                def pos_exp(i):
                    return tV0 + i + 1

                def pos_dest(i):
                    return tV0 + 32 + i + 1

                def pos_pfxe(i):
                    return tE0 + i + 1

                pos_tote, pos_ebe, pos_basee, pos_ejbce = (
                    tE0 + 33, tE0 + 34, tE0 + 35, tE0 + 36)

                def pos_bbe(i):
                    return tE0 + 36 + i + 1

                def pos_sexp(i):
                    return tS0 + i + 1

                pos_gpc = tS20 + 6

                @blk.gpsimd
                def _(g):
                    g.wait_ge(c.sem("gate"), gate_prev)
                    c.grpA.issue(g.dma_start(
                        out=gw_sb[:].rearrange("p (a e) -> p a e", a=NHT),
                        in_=io["gate_w"].rearrange("(a p) e -> p a e", p=128)), 0)
                    c.grpA.issue(g.dma_start(out=gb_row[:], in_=io["gate_b_r"]), 0)
                    c.inc("tS2", g.iota(sel32it[:].rearrange("k (i m) -> k i m", m=128),
                                        [[1, 32], [0, 128]], channel_multiplier=-1))
                    c.inc("tS2", g.iota(tokid[:], [[128, NTT]], channel_multiplier=1))
                    c.inc("tS2", g.iota(iota8i[:], [[1, NE]], channel_multiplier=0))
                    c.inc("tS2", g.memset(valTf[:], float(T)))
                    c.inc("tS2", g.memset(o128c[:], 1.0))
                    c.inc("tS2", g.memset(o32c[:], 1.0))
                    g.wait_ge(c.sem("tX"), pos_valt)
                    rts.issue(g.dma_start(
                        out=io["invh"].rearrange("(p a) o -> p (a o)", p=128),
                        in_=valT[:]), 0)
                    rts.issue(g.dma_start(
                        out=io["wslh"].rearrange("(p a) o -> p (a o)", p=128),
                        in_=valTf[:]), 0)
                    rts.wait_units(g, 0, rts.cnt[0])
                    for i in range(NTT):
                        g.wait_ge(c.sem("tV"), pos_dest(i))
                        rts.issue(g.indirect_dma_start(
                            out=io["invh"],
                            out_offset=bass.IndirectOffsetOnAxis(desti[:, i:i + 1], 0),
                            in_=tokid[:, i:i + 1], in_offset=None), 0)
                        rts.issue(g.indirect_dma_start(
                            out=io["wslh"],
                            out_offset=bass.IndirectOffsetOnAxis(desti[:, i:i + 1], 0),
                            in_=mxall[:, i:i + 1], in_offset=None), 0)


                @blk.sync
                def _(sy):
                    sy.wait_ge(c.sem("gate"), gate_prev)
                    for i in range(NTT):
                        if i >= 2:
                            c.wait(sy, "tP", pos_gate(i - 2))
                        c.ldA.issue(sy.dma_start(
                            out=xg[:].rearrange("p (b a f) -> p b a f", b=2, a=NHT)[:, i % 2],
                            in_=io["xn0fm_h"].rearrange("(a p) t -> p a t", p=128)[:, :, i * 128:(i + 1) * 128]), i)
                    rts.wait_units(sy, 0, rts.cnt[0])
                    c.ldA.issue(sy.dma_start(
                        out=invs[:], in_=io["invh"].rearrange("(p a) o -> p (a o)", p=128)), 0)
                    c.ldA.issue(sy.dma_start(
                        out=wsl[:], in_=io["wslh"].rearrange("(p a) o -> p (a o)", p=128)), 1)

                def routing_mm(t, i):
                    c.wait(t, "tX", pos_oh(i))
                    if i >= 2:
                        c.wait(t, "tE", pos_pfxe(i - 2))
                    c.inc("tP2", t.matmul(psR[i % 2][:, 0:NE], LTs[:],
                                          OHall[:, i * NE:(i + 1) * NE],
                                          start=True, stop=True))
                    c.inc("tP2", t.matmul(psCT[0:NE, i:i + 1],
                                          OHall[:, i * NE:(i + 1) * NE], o128c[:],
                                          start=True, stop=True))

                @blk.tensor
                def _(t):
                    t.wait_ge(c.sem("gate"), gate_prev)
                    c.grpA.wait_all(t)
                    c.inc("tP2", t.matmul(psG[0][:, 0:NE], ones128f[:], gb_row[:],
                                          start=True, stop=True))
                    for i in range(NTT):
                        c.ldA.wait_units(t, i, ldA0[i % 2] + 16 * (i // 2 + 1))
                        if i == 0:
                            c.wait(t, "tX", tX0 + 10)
                        if i >= 2:
                            c.wait(t, "tV", pos_exp(i - 2))
                        for hh in range(NHT):
                            ins = t.matmul(psG[i % 2][:, 0:NE],
                                           xg[:, (i % 2) * NHT * 128 + hh * 128:(i % 2) * NHT * 128 + (hh + 1) * 128],
                                           gw_sb[:, hh * NE:(hh + 1) * NE],
                                           start=(hh == 0), stop=(hh == NHT - 1))
                        c.inc("tP", ins)
                        if i >= 1:
                            routing_mm(t, i - 1)
                    routing_mm(t, NTT - 1)
                    c.wait(t, "tX", pos_combo)
                    c.wait(t, "tE", pos_pfxe(NTT - 1))
                    c.inc("tP2", t.transpose(psR[0][0:NTT + 1, 0:NE],
                                             combo[:], idn8f[:]))
                    c.wait(t, "tX", pos_pbx)
                    c.wait(t, "tE", pos_ebe)
                    c.inc("tP2", t.matmul(psR[0][0:32, 0:NE], ones128f[:, 0:32],
                                          rowp[0:1, 56:64], start=True, stop=True))
                    c.wait(t, "tX", pos_ej)
                    c.inc("tP2", t.matmul(psR[1][:, 0:NJT], ones128f[:], ejrow[:],
                                          start=True, stop=True))
                    for i in range(NTT):
                        if i == 0:
                            c.wait(t, "tE", pos_basee)
                        elif i == 1:
                            c.wait(t, "tE", pos_ejbce)
                        else:
                            c.wait(t, "tE", pos_bbe(i - 2))
                        c.inc("tP2", t.matmul(psR[i % 2][:, 0:NE],
                                              SEL32[:, i * 128:(i + 1) * 128],
                                              BASE[:], start=True, stop=True))

                @blk.vector
                def _(v):
                    v.wait_ge(c.sem("gate"), gate_prev)
                    v.wait_ge(c.sem("tS2"), pos_gpc)
                    c.inc("tX", v.tensor_single_scalar(SEL32[:], sel32it[:], 0, ALU.is_equal))
                    c.inc("tX", v.tensor_copy(iota8bc[:], iota8i[:]))
                    c.inc("tX", v.tensor_copy(iotapm[:], iot[:, 0:1]))
                    c.inc("tX", v.tensor_scalar(m99[:], iota8bc[:], -99.0, None, ALU.add))
                    c.inc("tX", v.tensor_scalar(iotapf[:], iotapm[:], -1.0, None, ALU.mult))
                    c.inc("tX", v.tensor_copy(valT[:], valTf[:]))
                    c.inc("tX", v.tensor_scalar(pfh1[:], iotapf[:], 128.0, None, ALU.add))
                    c.inc("tX", v.tensor_single_scalar(LTs[:], iot[:, 0:128], 0, ALU.is_gt))
                    c.inc("tX", v.tensor_single_scalar(idn8f[:], iot[0:NE, 0:NE], 0, ALU.is_equal))
                    c.wait(v, "tP2", tP20 + 1)
                    c.inc("tX", v.tensor_copy(gbbc[:, 0:NE], psG[0][:, 0:NE]))
                    for i in range(NTT):
                        c.wait(v, "tP", pos_gate(i))
                        c.vchain(v, v.tensor_tensor(gsc[:, 0:NE], psG[i % 2][:, 0:NE],
                                                    gbbc[:, 0:NE], ALU.add))
                        c.inc("tV", v.reduce_max(gsc[:, 8:9], gsc[:, 0:NE],
                                                 axis=AX, negate=True))
                        c.wait(v, "tS", pos_sexp(i))
                        c.vchain(v, v.reciprocal(gsc[:, 11:12], gsc[:, 10:11]))
                        c.vchain(v, v.tensor_scalar(gsc[:, 16:24], gsc[:, 16:24],
                                                    gsc[:, 11:12], None, ALU.mult))
                        c.vchain(v, v.reduce_max(mxall[:, i:i + 1], gsc[:, 16:24], axis=AX))
                        c.vchain(v, v.tensor_single_scalar(gsc[:, 0:NE], gsc[:, 16:24],
                                                           mxall[:, i:i + 1], ALU.is_ge))
                        c.vchain(v, v.tensor_tensor(gsc[:, 24:32], gsc[:, 0:NE],
                                                    m99[:], ALU.mult))
                        c.vchain(v, v.tensor_scalar(gsc[:, 24:32], gsc[:, 24:32],
                                                    99.0, None, ALU.add))
                        c.vchain(v, v.tensor_reduce(gsc[:, 32:33], gsc[:, 24:32],
                                                    AX, ALU.min))
                        c.inc("tX", v.tensor_single_scalar(OHall[:, i * NE:(i + 1) * NE],
                                                           iota8bc[:], gsc[:, 32:33],
                                                           ALU.is_equal))
                        if i >= 1:
                            c.wait(v, "tP2", pos_pfx(i - 1))
                            c.inc("tE", v.tensor_copy(PFXall[:, (i - 1) * NE:i * NE],
                                                      psR[(i - 1) % 2][:, 0:NE]))
                    c.wait(v, "tP2", pos_pfx(NTT - 1))
                    c.inc("tE", v.tensor_copy(PFXall[:, (NTT - 1) * NE:NTT * NE],
                                              psR[(NTT - 1) % 2][:, 0:NE]))
                    c.wait(v, "tP2", pos_cnt(NTT - 1))
                    c.chain(v, "tX", v.tensor_copy(cntT[:], psCT[0:NE, :]))
                    c.vchain(v, v.tensor_copy(cqB[:], cntT[:]))
                    pp = [(cqB, cqA), (cqA, cqB), (cqB, cqA), (cqA, cqB), (cqB, cqA)]
                    for sh, (src, dst) in zip([1, 2, 4, 8, 16], pp):
                        c.vchain(v, v.tensor_tensor(dst[:, sh:NTT], src[:, sh:NTT],
                                                    src[:, 0:NTT - sh], ALU.add))
                        c.vchain(v, v.tensor_copy(dst[:, 0:sh], src[:, 0:sh]))
                    c.vchain(v, v.tensor_tensor(combo[:, 0:NTT], cqA[:], cntT[:],
                                                ALU.subtract))
                    c.inc("tX", v.tensor_copy(combo[:, NTT:NTT + 1],
                                              cqA[:, NTT - 1:NTT]))
                    c.wait(v, "tP2", pos_trans)
                    c.chain(v, "tE", v.tensor_copy(rowp[0:1, 0:8],
                                                   psR[0][NTT:NTT + 1, 0:NE]))
                    c.chain(v, "tE", v.tensor_copy(EBs[:], psR[0][0:32, 0:NE]))
                    c.vchain(v, v.tensor_copy(rowi[0:1, 0:8], rowp[0:1, 0:8]))
                    c.vchain(v, v.tensor_single_scalar(rowi[0:1, 8:16], rowi[0:1, 0:8],
                                                       511, ALU.add))
                    c.vchain(v, v.tensor_single_scalar(rowi[0:1, 8:16], rowi[0:1, 8:16],
                                                       9, ALU.logical_shift_right))
                    c.vchain(v, v.tensor_single_scalar(rowi[0:1, 8:16], rowi[0:1, 8:16],
                                                       9, ALU.logical_shift_left))
                    c.vchain(v, v.tensor_copy(rowp[0:1, 32:40], rowi[0:1, 8:16]))
                    qA, qB, P5 = rowp[0:1, 40:48], rowp[0:1, 48:56], rowp[0:1, 32:40]
                    c.vchain(v, v.tensor_copy(qB[:, 0:8], P5[:, 0:8]))
                    for sh, (src, dst) in zip([1, 2, 4], [(qB, qA), (qA, qB), (qB, qA)]):
                        c.vchain(v, v.tensor_tensor(dst[:, sh:8], src[:, sh:8],
                                                    src[:, 0:8 - sh], ALU.add))
                        c.vchain(v, v.tensor_copy(dst[:, 0:sh], src[:, 0:sh]))
                    c.inc("tX", v.tensor_tensor(rowp[0:1, 56:64], qA[:, 0:8],
                                                P5[:, 0:8], ALU.subtract))
                    for j in range(NJT):
                        c.vchain(v, v.tensor_single_scalar(ejsc[:], qA[:, 0:8],
                                                           512.0 * j, ALU.is_le))
                        c.vchain(v, v.reduce_sum(ejrow[0:1, j:j + 1], ejsc[:], axis=AX))
                    c.vchain(v, v.tensor_single_scalar(ejrow[:], ejrow[:], 7.0, ALU.min))
                    c.inc("tX", v.tensor_copy(ejint[:], ejrow[:]))
                    c.wait(v, "tP2", pos_baseb)
                    c.inc("tE", v.tensor_tensor(BASE[:], psR[0][0:32, 0:NE],
                                                EBs[:], ALU.add))
                    c.wait(v, "tP2", pos_ejbc)
                    c.chain(v, "tE", v.tensor_copy(ejbcf[:], psR[1][:, 0:NJT]))
                    for j in range(NJT):
                        for fh in range(2):
                            c.vchain(v, v.tensor_scalar(
                                iw1f[:, 2 * j + fh:2 * j + fh + 1], ejbcf[:, j:j + 1],
                                256.0, iotapf[:] if fh == 0 else pfh1[:],
                                ALU.mult, ALU.add))
                        c.vchain(v, v.tensor_scalar(iw2f[:, j:j + 1], ejbcf[:, j:j + 1],
                                                    128.0, iotapf[:], ALU.mult, ALU.add))
                    c.vchain(v, v.tensor_copy(iwb[:], ejbcf[:]))
                    c.vchain(v, v.tensor_copy(iw1[:], iw1f[:]))
                    c.inc("tX", v.tensor_copy(iw2[:], iw2f[:]))
                    for i in range(NTT):
                        c.wait(v, "tP2", pos_bb(i))
                        c.chain(v, "tE", v.tensor_tensor(gsc[:, 0:NE], psR[i % 2][:, 0:NE],
                                                         PFXall[:, i * NE:(i + 1) * NE],
                                                         ALU.add))
                        c.vchain(v, v.tensor_tensor(gsc[:, 0:NE], gsc[:, 0:NE],
                                                    OHall[:, i * NE:(i + 1) * NE], ALU.mult))
                        c.vchain(v, v.reduce_sum(destf[:, 0:1], gsc[:, 0:NE], axis=AX))
                        c.vchain(v, v.tensor_copy(dsti[:, 0:1], destf[:, 0:1]))
                        c.vchain(v, v.tensor_single_scalar(dsti[:, 1:2], dsti[:, 0:1],
                                                           127, ALU.bitwise_and))
                        c.vchain(v, v.tensor_single_scalar(dsti[:, 1:2], dsti[:, 1:2],
                                                           6, ALU.logical_shift_left))
                        c.vchain(v, v.tensor_single_scalar(dsti[:, 2:3], dsti[:, 0:1],
                                                           7, ALU.logical_shift_right))
                        c.inc("tV", v.tensor_tensor(desti[:, i:i + 1], dsti[:, 1:2],
                                                    dsti[:, 2:3], ALU.add))

                @blk.scalar
                def _(s):
                    s.wait_ge(c.sem("gate"), gate_prev)
                    for i in range(NTT):
                        s.wait_ge(c.sem("tV"), pos_exp(i))
                        c.inc("tS", s.activation(gsc[:, 16:24], gsc[:, 0:NE], AF.Exp,
                                                 bias=gsc[:, 8:9], accum_out=gsc[:, 10:11]))

            inv_ld = c.ldA.cnt[0]
            wsl_ld = c.ldA.cnt[1]
            pos_iw_keep = pos_iw

        qsl = []
        if "wgate" in io:
            def _add_q(mat, dst, it, ncol):
                c0 = 0
                while c0 < ncol:
                    w_ = min(256, ncol - c0)
                    qsl.append((io[mat][it * 128:(it + 1) * 128, c0:c0 + w_],
                                io[dst][it * 128:(it + 1) * 128, c0:c0 + w_],
                                w_))
                    c0 += w_
            for it in range(NIT):
                _add_q("wgate", "wgq_h", it, D)
            for it in range(NIT):
                _add_q("wup", "wuq_h", it, D)
            for dt_ in range(NHT):
                _add_q("wdown", "wdq_h", dt_, I1)

        with contextlib.ExitStack() as pe_:
            w1h = sbuf(c, pe_, "w1h", [128, 2 * NHT * FH], BF16)
            w2h = sbuf(c, pe_, "w2h", [128, 2 * NFH * D], BF16)
            xgt = sbuf(c, pe_, "sm_xgt", [128, 4 * D], BF16)
            xcs = sbuf(c, pe_, "sm_xcs", [128, NHT * 512], BF16)
            h1h = sbuf(c, pe_, "h1h", [128, NFH * 512], BF16)
            ysc = sbuf(c, pe_, "ysc", [128, 4 * D], F32)
            b1je = sbuf(c, pe_, "sm_b1j", [128, 2 * 32], F32)
            b2bcj = sbuf(c, pe_, "sm_b2j", [128, 2 * D], BF16)
            if qsl:
                qin = sbuf(c, pe_, "q6_in", [128, 4 * 512], F32)
                qqb = sbuf(c, pe_, "q6_qb", [128, 8 * 512], BF16)
                qsA = sbuf(c, pe_, "q6_sA", [128, 512], F32)
                qsX = sbuf(c, pe_, "q6_sX", [128, 512], F32)
                qsC = sbuf(c, pe_, "q6_sC", [128, 512], F32)
                qsB = sbuf(c, pe_, "q6_sB", [128, 2 * 32], F32)

            psH = [psum(c, pe_, f"psH{i}", [128, 512]) for i in range(2)]
            psY2 = [psum(c, pe_, f"psY2{i}", [128, 512]) for i in range(3)]
            pst = [psum(c, pe_, f"sm_pst{i}", [128, 512], BF16) for i in range(2)]

            xgs = DSlots(c, "xgs", 4)
            bs1 = DSlots(c, "bs1", 2)
            bs2 = DSlots(c, "bs2", 2)
            ldQ = DSlots(c, "ldQ", 8)
            stQ = DSlots(c, "stQ", 8)

            with nc.Block("p3b") as blk:
                tP0b, tP20b, tE0b, tE20b, tS0b = c.snap("tP", "tP2", "tE", "tE2", "tS")
                c._dbg_p3b = (tP0b, tP20b, tE0b, tE20b, tS0b)
                wLa0 = c.wLa.snap()
                w2s0 = c.w2s.snap()
                qAs0 = c.qAs.snap()
                qW0, qL0, qS0 = c.snap("qW", "qL", "qS")
                qst = {"ld": 0, "q": 0, "tr": 0, "ev": 0}

                def q_loads(g, k):
                    while k > 0 and qst["ld"] < len(qsl):
                        i = qst["ld"]
                        srcap, dstap, w_ = qsl[i]
                        if i >= 8:
                            g.wait_ge(c.sem("qW"), qW0 + i - 7)
                        ldQ.issue(g.dma_start(out=qin[:, (i % 8) * 256:(i % 8) * 256 + w_],
                                              in_=srcap), i)
                        qst["ld"] += 1
                        k -= 1

                def q_quant(v, k):
                    while k > 0 and qst["q"] < len(qsl):
                        i = qst["q"]
                        _, _, w_ = qsl[i]
                        ldQ.wait_units(v, i, 16 * (i // 8 + 1))
                        if i >= 16:
                            stQ.wait_units(v, i - 16, 16 * ((i - 16) // 8 + 1))
                        ins = emit_quant(c, v, qin[:, (i % 8) * 256:(i % 8) * 256 + w_],
                                         qqb[:, (i % 16) * 256:(i % 16) * 256 + w_], w_,
                                         (qsA[:, 0:w_], qsX[:, 0:w_], qsC[:, 0:w_],
                                          qsB[:, 0:w_ // 16], qsB[:, 32:32 + w_ // 16]))
                        c.inc("qW", ins)
                        qst["q"] += 1
                        k -= 1

                def q_store(s, upto):
                    while qst["ev"] < min(upto, len(qsl)):
                        i = qst["ev"]
                        _, dstap, w_ = qsl[i]
                        c.wait(s, "qW", qW0 + i + 1)
                        stQ.issue(s.dma_start(
                            out=dstap, in_=qqb[:, (i % 16) * 256:(i % 16) * 256 + w_]), i)
                        qst["ev"] += 1

                def tp_h1(kk, f):
                    return tP0b + kk * 16 + f + 1

                def tp2_trans(j, hh):
                    return tP20b + j * 24 + hh + 1

                def tp2_y(kk, g):
                    return tP20b + (kk // 2) * 24 + 8 + (kk % 2) * 8 + g + 1

                def te_evac(kk, g):
                    j, fh = kk // 2, kk % 2
                    if fh == 0:
                        return tE0b + j * 20 + g + 1
                    return tE0b + j * 20 + 8 + (g // 2) * 3 + (g % 2) + 1

                def te_scale(j, t128):
                    return tE0b + j * 20 + 8 + t128 * 3 + 3

                def ts_gelu(kk, f):
                    return tS0b + kk * 16 + f + 1

                def te2_xc(j, hh):
                    return tE20b + j * 8 + hh + 1

                @blk.sync
                def _(sy):
                    if qsl:
                        sy.wait_ge(c.sem("gate"), gate_prev)
                        q_loads(sy, len(qsl))

                @blk.gpsimd
                def _(g):
                    g.wait_ge(c.sem("tX"), pos_iw_keep)
                    for kk in range(2 * NJT):
                        j, fh = kk // 2, kk % 2
                        if kk >= 2:
                            g.wait_ge(c.sem("tP"), tp_h1(kk - 2, 15))
                        c.wLa.issue(g.indirect_dma_start(
                            out=w1h[:, (kk % 2) * NHT * FH:(kk % 2 + 1) * NHT * FH],
                            out_offset=None, in_=io["w1"],
                            in_offset=bass.IndirectOffsetOnAxis(iw1[:, kk:kk + 1], 0)), kk)
                        if fh == 0:
                            if j >= 2:
                                g.wait_ge(c.sem("tS"), ts_gelu(2 * (j - 2) + 1, 15))
                            bs1.issue(g.indirect_dma_start(
                                out=b1je[:, (j % 2) * 32:(j % 2 + 1) * 32],
                                out_offset=None, in_=io["b1r"],
                                in_offset=bass.IndirectOffsetOnAxis(iw2[:, j:j + 1], 0)), j)
                            if j >= 2:
                                g.wait_ge(c.sem("tE"), te_evac(2 * (j - 2), 7))
                            bs2.issue(g.indirect_dma_start(
                                out=b2bcj[:, (j % 2) * D:(j % 2 + 1) * D],
                                out_offset=None, in_=io["b2"],
                                in_offset=bass.IndirectOffsetOnAxis(iwb[:, j:j + 1], 0)), j)
                            if j >= 1:
                                g.wait_ge(c.sem("tP2"), tp2_trans(j - 1, 7))
                            else:
                                c.ldA.wait_units(g, 0, inv_ld)
                            for q in range(4):
                                xgs.issue(g.indirect_dma_start(
                                    out=xgt[:, q * D:(q + 1) * D],
                                    out_offset=None, in_=io["xn0tm_h"],
                                    in_offset=bass.IndirectOffsetOnAxis(
                                        invs[:, j * 4 + q:j * 4 + q + 1], 0)), q)
                        if kk >= 2:
                            g.wait_ge(c.sem("tP2"), tp2_y(kk - 2, 7))
                        c.w2s.issue(g.indirect_dma_start(
                            out=w2h[:, (kk % 2) * NFH * D:(kk % 2 + 1) * NFH * D],
                            out_offset=None, in_=io["w2"],
                            in_offset=bass.IndirectOffsetOnAxis(iw2[:, j:j + 1], 0),
                            element_offset=fh * NFH * D), kk)
                        if fh == 1:
                            for t128 in range(4):
                                g.wait_ge(c.sem("tE"), te_scale(j, t128))
                                c.qAs.issue(g.indirect_dma_start(
                                    out=io["moe2_h"],
                                    out_offset=bass.IndirectOffsetOnAxis(
                                        invs[:, j * 4 + t128:j * 4 + t128 + 1], 0),
                                    in_=ysc[:, t128 * D:(t128 + 1) * D],
                                    in_offset=None), t128)
                    for t128 in range(4):
                        c.qAs.wait_units(g, t128, qAs0[t128] + 16 * NJT)
                    if qsl:
                        nq_ = len(qsl)
                        for s_ in range(8):
                            g.wait_ge(ldQ.sems[s_], 16 * ((nq_ - s_ + 7) // 8))
                        for s_ in range(8):
                            g.wait_ge(stQ.sems[s_], 16 * ((nq_ - s_ + 7) // 8))
                    g.sem_inc(c.sem("gate"), 1)
                    c.bump("gate")

                @blk.tensor
                def _(t):
                    for kk in range(2 * NJT):
                        j, fh = kk // 2, kk % 2
                        if fh == 0:
                            for hh in range(NHT):
                                gidx = j * 8 + hh
                                if gidx >= 2:
                                    jp, hp = divmod(gidx - 2, 8)
                                    c.wait(t, "tE2", te2_xc(jp, hp))
                                if hh == 0:
                                    for q in range(4):
                                        xgs.wait_units(t, q, 16 * (j + 1))
                                for q in range(4):
                                    ins = t.transpose(pst[hh % 2][:, q * 128:(q + 1) * 128],
                                                      xgt[:, q * D + hh * 128:q * D + (hh + 1) * 128],
                                                      idn[:])
                                c.inc("tP2", ins)
                        c.wLa.wait_units(t, kk, wLa0[kk % 2] + 16 * (kk // 2 + 1))
                        c.wait(t, "tE2", te2_xc(j, 7))
                        for f in range(NFH):
                            kkf = kk * 16 + f
                            if kkf >= 2:
                                kp, fp = divmod(kkf - 2, 16)
                                c.wait(t, "tS", ts_gelu(kp, fp))
                            for hh in range(NHT):
                                ins = t.matmul(
                                    psH[kkf % 2][:],
                                    w1h[:, (kk % 2) * NHT * FH + hh * FH + f * 128:
                                           (kk % 2) * NHT * FH + hh * FH + (f + 1) * 128],
                                    xcs[:, hh * 512:(hh + 1) * 512],
                                    start=(hh == 0), stop=(hh == NHT - 1))
                            c.inc("tP", ins)
                        c.w2s.wait_units(t, kk, w2s0[kk % 2] + 16 * (kk // 2 + 1))
                        c.wait(t, "tS", ts_gelu(kk, 15))
                        for t128 in range(4):
                            for dch in range(2):
                                g_ = t128 * 2 + dch
                                gg = kk * 8 + g_
                                if gg >= 3:
                                    kp, gp = divmod(gg - 3, 8)
                                    c.wait(t, "tE", te_evac(kp, gp))
                                for f in range(NFH):
                                    ins = t.matmul(
                                        psY2[gg % 3][:],
                                        h1h[:, f * 512 + t128 * 128:f * 512 + (t128 + 1) * 128],
                                        w2h[:, (kk % 2) * NFH * D + f * D + dch * 512:
                                               (kk % 2) * NFH * D + f * D + dch * 512 + 512],
                                        start=(f == 0), stop=(f == NFH - 1))
                                c.inc("tP2", ins)


                @blk.scalar
                def _(s):
                    for kk in range(2 * NJT):
                        j, fh = kk // 2, kk % 2
                        q_store(s, 8 * (kk - 1))
                        if fh == 0:
                            if j >= 1:
                                s.wait_ge(c.sem("tP"), tp_h1(2 * (j - 1) + 1, 15))
                            for hh in range(NHT):
                                s.wait_ge(c.sem("tP2"), tp2_trans(j, hh))
                                c.inc("tE2", s.copy(
                                    xcs[:, hh * 512:(hh + 1) * 512],
                                    pst[hh % 2][:, 0:512]))
                        if kk >= 1:
                            s.wait_ge(c.sem("tP2"), tp2_y(kk - 1, 7))
                        if fh == 0:
                            bs1.wait_units(s, j, 16 * (j // 2 + 1))
                        for f in range(NFH):
                            kkf = kk * 16 + f
                            s.wait_ge(c.sem("tP"), tp_h1(kk, f))
                            c.inc("tS", s.activation(
                                h1h[:, f * 512:(f + 1) * 512], psH[kkf % 2][:], AF.Gelu,
                                bias=b1je[:, (j % 2) * 32 + fh * 16 + f:
                                          (j % 2) * 32 + fh * 16 + f + 1]))
                    q_store(s, len(qsl))

                @blk.vector
                def _(v):
                    c.ldA.wait_units(v, 1, wsl_ld)
                    for kk in range(2 * NJT):
                        j, fh = kk // 2, kk % 2
                        for t128 in range(4):
                            for dch in range(2):
                                g_ = t128 * 2 + dch
                                if fh == 0 and dch == 0 and j >= 1:
                                    c.qAs.wait_units(v, t128, qAs0[t128] + 16 * j)
                                if fh == 0 and g_ == 0:
                                    bs2.wait_units(v, j, 16 * (j // 2 + 1))
                                v.wait_ge(c.sem("tP2"), tp2_y(kk, g_))
                                dst = ysc[:, t128 * D + dch * 512:t128 * D + dch * 512 + 512]
                                if fh == 0:
                                    c.inc("tE", v.tensor_tensor(
                                        dst, psY2[(kk * 8 + g_) % 3][:],
                                        b2bcj[:, (j % 2) * D + dch * 512:
                                              (j % 2) * D + dch * 512 + 512], ALU.add))
                                elif dch == 0:
                                    c.inc("tE", v.tensor_tensor(dst, psY2[(kk * 8 + g_) % 3][:],
                                                                dst, ALU.add))
                                else:
                                    c.chain(v, "tE", v.tensor_tensor(
                                        dst, psY2[(kk * 8 + g_) % 3][:], dst, ALU.add))
                                q_quant(v, 1)
                            if fh == 1:
                                c.inc("tE", v.tensor_scalar(
                                    ysc[:, t128 * D:(t128 + 1) * D],
                                    ysc[:, t128 * D:(t128 + 1) * D],
                                    wsl[:, j * 4 + t128:j * 4 + t128 + 1], None, ALU.mult))
                    q_quant(v, len(qsl))


def emit_quant(c, v, x, q, W, scrs):
    a, xn, c2, bm, sc2 = scrs
    NB = W // 16
    ch = lambda ins: c.vchain(v, ins)
    a3 = a.rearrange("p (b k) -> p b k", k=16)
    ch(v.tensor_single_scalar(a.bitcast(I32), x.bitcast(I32), 0x7FFFFFFF, ALU.bitwise_and))
    ch(v.tensor_tensor(a3[:, :, 0:8], a3[:, :, 0:8], a3[:, :, 8:16], ALU.max))
    ch(v.tensor_tensor(a3[:, :, 0:4], a3[:, :, 0:4], a3[:, :, 4:8], ALU.max))
    ch(v.tensor_tensor(a3[:, :, 0:2], a3[:, :, 0:2], a3[:, :, 2:4], ALU.max))
    ch(v.tensor_tensor(bm.rearrange("p (b o) -> p b o", o=1), a3[:, :, 0:1],
                       a3[:, :, 1:2], ALU.max))
    ch(v.tensor_scalar(sc2, bm, 1.0 / 6.0, 1e-12, ALU.mult, ALU.max))
    ch(v.reciprocal(sc2, sc2))
    ch(v.tensor_single_scalar(a.bitcast(I32), x.bitcast(I32), 0x7FFFFFFF, ALU.bitwise_and))
    ch(v.tensor_tensor(xn.rearrange("p (b k) -> p b k", k=16), a3,
                       sc2.rearrange("p (b o) -> p b o", o=1).broadcast_to([128, NB, 16]),
                       ALU.mult))
    ch(v.tensor_single_scalar(a, xn, 0.25, ALU.is_gt))
    for m in [0.75, 1.25, 1.75]:
        ch(v.scalar_tensor_tensor(a, xn, m, a, ALU.is_gt, ALU.add))
    ch(v.tensor_single_scalar(c2, xn, 2.5, ALU.is_gt))
    ch(v.scalar_tensor_tensor(c2, xn, 3.5, c2, ALU.is_gt, ALU.add))
    ch(v.tensor_single_scalar(xn, xn, 5.0, ALU.is_gt))
    ch(v.scalar_tensor_tensor(c2, xn, 2.0, c2, ALU.mult, ALU.add))
    ch(v.scalar_tensor_tensor(a, a, 0.5, c2, ALU.mult, ALU.add))
    ch(v.tensor_scalar(xn, x, 0.0, 2.0, ALU.is_ge, ALU.mult))
    ch(v.reciprocal(sc2, sc2))
    ch(v.tensor_tensor(a.rearrange("p (b k) -> p b k", k=16), a3,
                       sc2.rearrange("p (b o) -> p b o", o=1).broadcast_to([128, NB, 16]),
                       ALU.mult))
    ch(v.tensor_scalar(xn, xn, -1.0, None, ALU.add))
    return v.tensor_tensor(q, a, xn, ALU.mult)


def quant_transpose_phase(c, idn, name, items, W_of, nn_of, load_src, store_dst):
    nc = c.nc
    sb = sbuf
    gate_e = c.cnt["gate"]
    Wmax = max(W_of(it) for it in items)
    NNmax = max(nn_of(it) for it in items)

    def tp_thr(k):
        if k < 0:
            return 0
        return sum((nn_of(items[k2]) + 3) // 4 for k2 in range(k + 1))

    with contextlib.ExitStack() as ph:
        wld = sb(c, ph, f"{name}_wld", [128, 2 * Wmax], F32)
        qb = sb(c, ph, f"{name}_qb", [128, 2 * Wmax], BF16)
        sA = sb(c, ph, f"{name}_sA", [128, Wmax], F32)
        sX = sb(c, ph, f"{name}_sX", [128, Wmax], F32)
        sC = sb(c, ph, f"{name}_sC", [128, Wmax], F32)
        sB = sb(c, ph, f"{name}_sB", [128, 2 * (Wmax // 16)], F32)
        tst = sb(c, ph, f"{name}_tst", [128, 2 * NNmax * 128], BF16)
        pst = [psum(c, ph, f"{name}_ps{i}", [128, 512], BF16) for i in range(2)]

        with nc.Block(name) as blk:
            tV0, tP0, tE0 = c.snap("tV", "tP", "tE")
            ldA0 = c.ldA.snap()
            stA0 = c.stA.snap()

            @blk.sync
            def _(sy):
                sy.wait_ge(c.sem("gate"), gate_e)

                def store(k):
                    it = items[k]
                    nn = nn_of(it)
                    sy.wait_ge(c.sem("tE"), tE0 + tp_thr(k))
                    c.stA.issue(sy.dma_start(
                        out=store_dst(it),
                        in_=tst[:, (k % 2) * NNmax * 128:(k % 2) * NNmax * 128 + nn * 128]
                        .rearrange("p (a f) -> p a f", f=128)), k % 2)
                for k, it in enumerate(items):
                    if k >= 2:
                        c.wait(sy, "tV", tV0 + (k - 2) + 1)
                    c.ldA.issue(sy.dma_start(out=wld[:, (k % 2) * Wmax:(k % 2) * Wmax + W_of(it)],
                                             in_=load_src(it)), k)
                    if k >= 2:
                        store(k - 2)
                for k in range(max(0, len(items) - 2), len(items)):
                    store(k)
                c.stA.wait_all(sy)
                sy.sem_inc(c.sem("gate"), 1)
                c.bump("gate")

            @blk.vector
            def _(v):
                for k, it in enumerate(items):
                    W = W_of(it)
                    c.ldA.wait_units(v, k, ldA0[k % 2] + 16 * (k // 2 + 1))
                    if k >= 2:
                        c.wait(v, "tP", tP0 + tp_thr(k - 2))
                    ins = emit_quant(c, v, wld[:, (k % 2) * Wmax:(k % 2) * Wmax + W],
                                     qb[:, (k % 2) * Wmax:(k % 2) * Wmax + W], W,
                                     (sA[:, 0:W], sX[:, 0:W], sC[:, 0:W],
                                      sB[:, 0:W // 16], sB[:, Wmax // 16:Wmax // 16 + W // 16]))
                    c.inc("tV", ins)

            @blk.tensor
            def _(t):
                t.wait_ge(c.sem("tX"), 2)
                for k, it in enumerate(items):
                    nn = nn_of(it)
                    c.wait(t, "tV", tV0 + k + 1)
                    for q in range(nn):
                        grp = q // 4
                        if tp_thr(k - 1) + grp >= 2:
                            c.wait(t, "tE", tE0 + tp_thr(k - 1) + grp - 1)
                        ins = t.transpose(pst[grp % 2][:, (q % 4) * 128:(q % 4 + 1) * 128],
                                          qb[:, (k % 2) * Wmax + q * 128:(k % 2) * Wmax + (q + 1) * 128],
                                          idn[:])
                        if q % 4 == 3 or q == nn - 1:
                            c.inc("tP", ins)

            @blk.scalar
            def _(s):
                for k, it in enumerate(items):
                    nn = nn_of(it)
                    ngrp = (nn + 3) // 4
                    if k >= 2:
                        c.stA.wait_units(s, k % 2, stA0[k % 2] + 16 * (k // 2))
                    for grp in range(ngrp):
                        s.wait_ge(c.sem("tP"), tP0 + tp_thr(k - 1) + grp + 1)
                        cols = min(4, nn - grp * 4) * 128
                        c.inc("tE", s.copy(tst[:, (k % 2) * NNmax * 128 + grp * 512:
                                               (k % 2) * NNmax * 128 + grp * 512 + cols],
                                           pst[grp % 2][:, 0:cols]))


def scan_p5d(c, ones128f, Ssel, A_sb, dtb_sb, dvec_sb):
    nc = c.nc
    io = c.io
    sb = sbuf
    gate_c = c.cnt["gate"]

    with contextlib.ExitStack() as ph:
        hs2c = sb(c, ph, "sd_hs2", [128, NET * TC], BF16)
        gatec = sb(c, ph, "sd_gate", [128, NET * TC], BF16)
        sgc = gatec
        xpw = sb(c, ph, "sd_xpw", [128, NET * 128], BF16)
        dtw = sb(c, ph, "sd_dtw", [64, ED], BF16 if L1_SCANMODE != "f32d" else F32)
        bsb = sb(c, ph, "sd_bsb", [16, 512], BF16)
        csb = sb(c, ph, "sd_csb", [16, 512], BF16)
        dsr = sb(c, ph, "sd_dsr", [64, 512], BF16 if L1_SCANMODE != "f32d" else F32)
        Bbc = sb(c, ph, "sd_Bbc", [128, NST * TC], BF16)
        Cbc = sb(c, ph, "sd_Cbc", [128, NST * TC], BF16)
        dsb = sb(c, ph, "sd_delta", [128, 2 * TC], F32)
        spZ = sb(c, ph, "sd_spZ", [128, 2 * TC], F32)
        spA = sb(c, ph, "sd_spA", [128, TC], F32)
        spL = sb(c, ph, "sd_spL", [128, 2 * TC], F32)
        spR = sb(c, ph, "sd_spR", [128, 2 * TC], F32)
        du = sb(c, ph, "sd_du", [128, TC], BF16)
        dA = sb(c, ph, "sd_dA", [128, 2 * NST * TC], F32)
        bts = sb(c, ph, "sd_bts", [128, NST * TC], BF16)
        hts = sb(c, ph, "sd_hts", [128, NST * TC], BF16)
        y2sb = hs2c
        state = sb(c, ph, "sd_state", [128, NET * NST], F32)
        psm = psum(c, ph, "sd_psm", [128, 512])
        psd = [psum(c, ph, f"sd_psd{i}", [128, 512]) for i in range(2)]
        psb = [psum(c, ph, f"sd_psb{i}", [128, 512]) for i in range(2)]

        PE_N = 1 + 2 * NST + NET
        TE_N = 3 + 2 * NST
        TV_N = 3 * NET
        TS_N = 1 + 2 * NET

        with nc.Block("p5d") as blk:
            tP0, tE0, tV0, tS0, tX0 = c.snap("tP", "tE", "tV", "tS", "tX")
            ldB0 = c.ldB.snap()
            stA0 = c.stA.snap()

            def pe_ssm(tc):
                return tP0 + PE_N * tc + 1

            def pe_bc(tc, j):
                return tP0 + PE_N * tc + 1 + j + 1

            def pe_delta(tc, et):
                return tP0 + PE_N * tc + 1 + 2 * NST + et + 1

            def te_ssm(tc):
                return tE0 + TE_N * tc + 3

            def te_stage(tc):
                return tE0 + TE_N * tc + 2

            def te_dsr(tc):
                return tE0 + TE_N * tc + 3

            def te_bc(tc, j):
                return tE0 + TE_N * tc + 3 + j + 1

            _pz, _pd, _pe2 = {}, {}, {}
            _cnt = 0
            for _tc in range(NTC):
                for _k in range(NET + 2):
                    if _k < NET:
                        _cnt += 1
                        _pz[_tc * NET + _k] = tV0 + _cnt
                    if 1 <= _k <= NET:
                        _cnt += 1
                        _pd[_tc * NET + _k - 1] = tV0 + _cnt
                    if 2 <= _k <= NET + 1:
                        _cnt += 1
                        _pe2[_tc * NET + _k - 2] = tV0 + _cnt

            def tv_z(tc, et):
                return _pz[tc * NET + et]

            def tv_d(tc, et):
                return _pd[tc * NET + et]

            def tv_e(tc, et):
                return _pe2[tc * NET + et]

            def ts_silu(tc):
                return tS0 + TS_N * tc + 1

            def ts_l(tc, et):
                return tS0 + TS_N * tc + 1 + 2 * et + 1

            def ts_da(tc, et):
                return tS0 + TS_N * tc + 1 + 2 * et + 2

            @blk.sync
            def _(sy):
                sy.wait_ge(c.sem("gate"), gate_c)
                def store_y2(tc2):
                    for hh in range(2):
                        c.stA.issue(sy.dma_start(
                            out=io["y2_h"].rearrange("(a p) t -> p a t", p=128)[:, hh * 8:(hh + 1) * 8,
                                                                               tc2 * TC:(tc2 + 1) * TC],
                            in_=y2sb[:].rearrange("p (a f) -> p a f", a=NET)[:, hh * 8:(hh + 1) * 8]),
                            tc2 % 2)
                for tc in range(NTC):
                    if tc >= 1:
                        c.wait(sy, "tV", tv_e(tc - 1, NET - 1))
                        store_y2(tc - 1)
                        c.stA.wait_all(sy)
                    for hh in range(2):
                        c.ldB.issue(sy.dma_start(
                            out=hs2c[:].rearrange("p (a f) -> p a f", a=NET)[:, hh * 8:(hh + 1) * 8],
                            in_=io["hs2_h"].rearrange("(a p) t -> p a t", p=128)[:, hh * 8:(hh + 1) * 8,
                                                                                tc * TC:(tc + 1) * TC]), 0)
                        c.ldB.issue(sy.dma_start(
                            out=gatec[:].rearrange("p (a f) -> p a f", a=NET)[:, hh * 8:(hh + 1) * 8],
                            in_=io["gate_h"].rearrange("(a p) t -> p a t", p=128)[:, hh * 8:(hh + 1) * 8,
                                                                                 tc * TC:(tc + 1) * TC]), 1)
                c.wait(sy, "tV", tv_e(NTC - 1, NET - 1))
                store_y2(NTC - 1)
                c.stA.wait_all(sy)
                sy.sem_inc(c.sem("gate"), 1)
                c.bump("gate")

            @blk.gpsimd
            def _(g):
                g.wait_ge(c.sem("gate"), gate_c)
                c.grpA.issue(g.dma_start(out=xpw[:].rearrange("p (a f) -> p a f", a=NET),
                                         in_=io["xproj_pad"].rearrange("(a p) f -> p a f", p=128)), 0)
                c.pwld.issue(g.dma_start(out=dtw[:], in_=io["dtw_t"]), 0)
                c.inc("tX", g.memset(state[:], 0.0))

            @blk.scalar
            def _(s):
                s.wait_ge(c.sem("tX"), c._l1_ax)
                if L1_SCANMODE in ("front1", "front2", "front3"):
                    return
                if L1_SCANMODE == "front4":
                    for tc in range(NTC):
                        s.sem_inc(c.sem("tS"), 1)
                        c.bump("tS")
                        for et in range(NET):
                            s.sem_inc(c.sem("tS"), 1)
                            c.bump("tS")
                            s.sem_inc(c.sem("tS"), 1)
                            c.bump("tS")
                    return
                for tc in range(NTC):
                    c.ldB.wait_units(s, 1, ldB0[1] + 32 * (tc + 1))
                    if tc >= 1:
                        s.wait_ge(c.sem("tV"), tv_e(tc - 1, NET - 1))
                    if L1_SCANMODE in ("min2", "min3", "front5"):
                        c.inc("tS", s.copy(sgc[:], gatec[:]))
                    else:
                        c.inc("tS", s.activation(sgc[:], gatec[:], AF.Silu))
                    for et in range(NET):
                        s.wait_ge(c.sem("tV"), tv_z(tc, et))
                        zslice = spZ[:, (et % 2) * TC:(et % 2 + 1) * TC]
                        if L1_SCANMODE in ("min2", "min3", "front5"):
                            c.schain(s, s.copy(spA[:], zslice))
                            c.inc("tS", s.copy(spL[:, (et % 2) * TC:(et % 2 + 1) * TC], spA[:]))
                        else:
                            c.schain(s, s.activation(spA[:], zslice, AF.Abs))
                            c.schain(s, s.activation(spA[:], spA[:], AF.Exp, scale=-1.0))
                            c.inc("tS", s.activation(spL[:, (et % 2) * TC:(et % 2 + 1) * TC],
                                                     spA[:], AF.Ln, bias=1.0))
                        s.wait_ge(c.sem("tV"), tv_d(tc, et))
                        g_ = tc * NET + et
                        if g_ >= 2:
                            s.wait_ge(c.sem("tV"), tv_e((g_ - 2) // NET, (g_ - 2) % NET))
                        dab = (et % 2) * NST * TC
                        for n in range(NST):
                            ins = s.activation(dA[:, dab + n * TC:dab + (n + 1) * TC],
                                               dsb[:, (et % 2) * TC:(et % 2 + 1) * TC], AF.Exp,
                                               scale=A_sb[:, et * NST + n:et * NST + n + 1])
                        c.inc("tS", ins)

            @blk.tensor
            def _(t):
                t.wait_ge(c.sem("tX"), tX0 + 1)
                c.grpA.wait_all(t)
                c.pwld.wait_units(t, 0, c.pwld.cnt[0])
                for tc in range(NTC):
                    c.ldB.wait_units(t, 0, ldB0[0] + 32 * (tc + 1))
                    if tc >= 1:
                        t.wait_ge(c.sem("tE"), te_ssm(tc - 1))
                    for et in range(NET):
                        ins = t.matmul(psm[:], xpw[:, et * 128:(et + 1) * 128],
                                       hs2c[:, et * TC:(et + 1) * TC],
                                       start=(et == 0), stop=(et == NET - 1))
                    c.inc("tP", ins)
                    if L1_SCANMODE == "front1":
                        continue
                    t.wait_ge(c.sem("tE"), te_stage(tc))
                    for n in range(NST):
                        for j, src_ in [(0, bsb), (1, csb)]:
                            jj = 2 * n + j
                            if jj >= 2:
                                t.wait_ge(c.sem("tE"), te_bc(tc, jj - 2))
                            c.inc("tP", t.matmul(psb[j][:], Ssel[:, n * 128:(n + 1) * 128],
                                                 src_[:], start=True, stop=True))
                    if L1_SCANMODE == "front2":
                        continue
                    for et in range(NET):
                        if L1_SCANMODE != "front3" and tc + et > 0:
                            idx = NET * tc + et - 2
                            if idx >= 0:
                                t.wait_ge(c.sem("tV"), tv_z(idx // NET, idx % NET))
                        if et == 0:
                            t.wait_ge(c.sem("tE"), te_dsr(tc))
                        c.inc("tP", t.matmul(psd[et % 2][:], dtw[:, et * 128:(et + 1) * 128],
                                             dsr[:], start=True, stop=True))
                    if L1_SCANMODE == "front3":
                        continue

            @blk.vector
            def _(v):
                v.wait_ge(c.sem("tX"), c._l1_ax)
                for tc in range(NTC):
                    v.wait_ge(c.sem("tP"), pe_ssm(tc))
                    c.vchain(v, v.tensor_copy(bsb[:], psm[0:16, :]))
                    v.sem_inc(c.sem("tE"), 1)
                    c.bump("tE")
                    c.vchain(v, v.tensor_copy(csb[:], psm[32:48, :]))
                    v.sem_inc(c.sem("tE"), 1)
                    c.bump("tE")
                    c.vchain(v, v.tensor_copy(dsr[:], psm[64:128, :]))
                    v.sem_inc(c.sem("tE"), 1)
                    c.bump("tE")
                    for jj in range(2 * NST):
                        n, j = jj // 2, jj % 2
                        v.wait_ge(c.sem("tP"), pe_bc(tc, jj))
                        dst = Bbc if j == 0 else Cbc
                        c.inc("tE", v.tensor_copy(dst[:, n * TC:(n + 1) * TC], psb[j][:]))
                    if L1_SCANMODE in ("front2", "front3", "front5"):
                        for _ in range(3 * NET):
                            v.sem_inc(c.sem("tV"), 1)
                            c.bump("tV")
                        continue
                    def stage_a1(et):
                        g_ = tc * NET + et
                        c.wait(v, "tP", pe_delta(tc, et))
                        if g_ >= 2:
                            c.wait(v, "tS", ts_l((g_ - 2) // NET, (g_ - 2) % NET))
                        zsl = spZ[:, (et % 2) * TC:(et % 2 + 1) * TC]
                        c.vchain(v, v.tensor_scalar(zsl, psd[et % 2][:],
                                                    dtb_sb[:, et:et + 1], None, ALU.add))
                        v.sem_inc(c.sem("tV"), 1)
                        c.bump("tV")
                        c.vchain(v, v.tensor_single_scalar(
                            spR[:, (et % 2) * TC:(et % 2 + 1) * TC], zsl, 0.0, ALU.max))

                    def stage_a2(et):
                        c.wait(v, "tS", ts_l(tc, et))
                        c.vchain(v, v.tensor_tensor(
                            dsb[:, (et % 2) * TC:(et % 2 + 1) * TC],
                            spL[:, (et % 2) * TC:(et % 2 + 1) * TC],
                            spR[:, (et % 2) * TC:(et % 2 + 1) * TC], ALU.add))
                        v.sem_inc(c.sem("tV"), 1)
                        c.bump("tV")

                    def stage_b(et):
                        base = (et % 2) * NST * TC
                        dAf = dA[:, base:base + NST * TC]
                        c.vchain(v, v.tensor_tensor(du[:], dsb[:, (et % 2) * TC:(et % 2 + 1) * TC],
                                                    hs2c[:, et * TC:(et + 1) * TC], ALU.mult))
                        c.vchain(v, v.tensor_tensor(
                            bts[:].rearrange("p (n t) -> p n t", n=NST),
                            du[:].rearrange("p (o t) -> p o t", o=1).broadcast_to([128, NST, TC]),
                            Bbc[:].rearrange("p (n t) -> p n t", n=NST), ALU.mult))
                        v.wait_ge(c.sem("tS"), ts_da(tc, et))
                        for n in range(NST):
                            ins = v.tensor_tensor_scan(
                                hts[:, n * TC:(n + 1) * TC],
                                dA[:, base + n * TC:base + (n + 1) * TC],
                                bts[:, n * TC:(n + 1) * TC],
                                state[:, et * NST + n:et * NST + n + 1],
                                ALU.mult, ALU.add)
                        c.vchain(v, ins)
                        c.vchain(v, v.tensor_copy(
                            state[:, et * NST:(et + 1) * NST].rearrange("p (n o) -> p n o", o=1),
                            hts[:].rearrange("p (n t) -> p n t", n=NST)[:, :, TC - 1:TC]))
                        c.vchain(v, v.tensor_tensor(
                            dAf.rearrange("p (n t) -> p n t", n=NST),
                            hts[:].rearrange("p (n t) -> p n t", n=NST),
                            Cbc[:].rearrange("p (n t) -> p n t", n=NST), ALU.mult))
                        w = NST * TC
                        while w > TC:
                            c.vchain(v, v.tensor_tensor(dA[:, base:base + w // 2],
                                                        dA[:, base:base + w // 2],
                                                        dA[:, base + w // 2:base + w], ALU.add))
                            w //= 2
                        c.vchain(v, v.scalar_tensor_tensor(
                            dA[:, base + TC:base + 2 * TC], hs2c[:, et * TC:(et + 1) * TC],
                            dvec_sb[:, et:et + 1], dA[:, base:base + TC], ALU.mult, ALU.add))
                        if et == 0:
                            v.wait_ge(c.sem("tS"), ts_silu(tc))
                        ins = v.tensor_tensor(y2sb[:, et * TC:(et + 1) * TC],
                                              dA[:, base + TC:base + 2 * TC],
                                              sgc[:, et * TC:(et + 1) * TC], ALU.mult)
                        c.vchain(v, ins)
                        v.sem_inc(c.sem("tV"), 1)
                        c.bump("tV")

                    for k in range(NET + 2):
                        if k < NET:
                            stage_a1(k)
                        if 1 <= k <= NET:
                            stage_a2(k - 1)
                        if 2 <= k <= NET + 1:
                            stage_b(k - 2)


def ffn_p6(c, idn):
    io = c.io


    items_b = [("x", i) for i in range(NTT)]
    quant_transpose_phase(
        c, idn, "p6b", items_b,
        lambda it: D, lambda it: NHT,
        lambda it: io["xn1_h"][it[1] * 128:(it[1] + 1) * 128, :],
        lambda it: io["xq_h"].rearrange("(a p) t -> p a t", p=128)[:, :, it[1] * 128:(it[1] + 1) * 128])
    if L1_STOP == "p6b":
        return

    gateup_p6c(c, idn)
    if L1_STOP == "p6c":
        return
    down_p6d(c, idn)


ICH = [(0, 512), (512, 512), (1024, 512), (1536, 512), (2048, 512), (2560, 256)]


def gateup_p6c(c, idn):
    nc = c.nc
    io = c.io
    sb = sbuf
    gate_e = c.cnt["gate"]

    with contextlib.ExitStack() as ph:
        wg = sb(c, ph, "c_wg", [128, NHT * I1], BF16)
        wu = sb(c, ph, "c_wu", [128, NHT * I1], BF16)
        xq = sb(c, ph, "c_xq", [128, 2 * NHT * 128], BF16)
        gb = sb(c, ph, "c_gb", [128, I1], F32)
        ub = sb(c, ph, "c_ub", [128, I1], F32)
        sg = sb(c, ph, "c_sg", [128, I1], F32)
        qb = sb(c, ph, "c_qb", [128, 2 * I1], BF16)
        sA = sb(c, ph, "c_sA", [128, I1], F32)
        sX = sb(c, ph, "c_sX", [128, I1], F32)
        sC = sb(c, ph, "c_sC", [128, I1], F32)
        sB = sb(c, ph, "c_sB", [128, 2 * (I1 // 16)], F32)
        tst = sb(c, ph, "c_tst", [128, 2 * NIT * 128], BF16)
        wraw = sb(c, ph, "c_wraw", [128, 2 * D], BF16)
        psq = [psum(c, ph, f"c_ps{i}", [128, 512]) for i in range(4)]
        pst = [psum(c, ph, f"c_pt{i}", [128, 512], BF16) for i in range(2)]

        NGR = 2 * len(ICH)

        def tp_thr(i):
            return 0 if i < 0 else NGR * (i + 1)

        def tr_thr(i):
            return 0 if i < 0 else 6 * (i + 1)

        with nc.Block("p6c") as blk:
            tV0, tP0, tP20, tE0, tE20, tS0 = c.snap("tV", "tP", "tP2", "tE", "tE2", "tS")
            qW0c, qL0c = c.snap("qW", "qL")
            ldA0 = c.ldA.snap()
            stA0 = c.stA.snap()
            ldC0 = c.ldC.snap()

            @blk.gpsimd
            def _(g):
                g.wait_ge(c.sem("gate"), gate_e)

            @blk.sync
            def _(sy):
                sy.wait_ge(c.sem("gate"), gate_e)
                for k in range(2 * NIT):
                    m, it = k // NIT, k % NIT
                    if k >= 2:
                        sy.wait_ge(c.sem("qW"), qW0c + 2 * (k - 1))
                    c.ldC.issue(sy.dma_start(
                        out=wraw[:, (k % 2) * D:(k % 2 + 1) * D],
                        in_=io["wgq_h" if m == 0 else "wuq_h"][it * 128:(it + 1) * 128, :]), k)

                def store(i):
                    sy.wait_ge(c.sem("tE2"), tE20 + 6 * (i + 1))
                    c.stA.issue(sy.dma_start(
                        out=io["hqT_h"].rearrange("(a p) t -> p a t", p=128)[:, :, i * 128:(i + 1) * 128],
                        in_=tst[:, (i % 2) * NIT * 128:(i % 2 + 1) * NIT * 128]
                        .rearrange("p (a f) -> p a f", f=128)), i % 2)
                for i in range(NTT):
                    if i >= 2:
                        c.wait(sy, "tP", tP0 + tp_thr(i - 2))
                    c.ldA.issue(sy.dma_start(
                        out=xq[:].rearrange("p (b a f) -> p b a f", b=2, a=NHT)[:, i % 2],
                        in_=io["xq_h"].rearrange("(a p) t -> p a t", p=128)[:, :, i * 128:(i + 1) * 128]), i)
                    if i >= 2:
                        store(i - 2)
                store(NTT - 2)
                store(NTT - 1)
                c.stA.wait_all(sy)
                sy.sem_inc(c.sem("gate"), 1)
                c.bump("gate")

            @blk.tensor
            def _(t):
                for k in range(2 * NIT):
                    c.ldC.wait_units(t, k, ldC0[k % 2] + 16 * (k // 2 + 1))
                    for grp in range(2):
                        gk = 2 * k + grp
                        if gk >= 2:
                            c.wait(t, "qL", qL0c + gk - 1)
                        for q_ in range(4):
                            hh = grp * 4 + q_
                            ins = t.transpose(pst[gk % 2][:, q_ * 128:(q_ + 1) * 128],
                                              wraw[:, (k % 2) * D + hh * 128:
                                                   (k % 2) * D + (hh + 1) * 128],
                                              idn[:])
                        c.inc("qW", ins)
                c.wait(t, "qL", qL0c + 4 * NIT)
                for i in range(NTT):
                    c.ldA.wait_units(t, i, ldA0[i % 2] + 16 * (i // 2 + 1))
                    k = 0
                    for side, wsb in [(0, wg), (1, wu)]:
                        for (off, wdt) in ICH:
                            kk = NGR * i + k
                            if kk >= 4:
                                c.wait(t, "tE", tE0 + (kk - 4) + 1)
                            for hh in range(NHT):
                                ins = t.matmul(
                                    psq[kk % 4][:, 0:wdt],
                                    xq[:, (i % 2) * NHT * 128 + hh * 128:(i % 2) * NHT * 128 + (hh + 1) * 128],
                                    wsb[:, hh * I1 + off:hh * I1 + off + wdt],
                                    start=(hh == 0), stop=(hh == NHT - 1))
                            c.inc("tP", ins)
                            k += 1
                    c.wait(t, "tV", tV0 + 2 * (i + 1))
                    for q in range(NIT):
                        grp = q // 4
                        if tr_thr(i - 1) + grp >= 2:
                            c.wait(t, "tE2", tE20 + tr_thr(i - 1) + grp - 1)
                        ins = t.transpose(pst[grp % 2][:, (q % 4) * 128:(q % 4 + 1) * 128],
                                          qb[:, (i % 2) * I1 + q * 128:(i % 2) * I1 + (q + 1) * 128],
                                          idn[:])
                        if q % 4 == 3 or q == NIT - 1:
                            c.inc("tP2", ins)

            @blk.vector
            def _(v):
                for i in range(NTT):
                    k = 0
                    for side, dst in [(0, gb), (1, ub)]:
                        for (off, wdt) in ICH:
                            kk = NGR * i + k
                            c.wait(v, "tP", tP0 + kk + 1)
                            ins = v.tensor_copy(dst[:, off:off + wdt], psq[kk % 4][:, 0:wdt])
                            if k < NGR - 1:
                                c.inc("tE", ins)
                            else:
                                c.vchain(v, ins)
                                v.sem_inc(c.sem("tE"), 1)
                                c.bump("tE")
                                v.sem_inc(c.sem("tV"), 1)
                                c.bump("tV")
                            k += 1
                    c.wait(v, "tS", tS0 + i + 1)
                    c.vchain(v, v.tensor_tensor(gb[:], sg[:], ub[:], ALU.mult))
                    if i >= 2:
                        c.wait(v, "tP2", tP20 + tr_thr(i - 2))
                    ins = emit_quant(c, v, gb[:], qb[:, (i % 2) * I1:(i % 2 + 1) * I1], I1,
                                     (sA[:], sX[:], sC[:], sB[:, 0:I1 // 16], sB[:, I1 // 16:]))
                    c.inc("tV", ins)

            @blk.scalar
            def _(s):
                for gk in range(4 * NIT):
                    k, grp = gk // 2, gk % 2
                    m, it = k // NIT, k % NIT
                    s.wait_ge(c.sem("qW"), qW0c + gk + 1)
                    dstw = wg if m == 0 else wu
                    c.inc("qL", s.copy(
                        dstw[:].rearrange("p (h i) -> p h i", h=NHT)
                        [:, grp * 4:(grp + 1) * 4, it * 128:(it + 1) * 128],
                        pst[gk % 2][:, 0:512].rearrange("p (h i) -> p h i", i=128)))
                for i in range(NTT):
                    s.wait_ge(c.sem("tV"), tV0 + 2 * i + 1)
                    c.inc("tS", s.activation(sg[:], gb[:], AF.Silu))
                    if i >= 2:
                        c.stA.wait_units(s, i % 2, stA0[i % 2] + 16 * (i // 2))
                    for grp in range(6):
                        s.wait_ge(c.sem("tP2"), tP20 + tr_thr(i - 1) + grp + 1)
                        cols = min(4, NIT - grp * 4) * 128
                        c.inc("tE2", s.copy(tst[:, (i % 2) * NIT * 128 + grp * 512:
                                                (i % 2) * NIT * 128 + grp * 512 + cols],
                                            pst[grp % 2][:, 0:cols]))


def down_p6d(c, idn):
    nc = c.nc
    io = c.io
    sb = sbuf
    gate_e = c.cnt["gate"]

    with contextlib.ExitStack() as ph:
        wd = sb(c, ph, "d_wd", [128, NIT * D], BF16)
        wrawd = sb(c, ph, "d_wraw", [128, 2 * I1], BF16)
        hq = sb(c, ph, "d_hq", [128, 2 * NIT * 128], BF16)
        x1b = sb(c, ph, "d_x1", [128, 2 * D], F32)
        mmb = sb(c, ph, "d_mm", [128, 2 * D], F32)
        ob = sb(c, ph, "d_ob", [128, 2 * D], F32)
        psq = [psum(c, ph, f"d_ps{i}", [128, 512]) for i in range(4)]
        pstd = [psum(c, ph, f"d_pt{i}", [128, 512], BF16) for i in range(2)]

        NGRD = 6

        with nc.Block("p6d") as blk:
            tP0, tE0 = c.snap("tP", "tE")
            qW0d, qL0d = c.snap("qW", "qL")
            qAs0 = c.qAs.snap()
            ldA0 = c.ldA.snap()
            ldB0 = c.ldB.snap()
            ldC0 = c.ldC.snap()
            stG0 = c.stG.snap()

            @blk.gpsimd
            def _(g):
                g.wait_ge(c.sem("gate"), gate_e)
                for k in range(NHT):
                    if k >= 2:
                        g.wait_ge(c.sem("qW"), qW0d + NGRD * (k - 1))
                    c.qAs.issue(g.dma_start(
                        out=wrawd[:, (k % 2) * I1:(k % 2 + 1) * I1],
                        in_=io["wdq_h"][k * 128:(k + 1) * 128, :]), k % 2)
                for i in range(NTT):
                    g.wait_ge(c.sem("tE"), tE0 + 2 * (i + 1))
                    c.stG.issue(g.dma_start(out=io["out"][i * 128:(i + 1) * 128, :],
                                            in_=ob[:, (i % 2) * D:(i % 2 + 1) * D]), i % 2)
                c.stG.wait_all(g)
                g.sem_inc(c.sem("gate"), 1)
                c.bump("gate")

            @blk.sync
            def _(sy):
                sy.wait_ge(c.sem("gate"), gate_e)
                for i in range(NTT):
                    if i >= 2:
                        c.wait(sy, "tP", tP0 + 2 * (i - 1))
                    c.ldA.issue(sy.dma_start(
                        out=hq[:].rearrange("p (b a f) -> p b a f", b=2, a=NIT)[:, i % 2],
                        in_=io["hqT_h"].rearrange("(a p) t -> p a t", p=128)[:, :, i * 128:(i + 1) * 128]), i)
                    if i >= 2:
                        c.wait(sy, "tE", tE0 + 2 * (i - 1))
                    c.ldB.issue(sy.dma_start(out=x1b[:, (i % 2) * D:(i % 2 + 1) * D],
                                             in_=io["x1"][i * 128:(i + 1) * 128, :]), i % 2)
                    c.ldC.issue(sy.dma_start(out=mmb[:, (i % 2) * D:(i % 2 + 1) * D],
                                             in_=io["mam_h"][i * 128:(i + 1) * 128, :]), i % 2)

            @blk.tensor
            def _(t):
                for k in range(NHT):
                    c.qAs.wait_units(t, k % 2, qAs0[k % 2] + 16 * (k // 2 + 1))
                    for grp in range(NGRD):
                        gk = NGRD * k + grp
                        nq_ = min(4, NIT - grp * 4)
                        if gk >= 2:
                            c.wait(t, "qL", qL0d + gk - 1)
                        for q_ in range(nq_):
                            it = grp * 4 + q_
                            ins = t.transpose(pstd[gk % 2][:, q_ * 128:(q_ + 1) * 128],
                                              wrawd[:, (k % 2) * I1 + it * 128:
                                                    (k % 2) * I1 + (it + 1) * 128],
                                              idn[:])
                        c.inc("qW", ins)
                c.wait(t, "qL", qL0d + NGRD * NHT)
                for i in range(NTT):
                    c.ldA.wait_units(t, i, ldA0[i % 2] + 16 * (i // 2 + 1))
                    for dch in range(2):
                        k = i * 2 + dch
                        if k >= 4:
                            c.wait(t, "tE", tE0 + (k - 4) + 1)
                        for it in range(NIT):
                            ins = t.matmul(
                                psq[k % 4][:],
                                hq[:, (i % 2) * NIT * 128 + it * 128:(i % 2) * NIT * 128 + (it + 1) * 128],
                                wd[:, it * D + dch * 512:it * D + (dch + 1) * 512],
                                start=(it == 0), stop=(it == NIT - 1))
                        c.inc("tP", ins)

            @blk.scalar
            def _(s):
                for gk in range(NGRD * NHT):
                    k, grp = gk // NGRD, gk % NGRD
                    nq_ = min(4, NIT - grp * 4)
                    s.wait_ge(c.sem("qW"), qW0d + gk + 1)
                    c.inc("qL", s.copy(
                        wd[:].rearrange("p (it d) -> p it d", it=NIT)
                        [:, grp * 4:grp * 4 + nq_, k * 128:(k + 1) * 128],
                        pstd[gk % 2][:, 0:nq_ * 128].rearrange("p (it d) -> p it d", d=128)))

            @blk.vector
            def _(v):
                for i in range(NTT):
                    b = i % 2
                    c.ldB.wait_units(v, i % 2, ldB0[i % 2] + 16 * (i // 2 + 1))
                    c.ldC.wait_units(v, i % 2, ldC0[i % 2] + 16 * (i // 2 + 1))
                    for dch in range(2):
                        k = i * 2 + dch
                        c.wait(v, "tP", tP0 + k + 1)
                        if i >= 2 and dch == 0:
                            c.stG.wait_units(v, i % 2, stG0[i % 2] + 16 * (i // 2))
                        c.vchain(v, v.tensor_tensor(ob[:, b * D + dch * 512:b * D + (dch + 1) * 512],
                                                    psq[k % 4][:],
                                                    x1b[:, b * D + dch * 512:b * D + (dch + 1) * 512],
                                                    ALU.add))
                        ins = v.tensor_tensor(ob[:, b * D + dch * 512:b * D + (dch + 1) * 512],
                                              ob[:, b * D + dch * 512:b * D + (dch + 1) * 512],
                                              mmb[:, b * D + dch * 512:b * D + (dch + 1) * 512],
                                              ALU.add)
                        c.vchain(v, ins)
                        v.sem_inc(c.sem("tE"), 1)
                        c.bump("tE")


def build_l1(c, idn, iot, ones128, ones128f, onescol):
    nc = c.nc
    io = c.io
    sb = sbuf
    gate_prev = c.cnt["gate"]

    Ssel = sb(c, c.es, "Ssel", [16, NST * 128], BF16)
    A_sb = sb(c, c.es, "A_sb", [128, NET * NST], F32)
    dtb_sb = sb(c, c.es, "dtb_sb", [128, NET], F32)
    dvec_sb = sb(c, c.es, "dvec_sb", [128, NET], F32)
    cwb_sb = sb(c, c.es, "cwb_sb", [128, NET * KC + NET], F32)

    sselit = sb(c, c.es, "sselit", [16, NST * 128], I32)

    with nc.Block("l1i") as blk:
        tX0, tS0 = c.snap("tX", "tS")

        @blk.gpsimd
        def _(g):
            g.wait_ge(c.sem("gate"), gate_prev)
            c.inc("tX", g.iota(sselit[:].rearrange("k (n p) -> k n p", p=128),
                               [[1, NST], [0, 128]], channel_multiplier=-1))
            c.grpA.issue(g.dma_start(out=A_sb[:], in_=io["alog_t"]), 0)
            c.grpA.issue(g.dma_start(out=dtb_sb[:], in_=io["dtb_t"]), 0)
            c.grpA.issue(g.dma_start(out=dvec_sb[:], in_=io["dvec_t"]), 0)
            c.grpA.issue(g.dma_start(out=cwb_sb[:, 0:NET * KC], in_=io["convw_t"]), 0)
            c.grpA.issue(g.dma_start(out=cwb_sb[:, NET * KC:], in_=io["convb_t"]), 0)

        @blk.vector
        def _(v):
            c.wait(v, "tX", tX0 + 1)
            c.inc("tX", v.tensor_single_scalar(Ssel[:], sselit[:], 0, ALU.is_equal))
            c.wait(v, "tS", tS0 + 1)
            c.inc("tX", v.tensor_scalar(A_sb[:], A_sb[:], -1.0, None, ALU.mult))

        @blk.scalar
        def _(s):
            c.grpA.wait_all(s)
            c.inc("tS", s.activation(A_sb[:], A_sb[:], AF.Exp))

    c._l1_ax = c.cnt["tX"]

    with contextlib.ExitStack() as ph:
        xn1fm = sb(c, ph, "xn1fm", [128, NHT * T], BF16)

        with contextlib.ExitStack() as p1s:
            xbuf = sb(c, p1s, "l1_xbuf", [128, 2 * D], F32)
            sq = sb(c, p1s, "l1_sq", [128, 2 * D], F32)
            stats = sb(c, p1s, "l1_stats", [128, 2 * 8], F32)
            xnf = sb(c, p1s, "l1_xnf", [128, 2 * D], F32)
            xn_tm = sb(c, p1s, "l1_xn", [128, 2 * D], BF16)
            pst = [psum(c, p1s, f"l1_pst{i}", [128, 512], BF16) for i in range(2)]

            with nc.Block("p5a") as blk:
                tS0, tS20, tV0, tX0, tP0, tE0 = c.snap("tS", "tS2", "tV", "tX", "tP", "tE")
                ldA0 = c.ldA.snap()
                stG0 = c.stG.snap()

                @blk.sync
                def _(sy):
                    sy.wait_ge(c.sem("gate"), gate_prev)
                    for i in range(NTT):
                        if i >= 2:
                            c.wait(sy, "tX", tX0 + (i - 2) + 1)
                        c.ldA.issue(sy.dma_start(out=xbuf[:, (i % 2) * D:(i % 2 + 1) * D],
                                                 in_=io["x1"][i * 128:(i + 1) * 128, :]), i)

                @blk.scalar
                def _(s):
                    for i in range(NTT):
                        b = i % 2
                        c.ldA.wait_units(s, i, ldA0[i % 2] + 16 * (i // 2 + 1))
                        c.inc("tS", s.activation(sq[:, b * D:(b + 1) * D],
                                                 xbuf[:, b * D:(b + 1) * D], AF.Square))
                        c.wait(s, "tV", tV0 + i + 1)
                        emit_ln_scalar(c, s, stats[:, b * 8:(b + 1) * 8])

                @blk.vector
                def _(v):
                    def emit_evac(v, i):
                        for half in range(2):
                            k = i * 2 + half
                            c.wait(v, "tP", tP0 + k + 1)
                            for q in range(4):
                                hh = half * 4 + q
                                ins = v.tensor_copy(
                                    xn1fm[:, hh * T + i * 128:hh * T + (i + 1) * 128],
                                    pst[half][:, q * 128:(q + 1) * 128])
                            c.inc("tE", ins)
                    for i in range(NTT):
                        b = i % 2
                        st_ = stats[:, b * 8:(b + 1) * 8]
                        c.wait(v, "tS", tS0 + i + 1)
                        emit_ln_stats_vec(c, v, xbuf[:, b * D:(b + 1) * D],
                                          sq[:, b * D:(b + 1) * D], st_)
                        c.wait(v, "tS2", tS20 + i + 1)
                        if i >= 2:
                            c.wait(v, "tP", tP0 + 2 * (i - 1))
                            c.stG.wait_units(v, i % 2, stG0[i % 2] + 16 * (i // 2))
                        c.vchain(v, v.tensor_scalar(xnf[:, b * D:(b + 1) * D],
                                                    xbuf[:, b * D:(b + 1) * D],
                                                    st_[:, 2:3], st_[:, 6:7],
                                                    ALU.add, ALU.mult))
                        c.inc("tX", v.tensor_copy(xn_tm[:, b * D:(b + 1) * D],
                                                  xnf[:, b * D:(b + 1) * D]))
                        if i >= 1:
                            emit_evac(v, i - 1)
                    emit_evac(v, NTT - 1)

                @blk.tensor
                def _(t):
                    for i in range(NTT):
                        c.wait(t, "tX", tX0 + i + 1)
                        for half in range(2):
                            k = i * 2 + half
                            if k >= 2:
                                c.wait(t, "tE", tE0 + k - 1)
                            for q in range(4):
                                hh = half * 4 + q
                                ins = t.transpose(
                                    pst[half][:, q * 128:(q + 1) * 128],
                                    xn_tm[:, (i % 2) * D + hh * 128:(i % 2) * D + (hh + 1) * 128],
                                    idn[:])
                            c.inc("tP", ins)

                @blk.sync
                def _(sy2):
                    pass

                @blk.gpsimd
                def _(g):
                    for i in range(NTT):
                        c.wait(g, "tX", tX0 + i + 1)
                        c.stG.issue(g.dma_start(out=io["xn1_h"][i * 128:(i + 1) * 128, :],
                                                in_=xnf[:, (i % 2) * D:(i % 2 + 1) * D]), i % 2)
                    g.wait_ge(c.sem("tE"), tE0 + 2 * NTT)
                    for hh in range(NHT):
                        c.grpB.issue(g.dma_start(out=io["xn1fm_h"][hh * 128:(hh + 1) * 128, :],
                                                 in_=xn1fm[:, hh * T:(hh + 1) * T]), 0)
                    c.grpB.wait_all(g)
                    c.stG.wait_all(g)
                    g.sem_inc(c.sem("gate"), 1)
                    c.bump("gate")

        with contextlib.ExitStack() as p2:
            ipw = sb(c, p2, "ipw", [128, NHT * 2 * ED], BF16)
            stage = sb(c, p2, "p5b_stage", [128, 4 * 512], BF16)
            psq = [psum(c, p2, f"p5b_ps{i}", [128, 512]) for i in range(4)]

            with nc.Block("p5b") as blk:
                tP0, tE0 = c.snap("tP", "tE")
                stA0 = c.stA.snap()
                gate_a = c.cnt["gate"]

                @blk.gpsimd
                def _(g):
                    g.wait_ge(c.sem("gate"), gate_a)
                    for hh in range(NHT):
                        c.grpA.issue(g.dma_start(out=ipw[:, hh * 2 * ED:(hh + 1) * 2 * ED],
                                                 in_=io["inproj_t"][hh * 128:(hh + 1) * 128, :]), 0)

                @blk.tensor
                def _(t):
                    c.grpA.wait_all(t)
                    k = 0
                    for et in range(2 * NET):
                        for tcc in range(NTC):
                            if k >= 4:
                                c.wait(t, "tE", tE0 + (k - 4) + 1)
                            for hh in range(NHT):
                                ins = t.matmul(
                                    psq[k % 4][:],
                                    ipw[:, hh * 2 * ED + et * 128:hh * 2 * ED + (et + 1) * 128],
                                    xn1fm[:, hh * T + tcc * TC:hh * T + (tcc + 1) * TC],
                                    start=(hh == 0), stop=(hh == NHT - 1))
                            c.inc("tP", ins)
                            k += 1

                @blk.vector
                def _(v):
                    k = 0
                    for et in range(2 * NET):
                        for tcc in range(NTC):
                            c.wait(v, "tP", tP0 + k + 1)
                            if k >= 4:
                                c.stA.wait_units(v, k % 4, stA0[k % 4] + 16 * (k // 4))
                            c.inc("tE", v.tensor_copy(stage[:, (k % 4) * 512:(k % 4 + 1) * 512],
                                                      psq[k % 4][:]))
                            k += 1

                @blk.sync
                def _(sy):
                    k = 0
                    for et in range(2 * NET):
                        hbm = io["hsr_h"] if et < NET else io["gate_h"]
                        row = et * 128 if et < NET else (et - NET) * 128
                        for tcc in range(NTC):
                            sy.wait_ge(c.sem("tE"), tE0 + k + 1)
                            c.stA.issue(sy.dma_start(
                                out=hbm[row:row + 128, tcc * TC:(tcc + 1) * TC],
                                in_=stage[:, (k % 4) * 512:(k % 4 + 1) * 512]), k % 4)
                            k += 1
                    c.stA.wait_all(sy)
                    sy.sem_inc(c.sem("gate"), 1)
                    c.bump("gate")

    if L1_STOP == "p5b":
        return l1_fin(c)
    gate_b = c.cnt["gate"]
    with contextlib.ExitStack() as ph:
        hbuf = sb(c, ph, "cv_h", [128, 2 * (T + 4)], BF16)
        cbuf = sb(c, ph, "cv_c", [128, 2 * T], F32)
        obuf = sb(c, ph, "cv_o", [128, 2 * T], BF16)

        with nc.Block("p5c") as blk:
            tV0, tS0, tX0 = c.snap("tV", "tS", "tX")
            ldA0 = c.ldA.snap()
            stA0 = c.stA.snap()

            @blk.sync
            def _(sy):
                sy.wait_ge(c.sem("gate"), gate_b)

                def store(et):
                    c.wait(sy, "tS", tS0 + et + 1)
                    c.stA.issue(sy.dma_start(out=io["hs2_h"][et * 128:(et + 1) * 128, :],
                                             in_=obuf[:, (et % 2) * T:(et % 2 + 1) * T]), et % 2)
                for et in range(NET):
                    if et >= 2:
                        c.wait(sy, "tV", tV0 + (et - 2) + 1)
                    c.ldA.issue(sy.dma_start(out=hbuf[:, (et % 2) * (T + 4) + 4:(et % 2) * (T + 4) + 4 + T],
                                             in_=io["hsr_h"][et * 128:(et + 1) * 128, :]), et)
                    if et >= 1:
                        store(et - 1)
                store(NET - 1)
                c.stA.wait_all(sy)
                sy.sem_inc(c.sem("gate"), 1)
                c.bump("gate")

            @blk.gpsimd
            def _(g):
                c.inc("tX", g.memset(hbuf[:, 1:4], 0.0))
                c.inc("tX", g.memset(hbuf[:, T + 4 + 1:T + 4 + 4], 0.0))

            @blk.vector
            def _(v):
                v.wait_ge(c.sem("tX"), max(tX0 + 2, c._l1_ax))
                for et in range(NET):
                    b = (et % 2) * (T + 4)
                    cb = cbuf[:, (et % 2) * T:(et % 2 + 1) * T]
                    c.ldA.wait_units(v, et, ldA0[et % 2] + 16 * (et // 2 + 1))
                    if et >= 2:
                        c.wait(v, "tS", tS0 + et - 1)
                    c.vchain(v, v.tensor_scalar(cb, hbuf[:, b + 1:b + 1 + T],
                                                cwb_sb[:, et * KC:et * KC + 1],
                                                cwb_sb[:, NET * KC + et:NET * KC + et + 1],
                                                ALU.mult, ALU.add))
                    for kk in range(1, KC - 1):
                        c.vchain(v, v.scalar_tensor_tensor(cb, hbuf[:, b + 1 + kk:b + 1 + kk + T],
                                                           cwb_sb[:, et * KC + kk:et * KC + kk + 1],
                                                           cb, ALU.mult, ALU.add))
                    c.inc("tV", v.scalar_tensor_tensor(cb, hbuf[:, b + KC:b + KC + T],
                                                       cwb_sb[:, et * KC + KC - 1:et * KC + KC],
                                                       cb, ALU.mult, ALU.add))
                    c.wait(v, "tV", c.cnt["tV"])

            @blk.scalar
            def _(s):
                for et in range(NET):
                    c.wait(s, "tV", tV0 + et + 1)
                    if et >= 2:
                        c.stA.wait_units(s, et % 2, stA0[et % 2] + 16 * (et // 2))
                    c.inc("tS", s.activation(obuf[:, (et % 2) * T:(et % 2 + 1) * T],
                                             cbuf[:, (et % 2) * T:(et % 2 + 1) * T], AF.Silu))

    if L1_STOP == "p5c":
        return l1_fin(c)
    scan_p5d(c, ones128f, Ssel, A_sb, dtb_sb, dvec_sb)
    if L1_STOP == "p5d":
        return l1_fin(c)

    gate_d = c.cnt["gate"]
    with contextlib.ExitStack() as ph:
        opw = sb(c, ph, "opw", [128, NET * D], BF16)
        y2b = sb(c, ph, "y2b", [128, 2 * NET * TC], BF16)
        mst = sb(c, ph, "mst", [128, 4 * 512], F32)
        psq = [psum(c, ph, f"p5e_ps{i}", [128, 512]) for i in range(4)]

        with nc.Block("p5e") as blk:
            tP0, tE0 = c.snap("tP", "tE")
            ldA0 = c.ldA.snap()
            stA0 = c.stA.snap()

            @blk.gpsimd
            def _(g):
                g.wait_ge(c.sem("gate"), gate_d)
                for et in range(NET):
                    c.grpA.issue(g.dma_start(out=opw[:, et * D:(et + 1) * D],
                                             in_=io["outproj_t"][et * 128:(et + 1) * 128, :]), 0)

            @blk.sync
            def _(sy):
                sy.wait_ge(c.sem("gate"), gate_d)

                def store(k):
                    sy.wait_ge(c.sem("tE"), tE0 + k + 1)
                    t128g = k // 2
                    dch = k % 2
                    c.stA.issue(sy.dma_start(
                        out=io["mam_h"][t128g * 128:(t128g + 1) * 128, dch * 512:(dch + 1) * 512],
                        in_=mst[:, (k % 4) * 512:(k % 4 + 1) * 512]), k % 4)
                for tcc in range(NTC):
                    if tcc >= 2:
                        c.wait(sy, "tP", tP0 + (tcc - 1) * 8)
                    for hh in range(2):
                        c.ldA.issue(sy.dma_start(
                            out=y2b[:].rearrange("p (b a f) -> p b a f", b=2, a=NET)[:, tcc % 2, hh * 8:(hh + 1) * 8],
                            in_=io["y2_h"].rearrange("(a p) t -> p a t", p=128)[:, hh * 8:(hh + 1) * 8,
                                                                               tcc * TC:(tcc + 1) * TC]), tcc)
                    if tcc >= 1:
                        for k in range((tcc - 1) * 8, tcc * 8):
                            store(k)
                for k in range((NTC - 1) * 8, NTC * 8):
                    store(k)
                c.stA.wait_all(sy)
                sy.sem_inc(c.sem("gate"), 1)
                c.bump("gate")

            @blk.tensor
            def _(t):
                c.grpA.wait_all(t)
                k = 0
                for tcc in range(NTC):
                    c.ldA.wait_units(t, tcc, ldA0[tcc % 2] + 32 * (tcc // 2 + 1))
                    for t128 in range(4):
                        for dch in range(2):
                            if k >= 4:
                                c.wait(t, "tE", tE0 + (k - 4) + 1)
                            for et in range(NET):
                                ins = t.matmul(
                                    psq[k % 4][:],
                                    y2b[:, (tcc % 2) * NET * TC + et * TC + t128 * 128:
                                           (tcc % 2) * NET * TC + et * TC + (t128 + 1) * 128],
                                    opw[:, et * D + dch * 512:et * D + (dch + 1) * 512],
                                    start=(et == 0), stop=(et == NET - 1))
                            c.inc("tP", ins)
                            k += 1

            @blk.vector
            def _(v):
                k = 0
                for tcc in range(NTC):
                    for t128 in range(4):
                        for dch in range(2):
                            c.wait(v, "tP", tP0 + k + 1)
                            if k >= 4:
                                c.stA.wait_units(v, k % 4, stA0[k % 4] + 16 * (k // 4))
                            c.inc("tE", v.tensor_copy(mst[:, (k % 4) * 512:(k % 4 + 1) * 512],
                                                      psq[k % 4][:]))
                            k += 1

    if L1_STOP == "p5e":
        return l1_fin(c)
    ffn_p6(c, idn)
    l1_fin(c)


def l1_fin(c):
    gfin = c.cnt["gate"]
    with c.nc.Block("l1_fin") as blk:
        @blk.sync
        def _(sy):
            sy.wait_ge(c.sem("gate"), gfin)


def _prep_weights_l0(inputs):
    f32 = np.float32
    asc = np.ascontiguousarray
    w = {}
    w["wq"] = asc(inputs["L0_attn_Wq"])
    w["wk"] = asc(inputs["L0_attn_Wk"])
    w["wv"] = asc(inputs["L0_attn_Wv"])
    w["wo"] = asc(inputs["L0_attn_Wo"])
    w["bq_t"] = asc(inputs["L0_attn_bq"].reshape(NHT, 128).T)
    w["bk_t"] = asc(inputs["L0_attn_bk"].reshape(NHT, 128).T)
    w["bv_r"] = asc(inputs["L0_attn_bv"].reshape(1, D))
    w["bo_r"] = asc(inputs["L0_attn_bo"].reshape(1, D))
    w["gamma_r"] = asc(inputs["L0_attn_gamma"].reshape(1, D))
    w["beta_r"] = asc(inputs["L0_attn_beta"].reshape(1, D))
    w["gate_w"] = asc(inputs["L0_ffn_gate_w"])
    w["gate_b_r"] = asc(inputs["L0_ffn_gate_b"].reshape(1, NE))
    w["w1"] = asc(inputs["L0_ffn_w1"].reshape(NE, NHT, 128, 2, FH)
                  .transpose(0, 3, 2, 1, 4).reshape(NE * 2 * 128, NHT * FH))
    w["b1r"] = asc(inputs["L0_ffn_b1"].reshape(NE, 32, 128)
                   .transpose(0, 2, 1).reshape(NE * 128, 32))
    w["w2"] = asc(inputs["L0_ffn_w2"].reshape(NE, 32, 128, D)
                  .transpose(0, 2, 1, 3).reshape(NE * 128, 32 * D))
    w["b2"] = asc(inputs["L0_ffn_b2"])
    return {k: v.astype(f32, copy=False) for k, v in w.items()}


def _prep_weights_l1(inputs):
    f32 = np.float32
    asc = np.ascontiguousarray
    w = {}
    w["inproj_t"] = asc(inputs["L1_attn_in_proj_w"].T)
    w["convw_t"] = asc(inputs["L1_attn_conv_w"].reshape(NET, 128, KC).transpose(1, 0, 2).reshape(128, NET * KC))
    w["convb_t"] = asc(inputs["L1_attn_conv_b"].reshape(NET, 128).T)
    xp = inputs["L1_attn_x_proj_w"]
    pad = np.zeros((ED, 128), dtype=f32)
    pad[:, 0:16] = xp[64:80, :].T
    pad[:, 32:48] = xp[80:96, :].T
    pad[:, 64:128] = xp[0:64, :].T
    w["xproj_pad"] = pad
    w["dtw_t"] = asc(inputs["L1_attn_dt_w"].T)
    w["dtb_t"] = asc(inputs["L1_attn_dt_b"].reshape(NET, 128).T)
    w["alog_t"] = asc(inputs["L1_attn_A_log"].reshape(NET, 128, NST).transpose(1, 0, 2).reshape(128, NET * NST))
    w["dvec_t"] = asc(inputs["L1_attn_D"].reshape(NET, 128).T)
    w["outproj_t"] = asc(inputs["L1_attn_out_proj_w"].T)
    w["wgate"] = asc(inputs["L1_ffn_w_gate"])
    w["wup"] = asc(inputs["L1_ffn_w_up"])
    w["wdown"] = asc(inputs["L1_ffn_w_down"])
    return {k: v.astype(f32, copy=False) for k, v in w.items()}


FUSED = True


def kernel(**inputs):
    x_full = np.ascontiguousarray(np.asarray(inputs["L0_attn_x"], dtype=np.float32))
    wl0 = _prep_weights_l0(inputs)
    wl1 = _prep_weights_l1(inputs)

    if FUSED:
        nc, _ = build(do_l0=True, do_l1=True)
        in_maps = []
        for b in range(8):
            m = {"x": np.ascontiguousarray(x_full[b])}
            m.update(wl0)
            m.update(wl1)
            in_maps.append(m)
        res = run_bass_kernel_spmd(nc, in_maps, core_ids=list(range(8)))
        out = np.stack([res.results[b]["out"] for b in range(8)], axis=0)
        return out.astype(np.float32)

    nc, _ = build(do_l0=True, do_l1=False)
    in_maps = []
    for b in range(8):
        m = {"x": np.ascontiguousarray(x_full[b])}
        m.update(wl0)
        in_maps.append(m)
    res = run_bass_kernel_spmd(nc, in_maps, core_ids=list(range(8)))
    x1 = np.stack([res.results[b]["x1"] for b in range(8)], axis=0)

    nc2, _ = build(do_l0=False, do_l1=True)
    in_maps2 = []
    for b in range(8):
        m = {"x1": np.ascontiguousarray(x1[b])}
        m.update(wl1)
        in_maps2.append(m)
    res2 = run_bass_kernel_spmd(nc2, in_maps2, core_ids=list(range(8)))
    out = np.stack([res2.results[b]["out"] for b in range(8)], axis=0)
    return out.astype(np.float32)


# version 12 (final)
# speedup vs baseline: 1.0380x; 1.0380x over previous
import contextlib
import numpy as np
import concourse.bass as bass
import concourse.mybir as mybir
from concourse.bass_utils import run_bass_kernel_spmd

F32 = mybir.dt.float32
BF16 = mybir.dt.bfloat16
I32 = mybir.dt.int32
AF = mybir.ActivationFunctionType
ALU = mybir.AluOpType
AX = mybir.AxisListType.X

T = 4096
D = 1024
NTT = T // 128
NHT = D // 128
TC = 512
NTC = T // TC
HEADS = 16
DH = 64
NE = 8
FF = 4096
FH = FF // 2
NFH = FH // 128
ED = 2048
NET = ED // 128
NST = 16
KC = 4
I1 = 2816
NIT = I1 // 128
NEG = -1.0e30
NJT = 16
TPAD = NJT * 512

IN, OUT, INT = "ExternalInput", "ExternalOutput", "Internal"

DBG_REQ = set()
L1_STOP = ""
L1_SCANMODE = ""


class Ctx:
    def __init__(self, nc, es):
        self.nc = nc
        self.es = es
        self.io = {}
        self.sems = {}
        self.cnt = {}
        for name in ["qL", "qW", "qS", "qS2", "qA", "tP", "tP2", "tE", "tE2",
                     "tS", "tS2", "tV", "tX", "gate", "tCv", "tCs"]:
            self.sems[name] = es.enter_context(nc.semaphore(name))
            self.cnt[name] = 0

    def sem(self, name):
        return self.sems[name]

    def inc(self, name, ins, k=1):
        ins.then_inc(self.sems[name], k)
        self.cnt[name] += k
        return self.cnt[name]

    def bump(self, name, k=1):
        self.cnt[name] += k
        return self.cnt[name]

    def wait(self, eng, name, val):
        eng.wait_ge(self.sems[name], val)

    def chain(self, eng, name, ins):
        v = self.inc(name, ins)
        eng.wait_ge(self.sems[name], v)
        return ins

    def vchain(self, eng, ins):
        return self.chain(eng, "tCv", ins)

    def schain(self, eng, ins):
        return self.chain(eng, "tCs", ins)

    def snap(self, *names):
        if len(names) == 1:
            return self.cnt[names[0]]
        return tuple(self.cnt[n] for n in names)


class DSlots:
    def __init__(self, c, name, n):
        self.c = c
        self.n = n
        self.sems = [c.es.enter_context(c.nc.semaphore(f"{name}{i}")) for i in range(n)]
        self.cnt = [0] * n

    def issue(self, ins, slot, units=16):
        ins.then_inc(self.sems[slot % self.n], units)
        self.cnt[slot % self.n] += units
        return self.cnt[slot % self.n]

    def wait_units(self, eng, slot, units):
        eng.wait_ge(self.sems[slot % self.n], units)

    def wait_all(self, eng):
        for i in range(self.n):
            eng.wait_ge(self.sems[i], self.cnt[i])

    def snap(self):
        return list(self.cnt)


def dram(c, name, shape, dtype, kind):
    if kind == INT and name in DBG_REQ:
        kind = OUT
    ap = c.nc.dram_tensor(name, shape, dtype, kind=kind).ap()
    c.io[name] = ap
    return ap


def sbuf(c, stack, name, shape, dtype):
    return stack.enter_context(c.nc.sbuf_tensor(name, shape, dtype))


def psum(c, stack, name, shape, dtype=F32):
    return stack.enter_context(c.nc.psum_tensor(name, shape, dtype))


def build(do_l0=True, do_l1=True):
    nc = bass.Bass("TRN2", target_bir_lowering=False, detect_race_conditions=False)
    es = contextlib.ExitStack()
    c = Ctx(nc, es)

    if do_l0:
        dram(c, "x", [T, D], F32, IN)
        for w in ["wq", "wk", "wv", "wo"]:
            dram(c, w, [D, D], F32, IN)
        dram(c, "bq_t", [128, NHT], F32, IN)
        dram(c, "bk_t", [128, NHT], F32, IN)
        for r in ["bv_r", "bo_r", "gamma_r", "beta_r"]:
            dram(c, r, [1, D], F32, IN)
        dram(c, "gate_w", [D, NE], F32, IN)
        dram(c, "gate_b_r", [1, NE], F32, IN)
        dram(c, "w1", [NE * 2 * 128, NHT * FH], F32, IN)
        dram(c, "b1r", [NE * 128, 32], F32, IN)
        dram(c, "w2", [NE * 128, (FF // 128) * D], F32, IN)
        dram(c, "b2", [NE, D], F32, IN)
        dram(c, "xn0fm_h", [D, T], BF16, INT)
        dram(c, "xn0tm_h", [T + 1, D], BF16, INT)
        dram(c, "moe2_h", [T + 128, D], F32, INT)
        dram(c, "invh", [TPAD, 1], I32, INT)
        dram(c, "wslh", [TPAD, 1], F32, INT)
        dram(c, "qT_h", [D, T], BF16, INT)
        dram(c, "kT_h", [D, T], BF16, INT)
        dram(c, "v_h", [T, D], BF16, INT)
        dram(c, "attn_h", [T, D], BF16, INT)
        dram(c, "moe_h", [T, D], F32, INT)
    x1_kind = INT if (do_l0 and do_l1) else (OUT if do_l0 else IN)
    if x1_kind == INT and "x1" in DBG_REQ:
        x1_kind = OUT
    c.io["x1"] = nc.dram_tensor("x1", [T, D], F32, kind=x1_kind).ap()
    if do_l1:
        dram(c, "inproj_t", [D, 2 * ED], F32, IN)
        dram(c, "convw_t", [128, NET * KC], F32, IN)
        dram(c, "convb_t", [128, NET], F32, IN)
        dram(c, "xproj_pad", [ED, 128], F32, IN)
        dram(c, "dtw_t", [64, ED], F32, IN)
        dram(c, "dtb_t", [128, NET], F32, IN)
        dram(c, "alog_t", [128, NET * NST], F32, IN)
        dram(c, "dvec_t", [128, NET], F32, IN)
        dram(c, "outproj_t", [ED, D], F32, IN)
        dram(c, "wgate", [I1, D], F32, IN)
        dram(c, "wup", [I1, D], F32, IN)
        dram(c, "wdown", [D, I1], F32, IN)
        dram(c, "out", [T, D], F32, OUT)
        dram(c, "xn1_h", [T, D], F32, INT)
        dram(c, "xn1fm_h", [D, T], BF16, INT)
        dram(c, "hsr_h", [ED, T], BF16, INT)
        dram(c, "hs2_h", [ED, T], BF16, INT)
        dram(c, "y2_h", [ED, T], BF16, INT)
        dram(c, "gate_h", [ED, T], BF16, INT)
        dram(c, "mam_h", [T, D], F32, INT)
        dram(c, "hqT_h", [I1, T], BF16, INT)
        dram(c, "xq_h", [D, T], BF16, INT)
        dram(c, "wgq_h", [I1, D], BF16, INT)
        dram(c, "wuq_h", [I1, D], BF16, INT)
        dram(c, "wdq_h", [D, I1], BF16, INT)

    c.ldA = DSlots(c, "ldA", 2)
    c.ldB = DSlots(c, "ldB", 3)
    c.stA = DSlots(c, "stA", 4)
    c.wLa = DSlots(c, "wLa", 2)
    c.w2s = DSlots(c, "w2s", 2)
    c.b2s = DSlots(c, "b2s", 1)
    c.grpA = DSlots(c, "grpA", 1)
    c.grpB = DSlots(c, "grpB", 1)
    c.qAs = DSlots(c, "qAs", 8)
    c.pwld = DSlots(c, "pwld", 2)
    c.stG = DSlots(c, "stG", 2)
    c.grpC = DSlots(c, "grpC", 1)
    c.ldC = DSlots(c, "ldC", 2)

    idn = sbuf(c, es, "idn", [128, 128], BF16)
    iot = sbuf(c, es, "iot", [128, 512], I32)
    ones128 = sbuf(c, es, "ones128", [1, 128], BF16)
    onescol = sbuf(c, es, "onescol", [64, 1], BF16)
    ones128f = sbuf(c, es, "ones128f", [1, 128], F32)
    cpool = sbuf(c, es, "cpool", [128, 8], F32)
    CONSTS = [1e-5, -0.5, 0.125, -1.0, 0.25]
    for j, vv in enumerate(CONSTS):
        nc.const_aps.register(F32, vv, cpool[:, j:j + 1])

    with nc.Block("init") as blk:
        @blk.gpsimd
        def _(g):
            g.iota(iot[:], [[1, 512]], channel_multiplier=-1)
            g.memset(ones128[:], 1.0)
            g.memset(ones128f[:], 1.0)
            for j, vv in enumerate(CONSTS):
                g.memset(cpool[:, j:j + 1], vv)
            c.inc("tX", g.memset(onescol[:], 1.0))

        @blk.vector
        def _(v):
            c.wait(v, "tX", c.cnt["tX"])
            c.inc("tX", v.tensor_single_scalar(idn[:], iot[:, 0:128], 0, ALU.is_equal))

    if do_l0:
        build_l0(c, idn, iot, ones128, ones128f, onescol)
    if do_l1:
        build_l1(c, idn, iot, ones128, ones128f, onescol)

    es.close()
    return nc, c


def emit_ln_stats_vec(c, v, src, sq, stats):
    c.vchain(v, v.reduce_sum(stats[:, 0:1], src, axis=AX, negate=True))
    c.vchain(v, v.reduce_sum(stats[:, 1:2], sq, axis=AX))
    c.vchain(v, v.tensor_scalar(stats[:, 2:3], stats[:, 0:1], 1.0 / D, None, ALU.mult))
    c.vchain(v, v.tensor_scalar(stats[:, 3:4], stats[:, 1:2], 1.0 / D, None, ALU.mult))
    c.vchain(v, v.tensor_tensor(stats[:, 4:5], stats[:, 2:3], stats[:, 2:3], ALU.mult))
    c.vchain(v, v.tensor_tensor(stats[:, 4:5], stats[:, 3:4], stats[:, 4:5], ALU.subtract))
    return c.inc("tV", v.tensor_copy(stats[:, 5:6], stats[:, 4:5]))


def emit_ln_scalar(c, s, stats):
    c.schain(s, s.activation(stats[:, 6:7], stats[:, 5:6], AF.Ln, bias=1e-5))
    return c.inc("tS2", s.activation(stats[:, 6:7], stats[:, 6:7], AF.Exp, scale=-0.5))


def build_l0(c, idn, iot, ones128, ones128f, onescol):
    nc = c.nc
    io = c.io

    with contextlib.ExitStack() as ph:
        xn0fm = sbuf(c, ph, "xn0fm", [128, NHT * T], BF16)

        with contextlib.ExitStack() as p1s:
            xbuf = sbuf(c, p1s, "p1_xbuf", [128, 2 * D], F32)
            sq = sbuf(c, p1s, "p1_sq", [128, 2 * D], F32)
            stats = sbuf(c, p1s, "p1_stats", [128, 2 * 8], F32)
            xn_tm = sbuf(c, p1s, "p1_xn", [128, 2 * D], BF16)
            zrow = sbuf(c, p1s, "p1_zrow", [1, D], BF16)
            pst = [psum(c, p1s, f"p1_pst{i}", [128, 512], BF16) for i in range(2)]

            with nc.Block("p1") as blk:
                qL0, tS0, tS20, tV0, tX0, tP0, tE0 = c.snap(
                    "qL", "tS", "tS2", "tV", "tX", "tP", "tE")

                ldA0 = c.ldA.snap()
                stG0 = c.stG.snap()

                @blk.sync
                def _(sy):
                    sy.wait_ge(c.sem("tX"), 2)
                    for i in range(NTT):
                        if i >= 2:
                            c.wait(sy, "tX", tX0 + (i - 2) + 1)
                        c.ldA.issue(sy.dma_start(out=xbuf[:, (i % 2) * D:(i % 2 + 1) * D],
                                                 in_=io["x"][i * 128:(i + 1) * 128, :]), i)

                @blk.scalar
                def _(s):
                    for i in range(NTT):
                        b = i % 2
                        c.ldA.wait_units(s, i, ldA0[i % 2] + 16 * (i // 2 + 1))
                        c.inc("tS", s.activation(sq[:, b * D:(b + 1) * D],
                                                 xbuf[:, b * D:(b + 1) * D], AF.Square))
                        c.wait(s, "tV", tV0 + i + 1)
                        emit_ln_scalar(c, s, stats[:, b * 8:(b + 1) * 8])

                def emit_evac(v, i):
                    for half in range(2):
                        k = i * 2 + half
                        c.wait(v, "tP", tP0 + k + 1)
                        for q in range(4):
                            hh = half * 4 + q
                            ins = v.tensor_copy(
                                xn0fm[:, hh * T + i * 128:hh * T + (i + 1) * 128],
                                pst[half][:, q * 128:(q + 1) * 128])
                        c.inc("tE", ins)

                @blk.vector
                def _(v):
                    for i in range(NTT):
                        b = i % 2
                        st_ = stats[:, b * 8:(b + 1) * 8]
                        c.wait(v, "tS", tS0 + i + 1)
                        emit_ln_stats_vec(c, v, xbuf[:, b * D:(b + 1) * D],
                                          sq[:, b * D:(b + 1) * D], st_)
                        c.wait(v, "tS2", tS20 + i + 1)
                        if i >= 2:
                            c.wait(v, "tP", tP0 + 2 * (i - 1))
                            c.stG.wait_units(v, i, stG0[i % 2] + 16 * (i // 2))
                        c.inc("tX", v.tensor_scalar(xn_tm[:, b * D:(b + 1) * D],
                                                    xbuf[:, b * D:(b + 1) * D],
                                                    st_[:, 2:3], st_[:, 6:7],
                                                    ALU.add, ALU.mult))
                        if i >= 1:
                            emit_evac(v, i - 1)
                    emit_evac(v, NTT - 1)

                @blk.tensor
                def _(t):
                    t.wait_ge(c.sem("tX"), 2)
                    for i in range(NTT):
                        c.wait(t, "tX", tX0 + i + 1)
                        for half in range(2):
                            k = i * 2 + half
                            if k >= 2:
                                c.wait(t, "tE", tE0 + k - 1)
                            for q in range(4):
                                hh = half * 4 + q
                                ins = t.transpose(
                                    pst[half][:, q * 128:(q + 1) * 128],
                                    xn_tm[:, (i % 2) * D + hh * 128:(i % 2) * D + (hh + 1) * 128],
                                    idn[:])
                            c.inc("tP", ins)

                @blk.gpsimd
                def _(g):
                    g.memset(zrow[:], 0.0)
                    for i in range(NTT):
                        g.wait_ge(c.sem("tX"), tX0 + i + 1)
                        c.stG.issue(g.dma_start(
                            out=io["xn0tm_h"][i * 128:(i + 1) * 128, :],
                            in_=xn_tm[:, (i % 2) * D:(i % 2 + 1) * D]), i)
                    c.grpB.issue(g.dma_start(out=io["xn0tm_h"][T:T + 1, :],
                                             in_=zrow[:]), 0)
                    g.wait_ge(c.sem("tE"), tE0 + 2 * NTT)
                    for hh in range(NHT):
                        c.grpB.issue(g.dma_start(out=io["xn0fm_h"][hh * 128:(hh + 1) * 128, :],
                                                 in_=xn0fm[:, hh * T:(hh + 1) * T]), 0)
                    c.stG.wait_all(g)
                    c.grpB.wait_all(g)
                    g.sem_inc(c.sem("gate"), 1)
                    c.bump("gate")

        with contextlib.ExitStack() as p2:
            wq_sb = sbuf(c, p2, "wq_sb", [128, NHT * D], BF16)
            wk_sb = sbuf(c, p2, "wk_sb", [128, NHT * D], BF16)
            wv_sb = sbuf(c, p2, "wv_sb", [128, NHT * D], BF16)
            bq_sb = sbuf(c, p2, "bq_sb", [128, NHT], F32)
            bk_sb = sbuf(c, p2, "bk_sb", [128, NHT], F32)
            bvbc = sbuf(c, p2, "bvbc", [128, D], F32)
            bv_row = sbuf(c, p2, "bv_row", [1, D], F32)
            stage = sbuf(c, p2, "p2a_stage", [128, 4 * 512], BF16)
            psq = [psum(c, p2, f"psq{i}", [128, 512]) for i in range(4)]

            NQK = 2 * NHT * NTC

            with nc.Block("p2a") as blk:
                qW0, tP0, tP20, tE0, tE20, qS0, tX0 = c.snap(
                    "qW", "tP", "tP2", "tE", "tE2", "qS", "tX")
                stA0 = c.stA.snap()
                gate_p1 = c.cnt["gate"]

                @blk.gpsimd
                def _(g):
                    for name, buf in [("wq", wq_sb), ("wk", wk_sb), ("wv", wv_sb)]:
                        for hh in range(NHT):
                            c.grpA.issue(g.dma_start(out=buf[:, hh * D:(hh + 1) * D],
                                                     in_=io[name][hh * 128:(hh + 1) * 128, :]), 0)
                    c.grpA.issue(g.dma_start(out=bq_sb[:], in_=io["bq_t"]), 0)
                    c.grpA.issue(g.dma_start(out=bk_sb[:], in_=io["bk_t"]), 0)
                    c.grpA.issue(g.dma_start(out=bv_row[:], in_=io["bv_r"]), 0)

                @blk.tensor
                def _(t):
                    c.grpA.wait_all(t)
                    for dch in range(2):
                        c.inc("tP2", t.matmul(psq[dch][:], ones128f[:],
                                              bv_row[:, dch * 512:(dch + 1) * 512],
                                              start=True, stop=True))
                    c.wait(t, "tX", tX0 + 1)
                    k = 0
                    for wsb in [wq_sb, wk_sb]:
                        for dt_ in range(NHT):
                            for tcc in range(NTC):
                                if k >= 4:
                                    c.wait(t, "tE", tE0 + (k - 4) + 1)
                                for hh in range(NHT):
                                    ins = t.matmul(
                                        psq[k % 4][:],
                                        wsb[:, hh * D + dt_ * 128:hh * D + (dt_ + 1) * 128],
                                        xn0fm[:, hh * T + tcc * TC:hh * T + (tcc + 1) * TC],
                                        start=(hh == 0), stop=(hh == NHT - 1))
                                c.inc("tP", ins)
                                k += 1
                    for i in range(NTT):
                        for dch in range(2):
                            if k >= 4:
                                kprev = k - 4
                                if kprev < NQK:
                                    c.wait(t, "tE", tE0 + kprev + 1)
                                else:
                                    c.wait(t, "tE2", tE20 + (kprev - NQK) + 1)
                            for hh in range(NHT):
                                ins = t.matmul(
                                    psq[k % 4][:],
                                    xn0fm[:, hh * T + i * 128:hh * T + (i + 1) * 128],
                                    wv_sb[:, hh * D + dch * 512:hh * D + (dch + 1) * 512],
                                    start=(hh == 0), stop=(hh == NHT - 1))
                            c.inc("tP", ins)
                            k += 1

                @blk.scalar
                def _(s):
                    k = 0
                    for bsb in [bq_sb, bk_sb]:
                        for dt_ in range(NHT):
                            for tcc in range(NTC):
                                s.wait_ge(c.sem("tP"), tP0 + k + 1)
                                if k >= 4:
                                    c.stA.wait_units(s, k, stA0[k % 4] + 16 * (k // 4))
                                c.inc("tE", s.activation(
                                    stage[:, (k % 4) * 512:(k % 4 + 1) * 512],
                                    psq[k % 4][:], AF.Identity, bias=bsb[:, dt_:dt_ + 1]))
                                k += 1

                @blk.vector
                def _(v):
                    c.wait(v, "tP2", tP20 + 2)
                    v.tensor_copy(bvbc[:, 0:512], psq[0][:])
                    c.inc("tX", v.tensor_copy(bvbc[:, 512:1024], psq[1][:]))
                    v.wait_ge(c.sem("tE"), tE0 + NQK)
                    for i in range(NTT):
                        for dch in range(2):
                            kk = NQK + i * 2 + dch
                            v.wait_ge(c.sem("tP"), tP0 + kk + 1)
                            if kk >= 4:
                                c.stA.wait_units(v, kk, stA0[kk % 4] + 16 * (kk // 4))
                            c.inc("tE2", v.tensor_tensor(
                                stage[:, (kk % 4) * 512:(kk % 4 + 1) * 512],
                                psq[kk % 4][:], bvbc[:, dch * 512:(dch + 1) * 512], ALU.add))

                @blk.sync
                def _(sy):
                    k = 0
                    for hbm in [io["qT_h"], io["kT_h"]]:
                        for dt_ in range(NHT):
                            for tcc in range(NTC):
                                sy.wait_ge(c.sem("tE"), tE0 + k + 1)
                                c.stA.issue(sy.dma_start(
                                    out=hbm[dt_ * 128:(dt_ + 1) * 128, tcc * TC:(tcc + 1) * TC],
                                    in_=stage[:, (k % 4) * 512:(k % 4 + 1) * 512]), k)
                                k += 1
                    for i in range(NTT):
                        for dch in range(2):
                            sy.wait_ge(c.sem("tE2"), tE20 + i * 2 + dch + 1)
                            c.stA.issue(sy.dma_start(
                                out=io["v_h"][i * 128:(i + 1) * 128, dch * 512:(dch + 1) * 512],
                                in_=stage[:, (k % 4) * 512:(k % 4 + 1) * 512]), k)
                            k += 1
                    c.stA.wait_all(sy)
                    sy.sem_inc(c.sem("gate"), 1)
                    c.bump("gate")

    gate_p2a = c.cnt["gate"]
    with contextlib.ExitStack() as p2:
        q_aug = sbuf(c, p2, "q_aug", [65, T], BF16)
        k_aug = sbuf(c, p2, "k_aug", [65, T], BF16)
        v_sb = sbuf(c, p2, "v_sb", [128, NTT * 65], BF16)
        q2 = sbuf(c, p2, "q2", [64, T], BF16)
        nrm = sbuf(c, p2, "nrm", [1, 2 * T + 8], BF16)
        arow = None
        ptb = sbuf(c, p2, "ptb", [128, 4 * 512], BF16)
        rlb = sbuf(c, p2, "rlb", [1, 512], F32)
        rlbc = sbuf(c, p2, "rlbc", [64, 512], F32)
        maskT = sbuf(c, p2, "maskT", [128, 4 * 512], F32)
        oT_sb = sbuf(c, p2, "oT_sb", [128, NHT * T], BF16)
        ones64f = sbuf(c, p2, "ones64f", [1, 64], F32)
        p2ps = contextlib.ExitStack()
        psS = [psum(c, p2ps, f"psS{i}", [128, 512]) for i in range(4)]
        psO = [psum(c, p2ps, f"psO{i}", [65, 512]) for i in range(2)]
        psN = psum(c, p2ps, "psN", [64, 512])

        with nc.Block("p2b_init") as blk:
            @blk.vector
            def _(v):
                v.wait_ge(c.sem("gate"), gate_p2a)
                for j in range(4):
                    ins = v.tensor_scalar(maskT[:, j * 512:(j + 1) * 512], iot[:],
                                          j * 128, NEG, ALU.is_lt, ALU.mult)
                c.inc("tX", ins)

            @blk.gpsimd
            def _(g):
                g.wait_ge(c.sem("gate"), gate_p2a)
                g.memset(k_aug[64:65, :], 1.0)
                g.memset(ones64f[:], 1.0)
                c.inc("tX", g.memset(v_sb[:], 1.0))

        for h in range(HEADS):
            attn_head(c, h, q_aug, k_aug, v_sb, q2, nrm, arow, ptb, rlb, rlbc,
                      maskT, oT_sb, ones64f, psS, psO, psN, ones128, onescol)
        p2ps.close()

        with contextlib.ExitStack() as pc:
            wo_sb = sbuf(c, pc, "wo_sb", [128, NHT * D], BF16)
            rows = sbuf(c, pc, "p2c_rows", [1, 3 * D], F32)
            bobc = sbuf(c, pc, "bobc", [128, D], F32)
            gbc = sbuf(c, pc, "gbc", [128, D], F32)
            bbc = sbuf(c, pc, "bbc", [128, D], F32)
            ybuf = sbuf(c, pc, "ybuf", [128, 2 * D], F32)
            sq = sbuf(c, pc, "p2c_sq", [128, 2 * D], F32)
            stats = sbuf(c, pc, "p2c_stats", [128, 2 * 8], F32)
            aout = sbuf(c, pc, "aout", [128, 2 * D], BF16)
            psY = [psum(c, pc, f"psY{i}", [128, 512]) for i in range(4)]

            oT_done = c.cnt["tE2"]

            with nc.Block("p2c") as blk:
                qW0, tP0, tP20, tE0, tV0, tS0, tS20, tX0, qS0, tE20 = c.snap(
                    "qW", "tP", "tP2", "tE", "tV", "tS", "tS2", "tX", "qS", "tE2")
                stA0 = c.stA.snap()

                @blk.gpsimd
                def _(g):
                    for hh in range(NHT):
                        c.grpA.issue(g.dma_start(out=wo_sb[:, hh * D:(hh + 1) * D],
                                                 in_=io["wo"][hh * 128:(hh + 1) * 128, :]), 0)
                    c.grpA.issue(g.dma_start(out=rows[:, 0:D], in_=io["bo_r"]), 0)
                    c.grpA.issue(g.dma_start(out=rows[:, D:2 * D], in_=io["gamma_r"]), 0)
                    c.grpA.issue(g.dma_start(out=rows[:, 2 * D:3 * D], in_=io["beta_r"]), 0)

                @blk.tensor
                def _(t):
                    c.grpA.wait_all(t)
                    t.wait_ge(c.sem("tE2"), oT_done)
                    for j in range(3):
                        if j >= 2:
                            c.wait(t, "tX", tX0 + j - 1)
                        for dch in range(2):
                            ins = t.matmul(psY[(j * 2 + dch) % 4][:], ones128f[:],
                                           rows[:, j * D + dch * 512:j * D + (dch + 1) * 512],
                                           start=True, stop=True)
                        c.inc("tP2", ins)
                    c.wait(t, "tX", tX0 + 3)
                    for i in range(NTT):
                        for dch in range(2):
                            k = i * 2 + dch
                            if k >= 4:
                                c.wait(t, "tE", tE0 + (k - 4) + 1)
                            for hh in range(NHT):
                                ins = t.matmul(
                                    psY[k % 4][:],
                                    oT_sb[:, hh * T + i * 128:hh * T + (i + 1) * 128],
                                    wo_sb[:, hh * D + dch * 512:hh * D + (dch + 1) * 512],
                                    start=(hh == 0), stop=(hh == NHT - 1))
                            c.inc("tP", ins)

                @blk.vector
                def _(v):
                    for j, dst in enumerate([bobc, gbc, bbc]):
                        c.wait(v, "tP2", tP20 + j + 1)
                        v.tensor_copy(dst[:, 0:512], psY[(j * 2) % 4][:])
                        c.inc("tX", v.tensor_copy(dst[:, 512:1024], psY[(j * 2 + 1) % 4][:]))
                    for i in range(NTT):
                        b = i % 2
                        st_ = stats[:, b * 8:(b + 1) * 8]
                        for dch in range(2):
                            k = i * 2 + dch
                            c.wait(v, "tP", tP0 + k + 1)
                            ins = v.tensor_tensor(
                                ybuf[:, b * D + dch * 512:b * D + (dch + 1) * 512],
                                psY[k % 4][:], bobc[:, dch * 512:(dch + 1) * 512], ALU.add)
                            c.vchain(v, ins)
                            v.sem_inc(c.sem("tE"), 1)
                            c.bump("tE")
                        c.wait(v, "tS", tS0 + i + 1)
                        emit_ln_stats_vec(c, v, ybuf[:, b * D:(b + 1) * D],
                                          sq[:, b * D:(b + 1) * D], st_)
                        c.wait(v, "tS2", tS20 + i + 1)
                        c.vchain(v, v.tensor_scalar(sq[:, b * D:(b + 1) * D], ybuf[:, b * D:(b + 1) * D],
                                        st_[:, 2:3], st_[:, 6:7], ALU.add, ALU.mult))
                        c.vchain(v, v.tensor_tensor(sq[:, b * D:(b + 1) * D], sq[:, b * D:(b + 1) * D],
                                        gbc[:], ALU.mult))
                        if i >= 2:
                            c.stA.wait_units(v, i % 2, stA0[i % 2] + 16 * (i // 2))
                        c.inc("tE2", v.tensor_tensor(aout[:, b * D:(b + 1) * D],
                                                     sq[:, b * D:(b + 1) * D], bbc[:], ALU.add))

                @blk.scalar
                def _(s):
                    for i in range(NTT):
                        b = i % 2
                        s.wait_ge(c.sem("tE"), tE0 + 2 * (i + 1))
                        c.inc("tS", s.activation(sq[:, b * D:(b + 1) * D],
                                                 ybuf[:, b * D:(b + 1) * D], AF.Square))
                        c.wait(s, "tV", tV0 + i + 1)
                        emit_ln_scalar(c, s, stats[:, b * 8:(b + 1) * 8])

                @blk.sync
                def _(sy):
                    for i in range(NTT):
                        sy.wait_ge(c.sem("tE2"), tE20 + i + 1)
                        c.stA.issue(sy.dma_start(out=io["attn_h"][i * 128:(i + 1) * 128, :],
                                                 in_=aout[:, (i % 2) * D:(i % 2 + 1) * D]), i % 2)
                    c.stA.wait_all(sy)
                    sy.sem_inc(c.sem("gate"), 1)
                    c.bump("gate")

    moe_p3(c, ones128, ones128f, idn, iot)

    gate_p3 = c.cnt["gate"]
    with contextlib.ExitStack() as ph:
        xb = sbuf(c, ph, "p4_x", [128, 2 * D], F32)
        ab = sbuf(c, ph, "p4_a", [128, 2 * D], BF16)
        mb = sbuf(c, ph, "p4_m", [128, 2 * D], F32)
        x1b = sbuf(c, ph, "p4_o", [128, 2 * D], F32)

        with nc.Block("p4") as blk:
            qL0, tV0, qS0 = c.snap("qL", "tV", "qS")

            ldA0 = c.ldA.snap()
            stG0 = c.stG.snap()

            @blk.sync
            def _(sy):
                sy.wait_ge(c.sem("gate"), gate_p3)
                for i in range(NTT):
                    b = i % 2
                    if i >= 2:
                        c.wait(sy, "tV", tV0 + (i - 2) + 1)
                    c.ldA.issue(sy.dma_start(out=xb[:, b * D:(b + 1) * D],
                                             in_=io["x"][i * 128:(i + 1) * 128, :]), i)
                    c.ldA.issue(sy.dma_start(out=ab[:, b * D:(b + 1) * D],
                                             in_=io["attn_h"][i * 128:(i + 1) * 128, :]), i)
                    c.ldA.issue(sy.dma_start(out=mb[:, b * D:(b + 1) * D],
                                             in_=io["moe2_h"][i * 128:(i + 1) * 128, :]), i)

            @blk.vector
            def _(v):
                for i in range(NTT):
                    b = i % 2
                    c.ldA.wait_units(v, i, ldA0[i % 2] + 48 * (i // 2 + 1))
                    if i >= 2:
                        c.stG.wait_units(v, i % 2, stG0[i % 2] + 16 * (i // 2))
                    c.vchain(v, v.tensor_tensor(x1b[:, b * D:(b + 1) * D], xb[:, b * D:(b + 1) * D],
                                    mb[:, b * D:(b + 1) * D], ALU.add))
                    c.inc("tV", v.tensor_tensor(x1b[:, b * D:(b + 1) * D],
                                                x1b[:, b * D:(b + 1) * D],
                                                ab[:, b * D:(b + 1) * D], ALU.add))

            @blk.gpsimd
            def _(g):
                for i in range(NTT):
                    g.wait_ge(c.sem("tV"), tV0 + i + 1)
                    c.stG.issue(g.dma_start(out=io["x1"][i * 128:(i + 1) * 128, :],
                                            in_=x1b[:, (i % 2) * D:(i % 2 + 1) * D]), i % 2)
                c.stG.wait_all(g)
                g.sem_inc(c.sem("gate"), 1)
                c.bump("gate")

    gfin = c.cnt["gate"]
    with nc.Block("l0_fin") as blk:
        @blk.sync
        def _(sy):
            sy.wait_ge(c.sem("gate"), gfin)


def attn_head(c, h, q_aug, k_aug, v_sb, q2, nrm, arow, ptb, rlb, rlbc,
              maskT, oT_sb, ones64f, psS, psO, psN, ones128, onescol):
    nc = c.nc
    io = c.io
    ht, hp = h // 2, (h % 2) * 64

    with nc.Block(f"h{h}a") as blk:
        qL0, tS0, tP20, tV0 = c.snap("qL", "tS", "tP2", "tV")
        prev_done = c.cnt["tE2"]
        initX = c.cnt["tX"]

        @blk.sync
        def _(sy):
            sy.wait_ge(c.sem("tX"), initX)
            if h > 0:
                sy.wait_ge(c.sem("tE2"), prev_done)
            c.ldB.issue(sy.dma_start(out=q_aug[0:64, :],
                                     in_=io["qT_h"][h * DH:(h + 1) * DH, :]), 0)
            c.ldB.issue(sy.dma_start(out=k_aug[0:64, :],
                                     in_=io["kT_h"][h * DH:(h + 1) * DH, :]), 1)
            c.ldB.issue(sy.dma_start(
                out=v_sb[:].rearrange("p (a f) -> p a f", a=NTT)[:, :, 0:64],
                in_=io["v_h"].rearrange("(a p) d -> p a d", p=128)[:, :, h * DH:(h + 1) * DH]), 2)

        @blk.scalar
        def _(s):
            c.ldB.wait_units(s, 0, 16 * (h + 1))
            c.inc("tS", s.activation(q2[:], q_aug[0:64, :], AF.Square))
            for cc in range(NTC):
                s.wait_ge(c.sem("tP2"), tP20 + cc + 1)
                c.inc("tS", s.activation(nrm[0:1, cc * 512:(cc + 1) * 512],
                                         psN[0:1, :], AF.Sqrt))
            c.ldB.wait_units(s, 1, 16 * (h + 1))
            c.inc("tS", s.activation(q2[:], k_aug[0:64, :], AF.Square))
            for cc in range(NTC):
                s.wait_ge(c.sem("tP2"), tP20 + NTC + cc + 1)
                c.inc("tS", s.activation(nrm[0:1, T + cc * 512:T + (cc + 1) * 512],
                                         psN[0:1, :], AF.Sqrt))

        @blk.tensor
        def _(t):
            c.wait(t, "tS", tS0 + 1)
            for cc in range(NTC):
                if cc > 0:
                    c.wait(t, "tS", tS0 + 1 + cc)
                c.inc("tP2", t.matmul(psN[0:1, :], onescol[:],
                                      q2[:, cc * 512:(cc + 1) * 512],
                                      start=True, stop=True))
            c.wait(t, "tS", tS0 + NTC + 2)
            for cc in range(NTC):
                if cc > 0:
                    c.wait(t, "tS", tS0 + NTC + 2 + cc)
                c.inc("tP2", t.matmul(psN[0:1, :], onescol[:],
                                      q2[:, cc * 512:(cc + 1) * 512],
                                      start=True, stop=True))

        @blk.vector
        def _(v):
            c.wait(v, "tS", tS0 + 2 * NTC + 2)
            c.vchain(v, v.reduce_max(rlb[0:1, 0:1], nrm[0:1, T:2 * T], axis=AX))
            c.inc("tV", v.tensor_scalar(q_aug[64:65, :], nrm[0:1, 0:T],
                                        rlb[0:1, 0:1], -1.0,
                                        ALU.mult, ALU.mult))

    kt_list = []
    for g in range(NTC):
        for kt in range(4 * g + 4):
            j = kt - 4 * g
            kt_list.append((g, kt, j if 0 <= j <= 3 else None))
    NK = len(kt_list)
    g_first = {}
    idx = 0
    for g in range(NTC):
        g_first[g] = idx
        idx += 4 * g + 4

    def av_incs(n_tiles):
        return n_tiles

    with nc.Block(f"h{h}b") as blk:
        sS0, sM0, sE0, sA0, sV0, sP0, sO0 = c.snap(
            "tP", "tX", "tS", "tP2", "tV", "tS2", "tE2")

        def emit_av(t, li):
            g, kt, j = kt_list[li]
            first = (kt == 0)
            last = (kt == 4 * g + 3)
            if first and g >= 2:
                c.wait(t, "tE2", sO0 + (g - 2) + 1)
            c.wait(t, "tS", sE0 + li + 1)
            c.inc("tP2", t.matmul(psO[g % 2][:],
                                  v_sb[:, kt * 65:(kt + 1) * 65],
                                  ptb[:, (li % 4) * 512:(li % 4 + 1) * 512],
                                  start=first, stop=last))
            if last:
                c.wait(t, "tV", sV0 + g + 1)
                c.inc("tS2", t.matmul(psN[:], ones64f[:], rlb[:],
                                      start=True, stop=True))

        @blk.tensor
        def _(t):
            c.wait(t, "tV", sV0)
            c.ldB.wait_units(t, 2, 16 * (h + 1))
            for li in range(NK):
                g, kt, j = kt_list[li]
                if li >= 4:
                    c.wait(t, "tS", sE0 + (li - 4) + 1)
                c.inc("tP", t.matmul(psS[li % 4][:],
                                     k_aug[:, kt * 128:(kt + 1) * 128],
                                     q_aug[:, g * 512:(g + 1) * 512],
                                     start=True, stop=True))
                if li - 2 >= 0:
                    emit_av(t, li - 2)
            emit_av(t, NK - 2)
            emit_av(t, NK - 1)

        @blk.vector
        def _(v):
            nm = 0
            for g in range(NTC):
                for jj in range(4):
                    li = g_first[g] + 4 * g + jj
                    v.wait_ge(c.sem("tP"), sS0 + li + 1)
                    nm += 1
                    c.inc("tX", v.tensor_tensor(psS[li % 4][:], psS[li % 4][:],
                                                maskT[:, jj * 512:(jj + 1) * 512],
                                                ALU.add))
                v.wait_ge(c.sem("tP2"), sA0 + g_first[g] + 4 * g + 4)
                c.inc("tV", v.reciprocal(rlb[:], psO[g % 2][64:65, :]))
                c.wait(v, "tS2", sP0 + g + 1)
                c.vchain(v, v.tensor_copy(rlbc[:], psN[:]))
                c.inc("tE2", v.tensor_tensor(
                    oT_sb[hp:hp + 64, ht * T + g * 512:ht * T + (g + 1) * 512],
                    psO[g % 2][0:64, :], rlbc[:], ALU.mult))

        @blk.scalar
        def _(s):
            nmask = 0
            for li in range(NK):
                g, kt, j = kt_list[li]
                s.wait_ge(c.sem("tP"), sS0 + li + 1)
                if j is not None:
                    nmask += 1
                    s.wait_ge(c.sem("tX"), sM0 + nmask)
                if li >= 4:
                    s.wait_ge(c.sem("tP2"), sA0 + av_incs(li - 4 + 1))
                c.inc("tS", s.activation(ptb[:, (li % 4) * 512:(li % 4 + 1) * 512],
                                         psS[li % 4][:], AF.Exp, scale=0.125))


def moe_p3(c, ones128, ones128f, idn, iot):
    nc = c.nc
    io = c.io
    gate_prev = c.cnt["gate"]

    with contextlib.ExitStack() as ph:
        invs = sbuf(c, ph, "sm_invs", [128, 64], I32)
        wsl = sbuf(c, ph, "sm_wsl", [128, 64], F32)
        iw1 = sbuf(c, ph, "sm_iw1", [128, 2 * NJT], I32)
        iw2 = sbuf(c, ph, "sm_iw2", [128, NJT], I32)
        iwb = sbuf(c, ph, "sm_iwb", [128, NJT], I32)
        ejint = sbuf(c, ph, "sm_ejint", [1, NJT], I32)

        with contextlib.ExitStack() as rt:
            gw_sb = sbuf(c, rt, "gw_sb", [128, NHT * NE], BF16)
            gb_row = sbuf(c, rt, "gb_row", [1, NE], F32)
            gbbc = sbuf(c, rt, "gbbc", [128, NE], F32)
            OHall = sbuf(c, rt, "sm_oh", [128, NTT * NE], F32)
            PFXall = sbuf(c, rt, "sm_pfx", [128, NTT * NE], F32)
            mxall = sbuf(c, rt, "sm_mx", [128, NTT], F32)
            desti = sbuf(c, rt, "sm_dst", [128, NTT], I32)
            tokid = sbuf(c, rt, "sm_tok", [128, NTT], I32)
            iw1f = sbuf(c, rt, "sm_iw1f", [128, 2 * NJT], F32)
            iw2f = sbuf(c, rt, "sm_iw2f", [128, NJT], F32)
            iota8i = sbuf(c, rt, "sm_io8i", [128, NE], I32)
            iota8bc = sbuf(c, rt, "sm_io8", [128, NE], F32)
            m99 = sbuf(c, rt, "sm_m99", [128, NE], F32)
            iotapm = sbuf(c, rt, "sm_iopm", [128, 1], F32)
            iotapf = sbuf(c, rt, "sm_iop", [128, 1], F32)
            pfh1 = sbuf(c, rt, "sm_pfh1", [128, 1], F32)
            LTs = sbuf(c, rt, "sm_lts", [128, 128], F32)
            o128c = sbuf(c, rt, "sm_o128c", [128, 1], F32)
            o32c = sbuf(c, rt, "sm_o32c", [32, 1], F32)
            sel32it = sbuf(c, rt, "sm_s32i", [32, 32 * 128], I32)
            SEL32 = sbuf(c, rt, "sm_s32", [32, 32 * 128], F32)
            valTf = sbuf(c, rt, "sm_vtf", [128, 64], F32)
            valT = sbuf(c, rt, "sm_vt", [128, 64], I32)
            cntT = sbuf(c, rt, "sm_cntT", [NE, NTT], F32)
            cqA = sbuf(c, rt, "sm_cqA", [NE, NTT], F32)
            cqB = sbuf(c, rt, "sm_cqB", [NE, NTT], F32)
            combo = sbuf(c, rt, "sm_cmb", [NE, NTT + 1], F32)
            EBs = sbuf(c, rt, "sm_eb", [32, NE], F32)
            BASE = sbuf(c, rt, "sm_base", [32, NE], F32)
            ejbcf = sbuf(c, rt, "sm_ejbc", [128, NJT], F32)
            rowp = sbuf(c, rt, "sm_row", [1, 64], F32)
            rowi = sbuf(c, rt, "sm_rowi", [1, 16], I32)
            dsti = sbuf(c, rt, "sm_dsti", [128, 4], I32)
            ejrow = sbuf(c, rt, "sm_ejr", [1, NJT], F32)
            ejsc = sbuf(c, rt, "sm_ejs", [1, NE], F32)
            idn8f = sbuf(c, rt, "sm_id8", [NE, NE], F32)
            gsc = sbuf(c, rt, "p3_gsc", [128, 40], F32)
            destf = sbuf(c, rt, "sm_dstf", [128, 4], F32)
            xg = sbuf(c, rt, "p3_xg", [128, 2 * NHT * 128], BF16)

            psG = [psum(c, rt, f"psG{i}", [128, 64]) for i in range(2)]
            psR = [psum(c, rt, f"psR{i}", [128, 16]) for i in range(2)]
            psCT = psum(c, rt, "psCT", [NE, NTT])

            rts = DSlots(c, "rts", 1)

            with nc.Block("p3a") as blk:
                tP0, tP20, tV0, tX0, tS0, tS20, tE0 = c.snap(
                    "tP", "tP2", "tV", "tX", "tS", "tS2", "tE")
                ldA0 = c.ldA.snap()

                def pos_gate(i):
                    return tP0 + i + 1

                def pos_pfx(i):
                    return tP20 + 1 + 2 * i + 1

                def pos_cnt(i):
                    return tP20 + 1 + 2 * i + 2

                pos_trans, pos_baseb, pos_ejbc = (
                    tP20 + 66, tP20 + 67, tP20 + 68)

                def pos_bb(i):
                    return tP20 + 68 + i + 1

                pos_valt = tX0 + 6

                def pos_oh(i):
                    return tX0 + 10 + i + 1

                pos_cnts, pos_combo, pos_pbx, pos_ej, pos_iw = (
                    tX0 + 43, tX0 + 44, tX0 + 45, tX0 + 46, tX0 + 47)

                def pos_exp(i):
                    return tV0 + i + 1

                def pos_dest(i):
                    return tV0 + 32 + i + 1

                def pos_pfxe(i):
                    return tE0 + i + 1

                pos_tote, pos_ebe, pos_basee, pos_ejbce = (
                    tE0 + 33, tE0 + 34, tE0 + 35, tE0 + 36)

                def pos_bbe(i):
                    return tE0 + 36 + i + 1

                def pos_sexp(i):
                    return tS0 + i + 1

                pos_gpc = tS20 + 6

                @blk.gpsimd
                def _(g):
                    g.wait_ge(c.sem("gate"), gate_prev)
                    c.grpA.issue(g.dma_start(
                        out=gw_sb[:].rearrange("p (a e) -> p a e", a=NHT),
                        in_=io["gate_w"].rearrange("(a p) e -> p a e", p=128)), 0)
                    c.grpA.issue(g.dma_start(out=gb_row[:], in_=io["gate_b_r"]), 0)
                    c.inc("tS2", g.iota(sel32it[:].rearrange("k (i m) -> k i m", m=128),
                                        [[1, 32], [0, 128]], channel_multiplier=-1))
                    c.inc("tS2", g.iota(tokid[:], [[128, NTT]], channel_multiplier=1))
                    c.inc("tS2", g.iota(iota8i[:], [[1, NE]], channel_multiplier=0))
                    c.inc("tS2", g.memset(valTf[:], float(T)))
                    c.inc("tS2", g.memset(o128c[:], 1.0))
                    c.inc("tS2", g.memset(o32c[:], 1.0))
                    g.wait_ge(c.sem("tX"), pos_valt)
                    rts.issue(g.dma_start(
                        out=io["invh"].rearrange("(p a) o -> p (a o)", p=128),
                        in_=valT[:]), 0)
                    rts.issue(g.dma_start(
                        out=io["wslh"].rearrange("(p a) o -> p (a o)", p=128),
                        in_=valTf[:]), 0)
                    rts.wait_units(g, 0, rts.cnt[0])
                    for i in range(NTT):
                        g.wait_ge(c.sem("tV"), pos_dest(i))
                        rts.issue(g.indirect_dma_start(
                            out=io["invh"],
                            out_offset=bass.IndirectOffsetOnAxis(desti[:, i:i + 1], 0),
                            in_=tokid[:, i:i + 1], in_offset=None), 0)
                        rts.issue(g.indirect_dma_start(
                            out=io["wslh"],
                            out_offset=bass.IndirectOffsetOnAxis(desti[:, i:i + 1], 0),
                            in_=mxall[:, i:i + 1], in_offset=None), 0)


                @blk.sync
                def _(sy):
                    sy.wait_ge(c.sem("gate"), gate_prev)
                    for i in range(NTT):
                        if i >= 2:
                            c.wait(sy, "tP", pos_gate(i - 2))
                        c.ldA.issue(sy.dma_start(
                            out=xg[:].rearrange("p (b a f) -> p b a f", b=2, a=NHT)[:, i % 2],
                            in_=io["xn0fm_h"].rearrange("(a p) t -> p a t", p=128)[:, :, i * 128:(i + 1) * 128]), i)
                    rts.wait_units(sy, 0, rts.cnt[0])
                    c.ldA.issue(sy.dma_start(
                        out=invs[:], in_=io["invh"].rearrange("(p a) o -> p (a o)", p=128)), 0)
                    c.ldA.issue(sy.dma_start(
                        out=wsl[:], in_=io["wslh"].rearrange("(p a) o -> p (a o)", p=128)), 1)

                def routing_mm(t, i):
                    c.wait(t, "tX", pos_oh(i))
                    if i >= 2:
                        c.wait(t, "tE", pos_pfxe(i - 2))
                    c.inc("tP2", t.matmul(psR[i % 2][:, 0:NE], LTs[:],
                                          OHall[:, i * NE:(i + 1) * NE],
                                          start=True, stop=True))
                    c.inc("tP2", t.matmul(psCT[0:NE, i:i + 1],
                                          OHall[:, i * NE:(i + 1) * NE], o128c[:],
                                          start=True, stop=True))

                @blk.tensor
                def _(t):
                    t.wait_ge(c.sem("gate"), gate_prev)
                    c.grpA.wait_all(t)
                    c.inc("tP2", t.matmul(psG[0][:, 0:NE], ones128f[:], gb_row[:],
                                          start=True, stop=True))
                    for i in range(NTT):
                        c.ldA.wait_units(t, i, ldA0[i % 2] + 16 * (i // 2 + 1))
                        if i == 0:
                            c.wait(t, "tX", tX0 + 10)
                        if i >= 2:
                            c.wait(t, "tV", pos_exp(i - 2))
                        for hh in range(NHT):
                            ins = t.matmul(psG[i % 2][:, 0:NE],
                                           xg[:, (i % 2) * NHT * 128 + hh * 128:(i % 2) * NHT * 128 + (hh + 1) * 128],
                                           gw_sb[:, hh * NE:(hh + 1) * NE],
                                           start=(hh == 0), stop=(hh == NHT - 1))
                        c.inc("tP", ins)
                        if i >= 1:
                            routing_mm(t, i - 1)
                    routing_mm(t, NTT - 1)
                    c.wait(t, "tX", pos_combo)
                    c.wait(t, "tE", pos_pfxe(NTT - 1))
                    c.inc("tP2", t.transpose(psR[0][0:NTT + 1, 0:NE],
                                             combo[:], idn8f[:]))
                    c.wait(t, "tX", pos_pbx)
                    c.wait(t, "tE", pos_ebe)
                    c.inc("tP2", t.matmul(psR[0][0:32, 0:NE], ones128f[:, 0:32],
                                          rowp[0:1, 56:64], start=True, stop=True))
                    c.wait(t, "tX", pos_ej)
                    c.inc("tP2", t.matmul(psR[1][:, 0:NJT], ones128f[:], ejrow[:],
                                          start=True, stop=True))
                    for i in range(NTT):
                        if i == 0:
                            c.wait(t, "tE", pos_basee)
                        elif i == 1:
                            c.wait(t, "tE", pos_ejbce)
                        else:
                            c.wait(t, "tE", pos_bbe(i - 2))
                        c.inc("tP2", t.matmul(psR[i % 2][:, 0:NE],
                                              SEL32[:, i * 128:(i + 1) * 128],
                                              BASE[:], start=True, stop=True))

                @blk.vector
                def _(v):
                    v.wait_ge(c.sem("gate"), gate_prev)
                    v.wait_ge(c.sem("tS2"), pos_gpc)
                    c.inc("tX", v.tensor_single_scalar(SEL32[:], sel32it[:], 0, ALU.is_equal))
                    c.inc("tX", v.tensor_copy(iota8bc[:], iota8i[:]))
                    c.inc("tX", v.tensor_copy(iotapm[:], iot[:, 0:1]))
                    c.inc("tX", v.tensor_scalar(m99[:], iota8bc[:], -99.0, None, ALU.add))
                    c.inc("tX", v.tensor_scalar(iotapf[:], iotapm[:], -1.0, None, ALU.mult))
                    c.inc("tX", v.tensor_copy(valT[:], valTf[:]))
                    c.inc("tX", v.tensor_scalar(pfh1[:], iotapf[:], 128.0, None, ALU.add))
                    c.inc("tX", v.tensor_single_scalar(LTs[:], iot[:, 0:128], 0, ALU.is_gt))
                    c.inc("tX", v.tensor_single_scalar(idn8f[:], iot[0:NE, 0:NE], 0, ALU.is_equal))
                    c.wait(v, "tP2", tP20 + 1)
                    c.inc("tX", v.tensor_copy(gbbc[:, 0:NE], psG[0][:, 0:NE]))
                    for i in range(NTT):
                        c.wait(v, "tP", pos_gate(i))
                        c.vchain(v, v.tensor_tensor(gsc[:, 0:NE], psG[i % 2][:, 0:NE],
                                                    gbbc[:, 0:NE], ALU.add))
                        c.inc("tV", v.reduce_max(gsc[:, 8:9], gsc[:, 0:NE],
                                                 axis=AX, negate=True))
                        c.wait(v, "tS", pos_sexp(i))
                        c.vchain(v, v.reciprocal(gsc[:, 11:12], gsc[:, 10:11]))
                        c.vchain(v, v.tensor_scalar(gsc[:, 16:24], gsc[:, 16:24],
                                                    gsc[:, 11:12], None, ALU.mult))
                        c.vchain(v, v.reduce_max(mxall[:, i:i + 1], gsc[:, 16:24], axis=AX))
                        c.vchain(v, v.tensor_single_scalar(gsc[:, 0:NE], gsc[:, 16:24],
                                                           mxall[:, i:i + 1], ALU.is_ge))
                        c.vchain(v, v.tensor_tensor(gsc[:, 24:32], gsc[:, 0:NE],
                                                    m99[:], ALU.mult))
                        c.vchain(v, v.tensor_scalar(gsc[:, 24:32], gsc[:, 24:32],
                                                    99.0, None, ALU.add))
                        c.vchain(v, v.tensor_reduce(gsc[:, 32:33], gsc[:, 24:32],
                                                    AX, ALU.min))
                        c.inc("tX", v.tensor_single_scalar(OHall[:, i * NE:(i + 1) * NE],
                                                           iota8bc[:], gsc[:, 32:33],
                                                           ALU.is_equal))
                        if i >= 1:
                            c.wait(v, "tP2", pos_pfx(i - 1))
                            c.inc("tE", v.tensor_copy(PFXall[:, (i - 1) * NE:i * NE],
                                                      psR[(i - 1) % 2][:, 0:NE]))
                    c.wait(v, "tP2", pos_pfx(NTT - 1))
                    c.inc("tE", v.tensor_copy(PFXall[:, (NTT - 1) * NE:NTT * NE],
                                              psR[(NTT - 1) % 2][:, 0:NE]))
                    c.wait(v, "tP2", pos_cnt(NTT - 1))
                    c.chain(v, "tX", v.tensor_copy(cntT[:], psCT[0:NE, :]))
                    c.vchain(v, v.tensor_copy(cqB[:], cntT[:]))
                    pp = [(cqB, cqA), (cqA, cqB), (cqB, cqA), (cqA, cqB), (cqB, cqA)]
                    for sh, (src, dst) in zip([1, 2, 4, 8, 16], pp):
                        c.vchain(v, v.tensor_tensor(dst[:, sh:NTT], src[:, sh:NTT],
                                                    src[:, 0:NTT - sh], ALU.add))
                        c.vchain(v, v.tensor_copy(dst[:, 0:sh], src[:, 0:sh]))
                    c.vchain(v, v.tensor_tensor(combo[:, 0:NTT], cqA[:], cntT[:],
                                                ALU.subtract))
                    c.inc("tX", v.tensor_copy(combo[:, NTT:NTT + 1],
                                              cqA[:, NTT - 1:NTT]))
                    c.wait(v, "tP2", pos_trans)
                    c.chain(v, "tE", v.tensor_copy(rowp[0:1, 0:8],
                                                   psR[0][NTT:NTT + 1, 0:NE]))
                    c.chain(v, "tE", v.tensor_copy(EBs[:], psR[0][0:32, 0:NE]))
                    c.vchain(v, v.tensor_copy(rowi[0:1, 0:8], rowp[0:1, 0:8]))
                    c.vchain(v, v.tensor_single_scalar(rowi[0:1, 8:16], rowi[0:1, 0:8],
                                                       511, ALU.add))
                    c.vchain(v, v.tensor_single_scalar(rowi[0:1, 8:16], rowi[0:1, 8:16],
                                                       9, ALU.logical_shift_right))
                    c.vchain(v, v.tensor_single_scalar(rowi[0:1, 8:16], rowi[0:1, 8:16],
                                                       9, ALU.logical_shift_left))
                    c.vchain(v, v.tensor_copy(rowp[0:1, 32:40], rowi[0:1, 8:16]))
                    qA, qB, P5 = rowp[0:1, 40:48], rowp[0:1, 48:56], rowp[0:1, 32:40]
                    c.vchain(v, v.tensor_copy(qB[:, 0:8], P5[:, 0:8]))
                    for sh, (src, dst) in zip([1, 2, 4], [(qB, qA), (qA, qB), (qB, qA)]):
                        c.vchain(v, v.tensor_tensor(dst[:, sh:8], src[:, sh:8],
                                                    src[:, 0:8 - sh], ALU.add))
                        c.vchain(v, v.tensor_copy(dst[:, 0:sh], src[:, 0:sh]))
                    c.inc("tX", v.tensor_tensor(rowp[0:1, 56:64], qA[:, 0:8],
                                                P5[:, 0:8], ALU.subtract))
                    for j in range(NJT):
                        c.vchain(v, v.tensor_single_scalar(ejsc[:], qA[:, 0:8],
                                                           512.0 * j, ALU.is_le))
                        c.vchain(v, v.reduce_sum(ejrow[0:1, j:j + 1], ejsc[:], axis=AX))
                    c.vchain(v, v.tensor_single_scalar(ejrow[:], ejrow[:], 7.0, ALU.min))
                    c.inc("tX", v.tensor_copy(ejint[:], ejrow[:]))
                    c.wait(v, "tP2", pos_baseb)
                    c.inc("tE", v.tensor_tensor(BASE[:], psR[0][0:32, 0:NE],
                                                EBs[:], ALU.add))
                    c.wait(v, "tP2", pos_ejbc)
                    c.chain(v, "tE", v.tensor_copy(ejbcf[:], psR[1][:, 0:NJT]))
                    for j in range(NJT):
                        for fh in range(2):
                            c.vchain(v, v.tensor_scalar(
                                iw1f[:, 2 * j + fh:2 * j + fh + 1], ejbcf[:, j:j + 1],
                                256.0, iotapf[:] if fh == 0 else pfh1[:],
                                ALU.mult, ALU.add))
                        c.vchain(v, v.tensor_scalar(iw2f[:, j:j + 1], ejbcf[:, j:j + 1],
                                                    128.0, iotapf[:], ALU.mult, ALU.add))
                    c.vchain(v, v.tensor_copy(iwb[:], ejbcf[:]))
                    c.vchain(v, v.tensor_copy(iw1[:], iw1f[:]))
                    c.inc("tX", v.tensor_copy(iw2[:], iw2f[:]))
                    for i in range(NTT):
                        c.wait(v, "tP2", pos_bb(i))
                        c.chain(v, "tE", v.tensor_tensor(gsc[:, 0:NE], psR[i % 2][:, 0:NE],
                                                         PFXall[:, i * NE:(i + 1) * NE],
                                                         ALU.add))
                        c.vchain(v, v.tensor_tensor(gsc[:, 0:NE], gsc[:, 0:NE],
                                                    OHall[:, i * NE:(i + 1) * NE], ALU.mult))
                        c.vchain(v, v.reduce_sum(destf[:, 0:1], gsc[:, 0:NE], axis=AX))
                        c.vchain(v, v.tensor_copy(dsti[:, 0:1], destf[:, 0:1]))
                        c.vchain(v, v.tensor_single_scalar(dsti[:, 1:2], dsti[:, 0:1],
                                                           127, ALU.bitwise_and))
                        c.vchain(v, v.tensor_single_scalar(dsti[:, 1:2], dsti[:, 1:2],
                                                           6, ALU.logical_shift_left))
                        c.vchain(v, v.tensor_single_scalar(dsti[:, 2:3], dsti[:, 0:1],
                                                           7, ALU.logical_shift_right))
                        c.inc("tV", v.tensor_tensor(desti[:, i:i + 1], dsti[:, 1:2],
                                                    dsti[:, 2:3], ALU.add))

                @blk.scalar
                def _(s):
                    s.wait_ge(c.sem("gate"), gate_prev)
                    for i in range(NTT):
                        s.wait_ge(c.sem("tV"), pos_exp(i))
                        c.inc("tS", s.activation(gsc[:, 16:24], gsc[:, 0:NE], AF.Exp,
                                                 bias=gsc[:, 8:9], accum_out=gsc[:, 10:11]))

            inv_ld = c.ldA.cnt[0]
            wsl_ld = c.ldA.cnt[1]
            pos_iw_keep = pos_iw

        qsl = []
        if "wgate" in io:
            def _add_q(mat, dst, it, ncol):
                c0 = 0
                while c0 < ncol:
                    w_ = min(512, ncol - c0)
                    qsl.append((io[mat][it * 128:(it + 1) * 128, c0:c0 + w_],
                                io[dst][it * 128:(it + 1) * 128, c0:c0 + w_],
                                w_))
                    c0 += w_
            for it in range(NIT):
                _add_q("wgate", "wgq_h", it, D)
            for it in range(NIT):
                _add_q("wup", "wuq_h", it, D)
            for dt_ in range(NHT):
                _add_q("wdown", "wdq_h", dt_, I1)

        with contextlib.ExitStack() as pe_:
            w1h = sbuf(c, pe_, "w1h", [128, 2 * NHT * FH], BF16)
            w2h = sbuf(c, pe_, "w2h", [128, 2 * NFH * D], BF16)
            xgt = sbuf(c, pe_, "sm_xgt", [128, 4 * D], BF16)
            xcs = sbuf(c, pe_, "sm_xcs", [128, NHT * 512], BF16)
            h1h = sbuf(c, pe_, "h1h", [128, NFH * 512], BF16)
            ysc = sbuf(c, pe_, "ysc", [128, 4 * D], F32)
            b1je = sbuf(c, pe_, "sm_b1j", [128, 2 * 32], F32)
            b2bcj = sbuf(c, pe_, "sm_b2j", [128, 2 * D], BF16)
            if qsl:
                qin = sbuf(c, pe_, "q6_in", [128, 4 * 512], F32)
                qqb = sbuf(c, pe_, "q6_qb", [128, 8 * 512], BF16)
                qsA = sbuf(c, pe_, "q6_sA", [128, 512], F32)
                qsX = sbuf(c, pe_, "q6_sX", [128, 512], F32)
                qsC = sbuf(c, pe_, "q6_sC", [128, 512], F32)
                qsB = sbuf(c, pe_, "q6_sB", [128, 2 * 32], F32)

            psH = [psum(c, pe_, f"psH{i}", [128, 512]) for i in range(2)]
            psY2 = [psum(c, pe_, f"psY2{i}", [128, 512]) for i in range(3)]
            pst = [psum(c, pe_, f"sm_pst{i}", [128, 512], BF16) for i in range(2)]

            xgs = DSlots(c, "xgs", 4)
            bs1 = DSlots(c, "bs1", 2)
            bs2 = DSlots(c, "bs2", 2)
            ldQ = DSlots(c, "ldQ", 4)
            stQ = DSlots(c, "stQ", 8)

            with nc.Block("p3b") as blk:
                tP0b, tP20b, tE0b, tE20b, tS0b = c.snap("tP", "tP2", "tE", "tE2", "tS")
                c._dbg_p3b = (tP0b, tP20b, tE0b, tE20b, tS0b)
                wLa0 = c.wLa.snap()
                w2s0 = c.w2s.snap()
                qAs0 = c.qAs.snap()
                qW0, qL0, qS0 = c.snap("qW", "qL", "qS")
                qst = {"ld": 0, "q": 0, "qp": 0, "tr": 0, "ev": 0}

                def q_loads(g, k):
                    while k > 0 and qst["ld"] < len(qsl):
                        i = qst["ld"]
                        srcap, dstap, w_ = qsl[i]
                        if i >= 4:
                            g.wait_ge(c.sem("qW"), qW0 + i - 3)
                        ldQ.issue(g.dma_start(out=qin[:, (i % 4) * 512:(i % 4) * 512 + w_],
                                              in_=srcap), i)
                        qst["ld"] += 1
                        k -= 1

                def q_quant(v, nparts):
                    while nparts > 0:
                        i, p_ = qst["q"], qst["qp"]
                        if i >= len(qsl):
                            return
                        _, _, w_ = qsl[i]
                        if p_ == 0:
                            ldQ.wait_units(v, i, 16 * (i // 4 + 1))
                            if i >= 8:
                                stQ.wait_units(v, i - 8, 16 * ((i - 8) // 8 + 1))
                        ins = emit_quant_part(
                            c, v, qin[:, (i % 4) * 512:(i % 4) * 512 + w_],
                            qqb[:, (i % 8) * 512:(i % 8) * 512 + w_], w_,
                            (qsA[:, 0:w_], qsX[:, 0:w_], qsC[:, 0:w_],
                             qsB[:, 0:w_ // 16], qsB[:, 32:32 + w_ // 16]), p_)
                        if p_ == 3:
                            c.inc("qW", ins)
                            qst["q"] += 1
                            qst["qp"] = 0
                        else:
                            qst["qp"] = p_ + 1
                        nparts -= 1

                def q_store(s, upto):
                    while qst["ev"] < min(upto, len(qsl)):
                        i = qst["ev"]
                        _, dstap, w_ = qsl[i]
                        c.wait(s, "qW", qW0 + i + 1)
                        stQ.issue(s.dma_start(
                            out=dstap, in_=qqb[:, (i % 8) * 512:(i % 8) * 512 + w_]), i)
                        qst["ev"] += 1

                def tp_h1(kk, f):
                    return tP0b + kk * 16 + f + 1

                def tp2_trans(j, hh):
                    return tP20b + j * 24 + hh + 1

                def tp2_y(kk, g):
                    return tP20b + (kk // 2) * 24 + 8 + (kk % 2) * 8 + g + 1

                def te_evac(kk, g):
                    j, fh = kk // 2, kk % 2
                    if fh == 0:
                        return tE0b + j * 20 + g + 1
                    return tE0b + j * 20 + 8 + (g // 2) * 3 + (g % 2) + 1

                def te_scale(j, t128):
                    return tE0b + j * 20 + 8 + t128 * 3 + 3

                def ts_gelu(kk, f):
                    return tS0b + kk * 16 + f + 1

                def te2_xc(j, hh):
                    return tE20b + j * 8 + hh + 1

                @blk.sync
                def _(sy):
                    if qsl:
                        sy.wait_ge(c.sem("gate"), gate_prev)
                        q_loads(sy, len(qsl))

                @blk.gpsimd
                def _(g):
                    g.wait_ge(c.sem("tX"), pos_iw_keep)
                    for kk in range(2 * NJT):
                        j, fh = kk // 2, kk % 2
                        if kk >= 2:
                            g.wait_ge(c.sem("tP"), tp_h1(kk - 2, 15))
                        c.wLa.issue(g.indirect_dma_start(
                            out=w1h[:, (kk % 2) * NHT * FH:(kk % 2 + 1) * NHT * FH],
                            out_offset=None, in_=io["w1"],
                            in_offset=bass.IndirectOffsetOnAxis(iw1[:, kk:kk + 1], 0)), kk)
                        if fh == 0:
                            if j >= 2:
                                g.wait_ge(c.sem("tS"), ts_gelu(2 * (j - 2) + 1, 15))
                            bs1.issue(g.indirect_dma_start(
                                out=b1je[:, (j % 2) * 32:(j % 2 + 1) * 32],
                                out_offset=None, in_=io["b1r"],
                                in_offset=bass.IndirectOffsetOnAxis(iw2[:, j:j + 1], 0)), j)
                            if j >= 2:
                                g.wait_ge(c.sem("tE"), te_evac(2 * (j - 2), 7))
                            bs2.issue(g.indirect_dma_start(
                                out=b2bcj[:, (j % 2) * D:(j % 2 + 1) * D],
                                out_offset=None, in_=io["b2"],
                                in_offset=bass.IndirectOffsetOnAxis(iwb[:, j:j + 1], 0)), j)
                            if j >= 1:
                                g.wait_ge(c.sem("tP2"), tp2_trans(j - 1, 7))
                            else:
                                c.ldA.wait_units(g, 0, inv_ld)
                            for q in range(4):
                                xgs.issue(g.indirect_dma_start(
                                    out=xgt[:, q * D:(q + 1) * D],
                                    out_offset=None, in_=io["xn0tm_h"],
                                    in_offset=bass.IndirectOffsetOnAxis(
                                        invs[:, j * 4 + q:j * 4 + q + 1], 0)), q)
                        if kk >= 2:
                            g.wait_ge(c.sem("tP2"), tp2_y(kk - 2, 7))
                        c.w2s.issue(g.indirect_dma_start(
                            out=w2h[:, (kk % 2) * NFH * D:(kk % 2 + 1) * NFH * D],
                            out_offset=None, in_=io["w2"],
                            in_offset=bass.IndirectOffsetOnAxis(iw2[:, j:j + 1], 0),
                            element_offset=fh * NFH * D), kk)
                        if fh == 1:
                            for t128 in range(4):
                                g.wait_ge(c.sem("tE"), te_scale(j, t128))
                                c.qAs.issue(g.indirect_dma_start(
                                    out=io["moe2_h"],
                                    out_offset=bass.IndirectOffsetOnAxis(
                                        invs[:, j * 4 + t128:j * 4 + t128 + 1], 0),
                                    in_=ysc[:, t128 * D:(t128 + 1) * D],
                                    in_offset=None), t128)
                    for t128 in range(4):
                        c.qAs.wait_units(g, t128, qAs0[t128] + 16 * NJT)
                    if qsl:
                        nq_ = len(qsl)
                        for s_ in range(4):
                            g.wait_ge(ldQ.sems[s_], 16 * ((nq_ - s_ + 3) // 4))
                        for s_ in range(8):
                            g.wait_ge(stQ.sems[s_], 16 * ((nq_ - s_ + 7) // 8))
                    g.sem_inc(c.sem("gate"), 1)
                    c.bump("gate")

                @blk.tensor
                def _(t):
                    for kk in range(2 * NJT):
                        j, fh = kk // 2, kk % 2
                        if fh == 0:
                            for hh in range(NHT):
                                gidx = j * 8 + hh
                                if gidx >= 2:
                                    jp, hp = divmod(gidx - 2, 8)
                                    c.wait(t, "tE2", te2_xc(jp, hp))
                                if hh == 0:
                                    for q in range(4):
                                        xgs.wait_units(t, q, 16 * (j + 1))
                                for q in range(4):
                                    ins = t.transpose(pst[hh % 2][:, q * 128:(q + 1) * 128],
                                                      xgt[:, q * D + hh * 128:q * D + (hh + 1) * 128],
                                                      idn[:])
                                c.inc("tP2", ins)
                        c.wLa.wait_units(t, kk, wLa0[kk % 2] + 16 * (kk // 2 + 1))
                        c.wait(t, "tE2", te2_xc(j, 7))
                        for f in range(NFH):
                            kkf = kk * 16 + f
                            if kkf >= 2:
                                kp, fp = divmod(kkf - 2, 16)
                                c.wait(t, "tS", ts_gelu(kp, fp))
                            for hh in range(NHT):
                                ins = t.matmul(
                                    psH[kkf % 2][:],
                                    w1h[:, (kk % 2) * NHT * FH + hh * FH + f * 128:
                                           (kk % 2) * NHT * FH + hh * FH + (f + 1) * 128],
                                    xcs[:, hh * 512:(hh + 1) * 512],
                                    start=(hh == 0), stop=(hh == NHT - 1))
                            c.inc("tP", ins)
                        c.w2s.wait_units(t, kk, w2s0[kk % 2] + 16 * (kk // 2 + 1))
                        c.wait(t, "tS", ts_gelu(kk, 15))
                        for t128 in range(4):
                            for dch in range(2):
                                g_ = t128 * 2 + dch
                                gg = kk * 8 + g_
                                if gg >= 3:
                                    kp, gp = divmod(gg - 3, 8)
                                    c.wait(t, "tE", te_evac(kp, gp))
                                for f in range(NFH):
                                    ins = t.matmul(
                                        psY2[gg % 3][:],
                                        h1h[:, f * 512 + t128 * 128:f * 512 + (t128 + 1) * 128],
                                        w2h[:, (kk % 2) * NFH * D + f * D + dch * 512:
                                               (kk % 2) * NFH * D + f * D + dch * 512 + 512],
                                        start=(f == 0), stop=(f == NFH - 1))
                                c.inc("tP2", ins)


                @blk.scalar
                def _(s):
                    for kk in range(2 * NJT):
                        j, fh = kk // 2, kk % 2
                        q_store(s, 4 * (kk - 1))
                        if fh == 0:
                            if j >= 1:
                                s.wait_ge(c.sem("tP"), tp_h1(2 * (j - 1) + 1, 15))
                            for hh in range(NHT):
                                s.wait_ge(c.sem("tP2"), tp2_trans(j, hh))
                                c.inc("tE2", s.copy(
                                    xcs[:, hh * 512:(hh + 1) * 512],
                                    pst[hh % 2][:, 0:512]))
                        if kk >= 1:
                            s.wait_ge(c.sem("tP2"), tp2_y(kk - 1, 7))
                        if fh == 0:
                            bs1.wait_units(s, j, 16 * (j // 2 + 1))
                        for f in range(NFH):
                            kkf = kk * 16 + f
                            s.wait_ge(c.sem("tP"), tp_h1(kk, f))
                            c.inc("tS", s.activation(
                                h1h[:, f * 512:(f + 1) * 512], psH[kkf % 2][:], AF.Gelu,
                                bias=b1je[:, (j % 2) * 32 + fh * 16 + f:
                                          (j % 2) * 32 + fh * 16 + f + 1]))
                    q_store(s, len(qsl))

                @blk.vector
                def _(v):
                    c.ldA.wait_units(v, 1, wsl_ld)
                    for kk in range(2 * NJT):
                        j, fh = kk // 2, kk % 2
                        for t128 in range(4):
                            for dch in range(2):
                                g_ = t128 * 2 + dch
                                if fh == 0 and dch == 0 and j >= 1:
                                    c.qAs.wait_units(v, t128, qAs0[t128] + 16 * j)
                                if fh == 0 and g_ == 0:
                                    bs2.wait_units(v, j, 16 * (j // 2 + 1))
                                v.wait_ge(c.sem("tP2"), tp2_y(kk, g_))
                                dst = ysc[:, t128 * D + dch * 512:t128 * D + dch * 512 + 512]
                                if fh == 0:
                                    c.inc("tE", v.tensor_tensor(
                                        dst, psY2[(kk * 8 + g_) % 3][:],
                                        b2bcj[:, (j % 2) * D + dch * 512:
                                              (j % 2) * D + dch * 512 + 512], ALU.add))
                                elif dch == 0:
                                    c.inc("tE", v.tensor_tensor(dst, psY2[(kk * 8 + g_) % 3][:],
                                                                dst, ALU.add))
                                else:
                                    c.chain(v, "tE", v.tensor_tensor(
                                        dst, psY2[(kk * 8 + g_) % 3][:], dst, ALU.add))
                                q_quant(v, 1)
                            if fh == 1:
                                c.inc("tE", v.tensor_scalar(
                                    ysc[:, t128 * D:(t128 + 1) * D],
                                    ysc[:, t128 * D:(t128 + 1) * D],
                                    wsl[:, j * 4 + t128:j * 4 + t128 + 1], None, ALU.mult))
                            q_quant(v, 2)
                    q_quant(v, 4 * len(qsl))


def emit_quant(c, v, x, q, W, scrs):
    a, xn, c2, bm, sc2 = scrs
    NB = W // 16
    ch = lambda ins: c.vchain(v, ins)
    a3 = a.rearrange("p (b k) -> p b k", k=16)
    ch(v.tensor_single_scalar(a.bitcast(I32), x.bitcast(I32), 0x7FFFFFFF, ALU.bitwise_and))
    ch(v.tensor_tensor(a3[:, :, 0:8], a3[:, :, 0:8], a3[:, :, 8:16], ALU.max))
    ch(v.tensor_tensor(a3[:, :, 0:4], a3[:, :, 0:4], a3[:, :, 4:8], ALU.max))
    ch(v.tensor_tensor(a3[:, :, 0:2], a3[:, :, 0:2], a3[:, :, 2:4], ALU.max))
    ch(v.tensor_tensor(bm.rearrange("p (b o) -> p b o", o=1), a3[:, :, 0:1],
                       a3[:, :, 1:2], ALU.max))
    ch(v.tensor_scalar(sc2, bm, 1.0 / 6.0, 1e-12, ALU.mult, ALU.max))
    ch(v.reciprocal(sc2, sc2))
    ch(v.tensor_single_scalar(a.bitcast(I32), x.bitcast(I32), 0x7FFFFFFF, ALU.bitwise_and))
    ch(v.tensor_tensor(xn.rearrange("p (b k) -> p b k", k=16), a3,
                       sc2.rearrange("p (b o) -> p b o", o=1).broadcast_to([128, NB, 16]),
                       ALU.mult))
    ch(v.tensor_single_scalar(a, xn, 0.25, ALU.is_gt))
    for m in [0.75, 1.25, 1.75]:
        ch(v.scalar_tensor_tensor(a, xn, m, a, ALU.is_gt, ALU.add))
    ch(v.tensor_single_scalar(c2, xn, 2.5, ALU.is_gt))
    ch(v.scalar_tensor_tensor(c2, xn, 3.5, c2, ALU.is_gt, ALU.add))
    ch(v.tensor_single_scalar(xn, xn, 5.0, ALU.is_gt))
    ch(v.scalar_tensor_tensor(c2, xn, 2.0, c2, ALU.mult, ALU.add))
    ch(v.scalar_tensor_tensor(a, a, 0.5, c2, ALU.mult, ALU.add))
    ch(v.tensor_scalar(xn, x, 0.0, 2.0, ALU.is_ge, ALU.mult))
    ch(v.reciprocal(sc2, sc2))
    ch(v.tensor_tensor(a.rearrange("p (b k) -> p b k", k=16), a3,
                       sc2.rearrange("p (b o) -> p b o", o=1).broadcast_to([128, NB, 16]),
                       ALU.mult))
    ch(v.tensor_scalar(xn, xn, -1.0, None, ALU.add))
    return v.tensor_tensor(q, a, xn, ALU.mult)


def emit_quant_part(c, v, x, q, W, scrs, part):
    a, xn, c2, bm, sc2 = scrs
    NB = W // 16
    ch = lambda ins: c.vchain(v, ins)
    a3 = a.rearrange("p (b k) -> p b k", k=16)
    if part == 0:
        ch(v.tensor_single_scalar(a.bitcast(I32), x.bitcast(I32), 0x7FFFFFFF, ALU.bitwise_and))
        ch(v.tensor_tensor(a3[:, :, 0:8], a3[:, :, 0:8], a3[:, :, 8:16], ALU.max))
        ch(v.tensor_tensor(a3[:, :, 0:4], a3[:, :, 0:4], a3[:, :, 4:8], ALU.max))
        ch(v.tensor_tensor(a3[:, :, 0:2], a3[:, :, 0:2], a3[:, :, 2:4], ALU.max))
        ch(v.tensor_tensor(bm.rearrange("p (b o) -> p b o", o=1), a3[:, :, 0:1],
                           a3[:, :, 1:2], ALU.max))
        ch(v.tensor_scalar(sc2, bm, 1.0 / 6.0, 1e-12, ALU.mult, ALU.max))
        ch(v.reciprocal(sc2, sc2))
        return None
    if part == 1:
        ch(v.tensor_single_scalar(a.bitcast(I32), x.bitcast(I32), 0x7FFFFFFF, ALU.bitwise_and))
        ch(v.tensor_tensor(xn.rearrange("p (b k) -> p b k", k=16), a3,
                           sc2.rearrange("p (b o) -> p b o", o=1).broadcast_to([128, NB, 16]),
                           ALU.mult))
        ch(v.tensor_single_scalar(a, xn, 0.25, ALU.is_gt))
        for m in [0.75, 1.25, 1.75]:
            ch(v.scalar_tensor_tensor(a, xn, m, a, ALU.is_gt, ALU.add))
        return None
    if part == 2:
        ch(v.tensor_single_scalar(c2, xn, 2.5, ALU.is_gt))
        ch(v.scalar_tensor_tensor(c2, xn, 3.5, c2, ALU.is_gt, ALU.add))
        ch(v.tensor_single_scalar(xn, xn, 5.0, ALU.is_gt))
        ch(v.scalar_tensor_tensor(c2, xn, 2.0, c2, ALU.mult, ALU.add))
        ch(v.scalar_tensor_tensor(a, a, 0.5, c2, ALU.mult, ALU.add))
        return None
    ch(v.tensor_scalar(xn, x, 0.0, 2.0, ALU.is_ge, ALU.mult))
    ch(v.reciprocal(sc2, sc2))
    ch(v.tensor_tensor(a.rearrange("p (b k) -> p b k", k=16), a3,
                       sc2.rearrange("p (b o) -> p b o", o=1).broadcast_to([128, NB, 16]),
                       ALU.mult))
    ch(v.tensor_scalar(xn, xn, -1.0, None, ALU.add))
    return v.tensor_tensor(q, a, xn, ALU.mult)


def quant_transpose_phase(c, idn, name, items, W_of, nn_of, load_src, store_dst):
    nc = c.nc
    sb = sbuf
    gate_e = c.cnt["gate"]
    Wmax = max(W_of(it) for it in items)
    NNmax = max(nn_of(it) for it in items)

    def tp_thr(k):
        if k < 0:
            return 0
        return sum((nn_of(items[k2]) + 3) // 4 for k2 in range(k + 1))

    with contextlib.ExitStack() as ph:
        wld = sb(c, ph, f"{name}_wld", [128, 2 * Wmax], F32)
        qb = sb(c, ph, f"{name}_qb", [128, 2 * Wmax], BF16)
        sA = sb(c, ph, f"{name}_sA", [128, Wmax], F32)
        sX = sb(c, ph, f"{name}_sX", [128, Wmax], F32)
        sC = sb(c, ph, f"{name}_sC", [128, Wmax], F32)
        sB = sb(c, ph, f"{name}_sB", [128, 2 * (Wmax // 16)], F32)
        tst = sb(c, ph, f"{name}_tst", [128, 2 * NNmax * 128], BF16)
        pst = [psum(c, ph, f"{name}_ps{i}", [128, 512], BF16) for i in range(2)]

        with nc.Block(name) as blk:
            tV0, tP0, tE0 = c.snap("tV", "tP", "tE")
            ldA0 = c.ldA.snap()
            stA0 = c.stA.snap()

            @blk.sync
            def _(sy):
                sy.wait_ge(c.sem("gate"), gate_e)

                def store(k):
                    it = items[k]
                    nn = nn_of(it)
                    sy.wait_ge(c.sem("tE"), tE0 + tp_thr(k))
                    c.stA.issue(sy.dma_start(
                        out=store_dst(it),
                        in_=tst[:, (k % 2) * NNmax * 128:(k % 2) * NNmax * 128 + nn * 128]
                        .rearrange("p (a f) -> p a f", f=128)), k % 2)
                for k, it in enumerate(items):
                    if k >= 2:
                        c.wait(sy, "tV", tV0 + (k - 2) + 1)
                    c.ldA.issue(sy.dma_start(out=wld[:, (k % 2) * Wmax:(k % 2) * Wmax + W_of(it)],
                                             in_=load_src(it)), k)
                    if k >= 2:
                        store(k - 2)
                for k in range(max(0, len(items) - 2), len(items)):
                    store(k)
                c.stA.wait_all(sy)
                sy.sem_inc(c.sem("gate"), 1)
                c.bump("gate")

            @blk.vector
            def _(v):
                for k, it in enumerate(items):
                    W = W_of(it)
                    c.ldA.wait_units(v, k, ldA0[k % 2] + 16 * (k // 2 + 1))
                    if k >= 2:
                        c.wait(v, "tP", tP0 + tp_thr(k - 2))
                    ins = emit_quant(c, v, wld[:, (k % 2) * Wmax:(k % 2) * Wmax + W],
                                     qb[:, (k % 2) * Wmax:(k % 2) * Wmax + W], W,
                                     (sA[:, 0:W], sX[:, 0:W], sC[:, 0:W],
                                      sB[:, 0:W // 16], sB[:, Wmax // 16:Wmax // 16 + W // 16]))
                    c.inc("tV", ins)

            @blk.tensor
            def _(t):
                t.wait_ge(c.sem("tX"), 2)
                for k, it in enumerate(items):
                    nn = nn_of(it)
                    c.wait(t, "tV", tV0 + k + 1)
                    for q in range(nn):
                        grp = q // 4
                        if tp_thr(k - 1) + grp >= 2:
                            c.wait(t, "tE", tE0 + tp_thr(k - 1) + grp - 1)
                        ins = t.transpose(pst[grp % 2][:, (q % 4) * 128:(q % 4 + 1) * 128],
                                          qb[:, (k % 2) * Wmax + q * 128:(k % 2) * Wmax + (q + 1) * 128],
                                          idn[:])
                        if q % 4 == 3 or q == nn - 1:
                            c.inc("tP", ins)

            @blk.scalar
            def _(s):
                for k, it in enumerate(items):
                    nn = nn_of(it)
                    ngrp = (nn + 3) // 4
                    if k >= 2:
                        c.stA.wait_units(s, k % 2, stA0[k % 2] + 16 * (k // 2))
                    for grp in range(ngrp):
                        s.wait_ge(c.sem("tP"), tP0 + tp_thr(k - 1) + grp + 1)
                        cols = min(4, nn - grp * 4) * 128
                        c.inc("tE", s.copy(tst[:, (k % 2) * NNmax * 128 + grp * 512:
                                               (k % 2) * NNmax * 128 + grp * 512 + cols],
                                           pst[grp % 2][:, 0:cols]))


def scan_p5d(c, ones128f, Ssel, A_sb, dtb_sb, dvec_sb):
    nc = c.nc
    io = c.io
    sb = sbuf
    gate_c = c.cnt["gate"]

    with contextlib.ExitStack() as ph:
        hs2c = sb(c, ph, "sd_hs2", [128, NET * TC], BF16)
        gatec = sb(c, ph, "sd_gate", [128, NET * TC], BF16)
        sgc = gatec
        xpw = sb(c, ph, "sd_xpw", [128, NET * 128], BF16)
        dtw = sb(c, ph, "sd_dtw", [64, ED], BF16 if L1_SCANMODE != "f32d" else F32)
        bsb = sb(c, ph, "sd_bsb", [16, 512], BF16)
        csb = sb(c, ph, "sd_csb", [16, 512], BF16)
        dsr = sb(c, ph, "sd_dsr", [64, 512], BF16 if L1_SCANMODE != "f32d" else F32)
        Bbc = sb(c, ph, "sd_Bbc", [128, NST * TC], BF16)
        Cbc = sb(c, ph, "sd_Cbc", [128, NST * TC], BF16)
        dsb = sb(c, ph, "sd_delta", [128, 2 * TC], F32)
        spZ = sb(c, ph, "sd_spZ", [128, 2 * TC], F32)
        spA = sb(c, ph, "sd_spA", [128, TC], F32)
        spL = sb(c, ph, "sd_spL", [128, 2 * TC], F32)
        spR = sb(c, ph, "sd_spR", [128, 2 * TC], F32)
        du = sb(c, ph, "sd_du", [128, TC], BF16)
        dA = sb(c, ph, "sd_dA", [128, 2 * NST * TC], F32)
        bts = sb(c, ph, "sd_bts", [128, NST * TC], BF16)
        hts = sb(c, ph, "sd_hts", [128, NST * TC], BF16)
        y2sb = hs2c
        state = sb(c, ph, "sd_state", [128, NET * NST], F32)
        psm = psum(c, ph, "sd_psm", [128, 512])
        psd = [psum(c, ph, f"sd_psd{i}", [128, 512]) for i in range(2)]
        psb = [psum(c, ph, f"sd_psb{i}", [128, 512]) for i in range(2)]

        PE_N = 1 + 2 * NST + NET
        TE_N = 3 + 2 * NST
        TV_N = 3 * NET
        TS_N = 1 + 2 * NET

        with nc.Block("p5d") as blk:
            tP0, tE0, tV0, tS0, tX0 = c.snap("tP", "tE", "tV", "tS", "tX")
            ldB0 = c.ldB.snap()
            stA0 = c.stA.snap()

            def pe_ssm(tc):
                return tP0 + PE_N * tc + 1

            def pe_bc(tc, j):
                return tP0 + PE_N * tc + 1 + j + 1

            def pe_delta(tc, et):
                return tP0 + PE_N * tc + 1 + 2 * NST + et + 1

            def te_ssm(tc):
                return tE0 + TE_N * tc + 3

            def te_stage(tc):
                return tE0 + TE_N * tc + 2

            def te_dsr(tc):
                return tE0 + TE_N * tc + 3

            def te_bc(tc, j):
                return tE0 + TE_N * tc + 3 + j + 1

            _pz, _pd, _pe2 = {}, {}, {}
            _cnt = 0
            for _tc in range(NTC):
                for _k in range(NET + 2):
                    if _k < NET:
                        _cnt += 1
                        _pz[_tc * NET + _k] = tV0 + _cnt
                    if 1 <= _k <= NET:
                        _cnt += 1
                        _pd[_tc * NET + _k - 1] = tV0 + _cnt
                    if 2 <= _k <= NET + 1:
                        _cnt += 1
                        _pe2[_tc * NET + _k - 2] = tV0 + _cnt

            def tv_z(tc, et):
                return _pz[tc * NET + et]

            def tv_d(tc, et):
                return _pd[tc * NET + et]

            def tv_e(tc, et):
                return _pe2[tc * NET + et]

            def ts_silu(tc):
                return tS0 + TS_N * tc + 1

            def ts_l(tc, et):
                return tS0 + TS_N * tc + 1 + 2 * et + 1

            def ts_da(tc, et):
                return tS0 + TS_N * tc + 1 + 2 * et + 2

            @blk.sync
            def _(sy):
                sy.wait_ge(c.sem("gate"), gate_c)
                def store_y2(tc2):
                    for hh in range(2):
                        c.stA.issue(sy.dma_start(
                            out=io["y2_h"].rearrange("(a p) t -> p a t", p=128)[:, hh * 8:(hh + 1) * 8,
                                                                               tc2 * TC:(tc2 + 1) * TC],
                            in_=y2sb[:].rearrange("p (a f) -> p a f", a=NET)[:, hh * 8:(hh + 1) * 8]),
                            tc2 % 2)
                for tc in range(NTC):
                    if tc >= 1:
                        c.wait(sy, "tV", tv_e(tc - 1, NET - 1))
                        store_y2(tc - 1)
                        c.stA.wait_all(sy)
                    for hh in range(2):
                        c.ldB.issue(sy.dma_start(
                            out=hs2c[:].rearrange("p (a f) -> p a f", a=NET)[:, hh * 8:(hh + 1) * 8],
                            in_=io["hs2_h"].rearrange("(a p) t -> p a t", p=128)[:, hh * 8:(hh + 1) * 8,
                                                                                tc * TC:(tc + 1) * TC]), 0)
                        c.ldB.issue(sy.dma_start(
                            out=gatec[:].rearrange("p (a f) -> p a f", a=NET)[:, hh * 8:(hh + 1) * 8],
                            in_=io["gate_h"].rearrange("(a p) t -> p a t", p=128)[:, hh * 8:(hh + 1) * 8,
                                                                                 tc * TC:(tc + 1) * TC]), 1)
                c.wait(sy, "tV", tv_e(NTC - 1, NET - 1))
                store_y2(NTC - 1)
                c.stA.wait_all(sy)
                sy.sem_inc(c.sem("gate"), 1)
                c.bump("gate")

            @blk.gpsimd
            def _(g):
                g.wait_ge(c.sem("gate"), gate_c)
                c.grpA.issue(g.dma_start(out=xpw[:].rearrange("p (a f) -> p a f", a=NET),
                                         in_=io["xproj_pad"].rearrange("(a p) f -> p a f", p=128)), 0)
                c.pwld.issue(g.dma_start(out=dtw[:], in_=io["dtw_t"]), 0)
                c.inc("tX", g.memset(state[:], 0.0))

            @blk.scalar
            def _(s):
                s.wait_ge(c.sem("tX"), c._l1_ax)
                if L1_SCANMODE in ("front1", "front2", "front3"):
                    return
                if L1_SCANMODE == "front4":
                    for tc in range(NTC):
                        s.sem_inc(c.sem("tS"), 1)
                        c.bump("tS")
                        for et in range(NET):
                            s.sem_inc(c.sem("tS"), 1)
                            c.bump("tS")
                            s.sem_inc(c.sem("tS"), 1)
                            c.bump("tS")
                    return
                for tc in range(NTC):
                    c.ldB.wait_units(s, 1, ldB0[1] + 32 * (tc + 1))
                    if tc >= 1:
                        s.wait_ge(c.sem("tV"), tv_e(tc - 1, NET - 1))
                    if L1_SCANMODE in ("min2", "min3", "front5"):
                        c.inc("tS", s.copy(sgc[:], gatec[:]))
                    else:
                        c.inc("tS", s.activation(sgc[:], gatec[:], AF.Silu))
                    for et in range(NET):
                        s.wait_ge(c.sem("tV"), tv_z(tc, et))
                        zslice = spZ[:, (et % 2) * TC:(et % 2 + 1) * TC]
                        if L1_SCANMODE in ("min2", "min3", "front5"):
                            c.schain(s, s.copy(spA[:], zslice))
                            c.inc("tS", s.copy(spL[:, (et % 2) * TC:(et % 2 + 1) * TC], spA[:]))
                        else:
                            c.schain(s, s.activation(spA[:], zslice, AF.Abs))
                            c.schain(s, s.activation(spA[:], spA[:], AF.Exp, scale=-1.0))
                            c.inc("tS", s.activation(spL[:, (et % 2) * TC:(et % 2 + 1) * TC],
                                                     spA[:], AF.Ln, bias=1.0))
                        s.wait_ge(c.sem("tV"), tv_d(tc, et))
                        g_ = tc * NET + et
                        if g_ >= 2:
                            s.wait_ge(c.sem("tV"), tv_e((g_ - 2) // NET, (g_ - 2) % NET))
                        dab = (et % 2) * NST * TC
                        for n in range(NST):
                            ins = s.activation(dA[:, dab + n * TC:dab + (n + 1) * TC],
                                               dsb[:, (et % 2) * TC:(et % 2 + 1) * TC], AF.Exp,
                                               scale=A_sb[:, et * NST + n:et * NST + n + 1])
                        c.inc("tS", ins)

            @blk.tensor
            def _(t):
                t.wait_ge(c.sem("tX"), tX0 + 1)
                c.grpA.wait_all(t)
                c.pwld.wait_units(t, 0, c.pwld.cnt[0])
                for tc in range(NTC):
                    c.ldB.wait_units(t, 0, ldB0[0] + 32 * (tc + 1))
                    if tc >= 1:
                        t.wait_ge(c.sem("tE"), te_ssm(tc - 1))
                    for et in range(NET):
                        ins = t.matmul(psm[:], xpw[:, et * 128:(et + 1) * 128],
                                       hs2c[:, et * TC:(et + 1) * TC],
                                       start=(et == 0), stop=(et == NET - 1))
                    c.inc("tP", ins)
                    if L1_SCANMODE == "front1":
                        continue
                    t.wait_ge(c.sem("tE"), te_stage(tc))
                    for n in range(NST):
                        for j, src_ in [(0, bsb), (1, csb)]:
                            jj = 2 * n + j
                            if jj >= 2:
                                t.wait_ge(c.sem("tE"), te_bc(tc, jj - 2))
                            c.inc("tP", t.matmul(psb[j][:], Ssel[:, n * 128:(n + 1) * 128],
                                                 src_[:], start=True, stop=True))
                    if L1_SCANMODE == "front2":
                        continue
                    for et in range(NET):
                        if L1_SCANMODE != "front3" and tc + et > 0:
                            idx = NET * tc + et - 2
                            if idx >= 0:
                                t.wait_ge(c.sem("tV"), tv_z(idx // NET, idx % NET))
                        if et == 0:
                            t.wait_ge(c.sem("tE"), te_dsr(tc))
                        c.inc("tP", t.matmul(psd[et % 2][:], dtw[:, et * 128:(et + 1) * 128],
                                             dsr[:], start=True, stop=True))
                    if L1_SCANMODE == "front3":
                        continue

            @blk.vector
            def _(v):
                v.wait_ge(c.sem("tX"), c._l1_ax)
                for tc in range(NTC):
                    v.wait_ge(c.sem("tP"), pe_ssm(tc))
                    c.vchain(v, v.tensor_copy(bsb[:], psm[0:16, :]))
                    v.sem_inc(c.sem("tE"), 1)
                    c.bump("tE")
                    c.vchain(v, v.tensor_copy(csb[:], psm[32:48, :]))
                    v.sem_inc(c.sem("tE"), 1)
                    c.bump("tE")
                    c.vchain(v, v.tensor_copy(dsr[:], psm[64:128, :]))
                    v.sem_inc(c.sem("tE"), 1)
                    c.bump("tE")
                    for jj in range(2 * NST):
                        n, j = jj // 2, jj % 2
                        v.wait_ge(c.sem("tP"), pe_bc(tc, jj))
                        dst = Bbc if j == 0 else Cbc
                        c.inc("tE", v.tensor_copy(dst[:, n * TC:(n + 1) * TC], psb[j][:]))
                    if L1_SCANMODE in ("front2", "front3", "front5"):
                        for _ in range(3 * NET):
                            v.sem_inc(c.sem("tV"), 1)
                            c.bump("tV")
                        continue
                    def stage_a1(et):
                        g_ = tc * NET + et
                        c.wait(v, "tP", pe_delta(tc, et))
                        if g_ >= 2:
                            c.wait(v, "tS", ts_l((g_ - 2) // NET, (g_ - 2) % NET))
                        zsl = spZ[:, (et % 2) * TC:(et % 2 + 1) * TC]
                        c.vchain(v, v.tensor_scalar(zsl, psd[et % 2][:],
                                                    dtb_sb[:, et:et + 1], None, ALU.add))
                        v.sem_inc(c.sem("tV"), 1)
                        c.bump("tV")
                        c.vchain(v, v.tensor_single_scalar(
                            spR[:, (et % 2) * TC:(et % 2 + 1) * TC], zsl, 0.0, ALU.max))

                    def stage_a2(et):
                        c.wait(v, "tS", ts_l(tc, et))
                        c.vchain(v, v.tensor_tensor(
                            dsb[:, (et % 2) * TC:(et % 2 + 1) * TC],
                            spL[:, (et % 2) * TC:(et % 2 + 1) * TC],
                            spR[:, (et % 2) * TC:(et % 2 + 1) * TC], ALU.add))
                        v.sem_inc(c.sem("tV"), 1)
                        c.bump("tV")

                    def stage_b(et):
                        base = (et % 2) * NST * TC
                        dAf = dA[:, base:base + NST * TC]
                        c.vchain(v, v.tensor_tensor(du[:], dsb[:, (et % 2) * TC:(et % 2 + 1) * TC],
                                                    hs2c[:, et * TC:(et + 1) * TC], ALU.mult))
                        c.vchain(v, v.tensor_tensor(
                            bts[:].rearrange("p (n t) -> p n t", n=NST),
                            du[:].rearrange("p (o t) -> p o t", o=1).broadcast_to([128, NST, TC]),
                            Bbc[:].rearrange("p (n t) -> p n t", n=NST), ALU.mult))
                        v.wait_ge(c.sem("tS"), ts_da(tc, et))
                        for n in range(NST):
                            ins = v.tensor_tensor_scan(
                                hts[:, n * TC:(n + 1) * TC],
                                dA[:, base + n * TC:base + (n + 1) * TC],
                                bts[:, n * TC:(n + 1) * TC],
                                state[:, et * NST + n:et * NST + n + 1],
                                ALU.mult, ALU.add)
                        c.vchain(v, ins)
                        c.vchain(v, v.tensor_copy(
                            state[:, et * NST:(et + 1) * NST].rearrange("p (n o) -> p n o", o=1),
                            hts[:].rearrange("p (n t) -> p n t", n=NST)[:, :, TC - 1:TC]))
                        c.vchain(v, v.tensor_tensor(
                            dAf.rearrange("p (n t) -> p n t", n=NST),
                            hts[:].rearrange("p (n t) -> p n t", n=NST),
                            Cbc[:].rearrange("p (n t) -> p n t", n=NST), ALU.mult))
                        w = NST * TC
                        while w > TC:
                            c.vchain(v, v.tensor_tensor(dA[:, base:base + w // 2],
                                                        dA[:, base:base + w // 2],
                                                        dA[:, base + w // 2:base + w], ALU.add))
                            w //= 2
                        c.vchain(v, v.scalar_tensor_tensor(
                            dA[:, base + TC:base + 2 * TC], hs2c[:, et * TC:(et + 1) * TC],
                            dvec_sb[:, et:et + 1], dA[:, base:base + TC], ALU.mult, ALU.add))
                        if et == 0:
                            v.wait_ge(c.sem("tS"), ts_silu(tc))
                        ins = v.tensor_tensor(y2sb[:, et * TC:(et + 1) * TC],
                                              dA[:, base + TC:base + 2 * TC],
                                              sgc[:, et * TC:(et + 1) * TC], ALU.mult)
                        c.vchain(v, ins)
                        v.sem_inc(c.sem("tV"), 1)
                        c.bump("tV")

                    for k in range(NET + 2):
                        if k < NET:
                            stage_a1(k)
                        if 1 <= k <= NET:
                            stage_a2(k - 1)
                        if 2 <= k <= NET + 1:
                            stage_b(k - 2)


def ffn_p6(c, idn):
    io = c.io


    items_b = [("x", i) for i in range(NTT)]
    quant_transpose_phase(
        c, idn, "p6b", items_b,
        lambda it: D, lambda it: NHT,
        lambda it: io["xn1_h"][it[1] * 128:(it[1] + 1) * 128, :],
        lambda it: io["xq_h"].rearrange("(a p) t -> p a t", p=128)[:, :, it[1] * 128:(it[1] + 1) * 128])
    if L1_STOP == "p6b":
        return

    gateup_p6c(c, idn)
    if L1_STOP == "p6c":
        return
    down_p6d(c, idn)


ICH = [(0, 512), (512, 512), (1024, 512), (1536, 512), (2048, 512), (2560, 256)]


def gateup_p6c(c, idn):
    nc = c.nc
    io = c.io
    sb = sbuf
    gate_e = c.cnt["gate"]

    with contextlib.ExitStack() as ph:
        wg = sb(c, ph, "c_wg", [128, NHT * I1], BF16)
        wu = sb(c, ph, "c_wu", [128, NHT * I1], BF16)
        xq = sb(c, ph, "c_xq", [128, 2 * NHT * 128], BF16)
        gb = sb(c, ph, "c_gb", [128, I1], F32)
        ub = sb(c, ph, "c_ub", [128, I1], F32)
        sg = sb(c, ph, "c_sg", [128, I1], F32)
        qb = sb(c, ph, "c_qb", [128, 2 * I1], BF16)
        sA = sb(c, ph, "c_sA", [128, I1], F32)
        sX = sb(c, ph, "c_sX", [128, I1], F32)
        sC = sb(c, ph, "c_sC", [128, I1], F32)
        sB = sb(c, ph, "c_sB", [128, 2 * (I1 // 16)], F32)
        tst = sb(c, ph, "c_tst", [128, 2 * NIT * 128], BF16)
        wraw = sb(c, ph, "c_wraw", [128, 2 * D], BF16)
        psq = [psum(c, ph, f"c_ps{i}", [128, 512]) for i in range(4)]
        pst = [psum(c, ph, f"c_pt{i}", [128, 512], BF16) for i in range(2)]

        NGR = 2 * len(ICH)

        def tp_thr(i):
            return 0 if i < 0 else NGR * (i + 1)

        def tr_thr(i):
            return 0 if i < 0 else 6 * (i + 1)

        with nc.Block("p6c") as blk:
            tV0, tP0, tP20, tE0, tE20, tS0 = c.snap("tV", "tP", "tP2", "tE", "tE2", "tS")
            qW0c, qL0c = c.snap("qW", "qL")
            ldA0 = c.ldA.snap()
            stA0 = c.stA.snap()
            ldC0 = c.ldC.snap()

            @blk.gpsimd
            def _(g):
                g.wait_ge(c.sem("gate"), gate_e)

            @blk.sync
            def _(sy):
                sy.wait_ge(c.sem("gate"), gate_e)
                for k in range(2 * NIT):
                    m, it = k // NIT, k % NIT
                    if k >= 2:
                        sy.wait_ge(c.sem("qW"), qW0c + 2 * (k - 1))
                    c.ldC.issue(sy.dma_start(
                        out=wraw[:, (k % 2) * D:(k % 2 + 1) * D],
                        in_=io["wgq_h" if m == 0 else "wuq_h"][it * 128:(it + 1) * 128, :]), k)

                def store(i):
                    sy.wait_ge(c.sem("tE2"), tE20 + 6 * (i + 1))
                    c.stA.issue(sy.dma_start(
                        out=io["hqT_h"].rearrange("(a p) t -> p a t", p=128)[:, :, i * 128:(i + 1) * 128],
                        in_=tst[:, (i % 2) * NIT * 128:(i % 2 + 1) * NIT * 128]
                        .rearrange("p (a f) -> p a f", f=128)), i % 2)
                for i in range(NTT):
                    if i >= 2:
                        c.wait(sy, "tP", tP0 + tp_thr(i - 2))
                    c.ldA.issue(sy.dma_start(
                        out=xq[:].rearrange("p (b a f) -> p b a f", b=2, a=NHT)[:, i % 2],
                        in_=io["xq_h"].rearrange("(a p) t -> p a t", p=128)[:, :, i * 128:(i + 1) * 128]), i)
                    if i >= 2:
                        store(i - 2)
                store(NTT - 2)
                store(NTT - 1)
                c.stA.wait_all(sy)
                sy.sem_inc(c.sem("gate"), 1)
                c.bump("gate")

            @blk.tensor
            def _(t):
                for k in range(2 * NIT):
                    c.ldC.wait_units(t, k, ldC0[k % 2] + 16 * (k // 2 + 1))
                    for grp in range(2):
                        gk = 2 * k + grp
                        if gk >= 2:
                            c.wait(t, "qL", qL0c + gk - 1)
                        for q_ in range(4):
                            hh = grp * 4 + q_
                            ins = t.transpose(pst[gk % 2][:, q_ * 128:(q_ + 1) * 128],
                                              wraw[:, (k % 2) * D + hh * 128:
                                                   (k % 2) * D + (hh + 1) * 128],
                                              idn[:])
                        c.inc("qW", ins)
                c.wait(t, "qL", qL0c + 4 * NIT)
                for i in range(NTT):
                    c.ldA.wait_units(t, i, ldA0[i % 2] + 16 * (i // 2 + 1))
                    k = 0
                    for side, wsb in [(0, wg), (1, wu)]:
                        for (off, wdt) in ICH:
                            kk = NGR * i + k
                            if kk >= 4:
                                c.wait(t, "tE", tE0 + (kk - 4) + 1)
                            for hh in range(NHT):
                                ins = t.matmul(
                                    psq[kk % 4][:, 0:wdt],
                                    xq[:, (i % 2) * NHT * 128 + hh * 128:(i % 2) * NHT * 128 + (hh + 1) * 128],
                                    wsb[:, hh * I1 + off:hh * I1 + off + wdt],
                                    start=(hh == 0), stop=(hh == NHT - 1))
                            c.inc("tP", ins)
                            k += 1
                    c.wait(t, "tV", tV0 + 2 * (i + 1))
                    for q in range(NIT):
                        grp = q // 4
                        if tr_thr(i - 1) + grp >= 2:
                            c.wait(t, "tE2", tE20 + tr_thr(i - 1) + grp - 1)
                        ins = t.transpose(pst[grp % 2][:, (q % 4) * 128:(q % 4 + 1) * 128],
                                          qb[:, (i % 2) * I1 + q * 128:(i % 2) * I1 + (q + 1) * 128],
                                          idn[:])
                        if q % 4 == 3 or q == NIT - 1:
                            c.inc("tP2", ins)

            @blk.vector
            def _(v):
                for i in range(NTT):
                    k = 0
                    for side, dst in [(0, gb), (1, ub)]:
                        for (off, wdt) in ICH:
                            kk = NGR * i + k
                            c.wait(v, "tP", tP0 + kk + 1)
                            ins = v.tensor_copy(dst[:, off:off + wdt], psq[kk % 4][:, 0:wdt])
                            if k < NGR - 1:
                                c.inc("tE", ins)
                            else:
                                c.vchain(v, ins)
                                v.sem_inc(c.sem("tE"), 1)
                                c.bump("tE")
                                v.sem_inc(c.sem("tV"), 1)
                                c.bump("tV")
                            k += 1
                    c.wait(v, "tS", tS0 + i + 1)
                    c.vchain(v, v.tensor_tensor(gb[:], sg[:], ub[:], ALU.mult))
                    if i >= 2:
                        c.wait(v, "tP2", tP20 + tr_thr(i - 2))
                    ins = emit_quant(c, v, gb[:], qb[:, (i % 2) * I1:(i % 2 + 1) * I1], I1,
                                     (sA[:], sX[:], sC[:], sB[:, 0:I1 // 16], sB[:, I1 // 16:]))
                    c.inc("tV", ins)

            @blk.scalar
            def _(s):
                for gk in range(4 * NIT):
                    k, grp = gk // 2, gk % 2
                    m, it = k // NIT, k % NIT
                    s.wait_ge(c.sem("qW"), qW0c + gk + 1)
                    dstw = wg if m == 0 else wu
                    c.inc("qL", s.copy(
                        dstw[:].rearrange("p (h i) -> p h i", h=NHT)
                        [:, grp * 4:(grp + 1) * 4, it * 128:(it + 1) * 128],
                        pst[gk % 2][:, 0:512].rearrange("p (h i) -> p h i", i=128)))
                for i in range(NTT):
                    s.wait_ge(c.sem("tV"), tV0 + 2 * i + 1)
                    c.inc("tS", s.activation(sg[:], gb[:], AF.Silu))
                    if i >= 2:
                        c.stA.wait_units(s, i % 2, stA0[i % 2] + 16 * (i // 2))
                    for grp in range(6):
                        s.wait_ge(c.sem("tP2"), tP20 + tr_thr(i - 1) + grp + 1)
                        cols = min(4, NIT - grp * 4) * 128
                        c.inc("tE2", s.copy(tst[:, (i % 2) * NIT * 128 + grp * 512:
                                                (i % 2) * NIT * 128 + grp * 512 + cols],
                                            pst[grp % 2][:, 0:cols]))


def down_p6d(c, idn):
    nc = c.nc
    io = c.io
    sb = sbuf
    gate_e = c.cnt["gate"]

    with contextlib.ExitStack() as ph:
        wd = sb(c, ph, "d_wd", [128, NIT * D], BF16)
        wrawd = sb(c, ph, "d_wraw", [128, 2 * I1], BF16)
        hq = sb(c, ph, "d_hq", [128, 2 * NIT * 128], BF16)
        x1b = sb(c, ph, "d_x1", [128, 2 * D], F32)
        mmb = sb(c, ph, "d_mm", [128, 2 * D], F32)
        ob = sb(c, ph, "d_ob", [128, 2 * D], F32)
        psq = [psum(c, ph, f"d_ps{i}", [128, 512]) for i in range(4)]
        pstd = [psum(c, ph, f"d_pt{i}", [128, 512], BF16) for i in range(2)]

        NGRD = 6

        with nc.Block("p6d") as blk:
            tP0, tE0 = c.snap("tP", "tE")
            qW0d, qL0d = c.snap("qW", "qL")
            qAs0 = c.qAs.snap()
            ldA0 = c.ldA.snap()
            ldB0 = c.ldB.snap()
            ldC0 = c.ldC.snap()
            stG0 = c.stG.snap()

            @blk.gpsimd
            def _(g):
                g.wait_ge(c.sem("gate"), gate_e)
                for k in range(NHT):
                    if k >= 2:
                        g.wait_ge(c.sem("qW"), qW0d + NGRD * (k - 1))
                    c.qAs.issue(g.dma_start(
                        out=wrawd[:, (k % 2) * I1:(k % 2 + 1) * I1],
                        in_=io["wdq_h"][k * 128:(k + 1) * 128, :]), k % 2)
                for i in range(NTT):
                    g.wait_ge(c.sem("tE"), tE0 + 2 * (i + 1))
                    c.stG.issue(g.dma_start(out=io["out"][i * 128:(i + 1) * 128, :],
                                            in_=ob[:, (i % 2) * D:(i % 2 + 1) * D]), i % 2)
                c.stG.wait_all(g)
                g.sem_inc(c.sem("gate"), 1)
                c.bump("gate")

            @blk.sync
            def _(sy):
                sy.wait_ge(c.sem("gate"), gate_e)
                for i in range(NTT):
                    if i >= 2:
                        c.wait(sy, "tP", tP0 + 2 * (i - 1))
                    c.ldA.issue(sy.dma_start(
                        out=hq[:].rearrange("p (b a f) -> p b a f", b=2, a=NIT)[:, i % 2],
                        in_=io["hqT_h"].rearrange("(a p) t -> p a t", p=128)[:, :, i * 128:(i + 1) * 128]), i)
                    if i >= 2:
                        c.wait(sy, "tE", tE0 + 2 * (i - 1))
                    c.ldB.issue(sy.dma_start(out=x1b[:, (i % 2) * D:(i % 2 + 1) * D],
                                             in_=io["x1"][i * 128:(i + 1) * 128, :]), i % 2)
                    c.ldC.issue(sy.dma_start(out=mmb[:, (i % 2) * D:(i % 2 + 1) * D],
                                             in_=io["mam_h"][i * 128:(i + 1) * 128, :]), i % 2)

            @blk.tensor
            def _(t):
                for k in range(NHT):
                    c.qAs.wait_units(t, k % 2, qAs0[k % 2] + 16 * (k // 2 + 1))
                    for grp in range(NGRD):
                        gk = NGRD * k + grp
                        nq_ = min(4, NIT - grp * 4)
                        if gk >= 2:
                            c.wait(t, "qL", qL0d + gk - 1)
                        for q_ in range(nq_):
                            it = grp * 4 + q_
                            ins = t.transpose(pstd[gk % 2][:, q_ * 128:(q_ + 1) * 128],
                                              wrawd[:, (k % 2) * I1 + it * 128:
                                                    (k % 2) * I1 + (it + 1) * 128],
                                              idn[:])
                        c.inc("qW", ins)
                c.wait(t, "qL", qL0d + NGRD * NHT)
                for i in range(NTT):
                    c.ldA.wait_units(t, i, ldA0[i % 2] + 16 * (i // 2 + 1))
                    for dch in range(2):
                        k = i * 2 + dch
                        if k >= 4:
                            c.wait(t, "tE", tE0 + (k - 4) + 1)
                        for it in range(NIT):
                            ins = t.matmul(
                                psq[k % 4][:],
                                hq[:, (i % 2) * NIT * 128 + it * 128:(i % 2) * NIT * 128 + (it + 1) * 128],
                                wd[:, it * D + dch * 512:it * D + (dch + 1) * 512],
                                start=(it == 0), stop=(it == NIT - 1))
                        c.inc("tP", ins)

            @blk.scalar
            def _(s):
                for gk in range(NGRD * NHT):
                    k, grp = gk // NGRD, gk % NGRD
                    nq_ = min(4, NIT - grp * 4)
                    s.wait_ge(c.sem("qW"), qW0d + gk + 1)
                    c.inc("qL", s.copy(
                        wd[:].rearrange("p (it d) -> p it d", it=NIT)
                        [:, grp * 4:grp * 4 + nq_, k * 128:(k + 1) * 128],
                        pstd[gk % 2][:, 0:nq_ * 128].rearrange("p (it d) -> p it d", d=128)))

            @blk.vector
            def _(v):
                for i in range(NTT):
                    b = i % 2
                    c.ldB.wait_units(v, i % 2, ldB0[i % 2] + 16 * (i // 2 + 1))
                    c.ldC.wait_units(v, i % 2, ldC0[i % 2] + 16 * (i // 2 + 1))
                    for dch in range(2):
                        k = i * 2 + dch
                        c.wait(v, "tP", tP0 + k + 1)
                        if i >= 2 and dch == 0:
                            c.stG.wait_units(v, i % 2, stG0[i % 2] + 16 * (i // 2))
                        c.vchain(v, v.tensor_tensor(ob[:, b * D + dch * 512:b * D + (dch + 1) * 512],
                                                    psq[k % 4][:],
                                                    x1b[:, b * D + dch * 512:b * D + (dch + 1) * 512],
                                                    ALU.add))
                        ins = v.tensor_tensor(ob[:, b * D + dch * 512:b * D + (dch + 1) * 512],
                                              ob[:, b * D + dch * 512:b * D + (dch + 1) * 512],
                                              mmb[:, b * D + dch * 512:b * D + (dch + 1) * 512],
                                              ALU.add)
                        c.vchain(v, ins)
                        v.sem_inc(c.sem("tE"), 1)
                        c.bump("tE")


def build_l1(c, idn, iot, ones128, ones128f, onescol):
    nc = c.nc
    io = c.io
    sb = sbuf
    gate_prev = c.cnt["gate"]

    Ssel = sb(c, c.es, "Ssel", [16, NST * 128], BF16)
    A_sb = sb(c, c.es, "A_sb", [128, NET * NST], F32)
    dtb_sb = sb(c, c.es, "dtb_sb", [128, NET], F32)
    dvec_sb = sb(c, c.es, "dvec_sb", [128, NET], F32)
    cwb_sb = sb(c, c.es, "cwb_sb", [128, NET * KC + NET], F32)

    sselit = sb(c, c.es, "sselit", [16, NST * 128], I32)

    with nc.Block("l1i") as blk:
        tX0, tS0 = c.snap("tX", "tS")

        @blk.gpsimd
        def _(g):
            g.wait_ge(c.sem("gate"), gate_prev)
            c.inc("tX", g.iota(sselit[:].rearrange("k (n p) -> k n p", p=128),
                               [[1, NST], [0, 128]], channel_multiplier=-1))
            c.grpA.issue(g.dma_start(out=A_sb[:], in_=io["alog_t"]), 0)
            c.grpA.issue(g.dma_start(out=dtb_sb[:], in_=io["dtb_t"]), 0)
            c.grpA.issue(g.dma_start(out=dvec_sb[:], in_=io["dvec_t"]), 0)
            c.grpA.issue(g.dma_start(out=cwb_sb[:, 0:NET * KC], in_=io["convw_t"]), 0)
            c.grpA.issue(g.dma_start(out=cwb_sb[:, NET * KC:], in_=io["convb_t"]), 0)

        @blk.vector
        def _(v):
            c.wait(v, "tX", tX0 + 1)
            c.inc("tX", v.tensor_single_scalar(Ssel[:], sselit[:], 0, ALU.is_equal))
            c.wait(v, "tS", tS0 + 1)
            c.inc("tX", v.tensor_scalar(A_sb[:], A_sb[:], -1.0, None, ALU.mult))

        @blk.scalar
        def _(s):
            c.grpA.wait_all(s)
            c.inc("tS", s.activation(A_sb[:], A_sb[:], AF.Exp))

    c._l1_ax = c.cnt["tX"]

    with contextlib.ExitStack() as ph:
        xn1fm = sb(c, ph, "xn1fm", [128, NHT * T], BF16)

        with contextlib.ExitStack() as p1s:
            xbuf = sb(c, p1s, "l1_xbuf", [128, 2 * D], F32)
            sq = sb(c, p1s, "l1_sq", [128, 2 * D], F32)
            stats = sb(c, p1s, "l1_stats", [128, 2 * 8], F32)
            xnf = sb(c, p1s, "l1_xnf", [128, 2 * D], F32)
            xn_tm = sb(c, p1s, "l1_xn", [128, 2 * D], BF16)
            pst = [psum(c, p1s, f"l1_pst{i}", [128, 512], BF16) for i in range(2)]

            with nc.Block("p5a") as blk:
                tS0, tS20, tV0, tX0, tP0, tE0 = c.snap("tS", "tS2", "tV", "tX", "tP", "tE")
                ldA0 = c.ldA.snap()
                stG0 = c.stG.snap()

                @blk.sync
                def _(sy):
                    sy.wait_ge(c.sem("gate"), gate_prev)
                    for i in range(NTT):
                        if i >= 2:
                            c.wait(sy, "tX", tX0 + (i - 2) + 1)
                        c.ldA.issue(sy.dma_start(out=xbuf[:, (i % 2) * D:(i % 2 + 1) * D],
                                                 in_=io["x1"][i * 128:(i + 1) * 128, :]), i)

                @blk.scalar
                def _(s):
                    for i in range(NTT):
                        b = i % 2
                        c.ldA.wait_units(s, i, ldA0[i % 2] + 16 * (i // 2 + 1))
                        c.inc("tS", s.activation(sq[:, b * D:(b + 1) * D],
                                                 xbuf[:, b * D:(b + 1) * D], AF.Square))
                        c.wait(s, "tV", tV0 + i + 1)
                        emit_ln_scalar(c, s, stats[:, b * 8:(b + 1) * 8])

                @blk.vector
                def _(v):
                    def emit_evac(v, i):
                        for half in range(2):
                            k = i * 2 + half
                            c.wait(v, "tP", tP0 + k + 1)
                            for q in range(4):
                                hh = half * 4 + q
                                ins = v.tensor_copy(
                                    xn1fm[:, hh * T + i * 128:hh * T + (i + 1) * 128],
                                    pst[half][:, q * 128:(q + 1) * 128])
                            c.inc("tE", ins)
                    for i in range(NTT):
                        b = i % 2
                        st_ = stats[:, b * 8:(b + 1) * 8]
                        c.wait(v, "tS", tS0 + i + 1)
                        emit_ln_stats_vec(c, v, xbuf[:, b * D:(b + 1) * D],
                                          sq[:, b * D:(b + 1) * D], st_)
                        c.wait(v, "tS2", tS20 + i + 1)
                        if i >= 2:
                            c.wait(v, "tP", tP0 + 2 * (i - 1))
                            c.stG.wait_units(v, i % 2, stG0[i % 2] + 16 * (i // 2))
                        c.vchain(v, v.tensor_scalar(xnf[:, b * D:(b + 1) * D],
                                                    xbuf[:, b * D:(b + 1) * D],
                                                    st_[:, 2:3], st_[:, 6:7],
                                                    ALU.add, ALU.mult))
                        c.inc("tX", v.tensor_copy(xn_tm[:, b * D:(b + 1) * D],
                                                  xnf[:, b * D:(b + 1) * D]))
                        if i >= 1:
                            emit_evac(v, i - 1)
                    emit_evac(v, NTT - 1)

                @blk.tensor
                def _(t):
                    for i in range(NTT):
                        c.wait(t, "tX", tX0 + i + 1)
                        for half in range(2):
                            k = i * 2 + half
                            if k >= 2:
                                c.wait(t, "tE", tE0 + k - 1)
                            for q in range(4):
                                hh = half * 4 + q
                                ins = t.transpose(
                                    pst[half][:, q * 128:(q + 1) * 128],
                                    xn_tm[:, (i % 2) * D + hh * 128:(i % 2) * D + (hh + 1) * 128],
                                    idn[:])
                            c.inc("tP", ins)

                @blk.sync
                def _(sy2):
                    pass

                @blk.gpsimd
                def _(g):
                    for i in range(NTT):
                        c.wait(g, "tX", tX0 + i + 1)
                        c.stG.issue(g.dma_start(out=io["xn1_h"][i * 128:(i + 1) * 128, :],
                                                in_=xnf[:, (i % 2) * D:(i % 2 + 1) * D]), i % 2)
                    g.wait_ge(c.sem("tE"), tE0 + 2 * NTT)
                    for hh in range(NHT):
                        c.grpB.issue(g.dma_start(out=io["xn1fm_h"][hh * 128:(hh + 1) * 128, :],
                                                 in_=xn1fm[:, hh * T:(hh + 1) * T]), 0)
                    c.grpB.wait_all(g)
                    c.stG.wait_all(g)
                    g.sem_inc(c.sem("gate"), 1)
                    c.bump("gate")

        with contextlib.ExitStack() as p2:
            ipw = sb(c, p2, "ipw", [128, NHT * 2 * ED], BF16)
            stage = sb(c, p2, "p5b_stage", [128, 4 * 512], BF16)
            psq = [psum(c, p2, f"p5b_ps{i}", [128, 512]) for i in range(4)]

            with nc.Block("p5b") as blk:
                tP0, tE0 = c.snap("tP", "tE")
                stA0 = c.stA.snap()
                gate_a = c.cnt["gate"]

                @blk.gpsimd
                def _(g):
                    g.wait_ge(c.sem("gate"), gate_a)
                    for hh in range(NHT):
                        c.grpA.issue(g.dma_start(out=ipw[:, hh * 2 * ED:(hh + 1) * 2 * ED],
                                                 in_=io["inproj_t"][hh * 128:(hh + 1) * 128, :]), 0)

                @blk.tensor
                def _(t):
                    c.grpA.wait_all(t)
                    k = 0
                    for et in range(2 * NET):
                        for tcc in range(NTC):
                            if k >= 4:
                                c.wait(t, "tE", tE0 + (k - 4) + 1)
                            for hh in range(NHT):
                                ins = t.matmul(
                                    psq[k % 4][:],
                                    ipw[:, hh * 2 * ED + et * 128:hh * 2 * ED + (et + 1) * 128],
                                    xn1fm[:, hh * T + tcc * TC:hh * T + (tcc + 1) * TC],
                                    start=(hh == 0), stop=(hh == NHT - 1))
                            c.inc("tP", ins)
                            k += 1

                @blk.vector
                def _(v):
                    k = 0
                    for et in range(2 * NET):
                        for tcc in range(NTC):
                            c.wait(v, "tP", tP0 + k + 1)
                            if k >= 4:
                                c.stA.wait_units(v, k % 4, stA0[k % 4] + 16 * (k // 4))
                            c.inc("tE", v.tensor_copy(stage[:, (k % 4) * 512:(k % 4 + 1) * 512],
                                                      psq[k % 4][:]))
                            k += 1

                @blk.sync
                def _(sy):
                    k = 0
                    for et in range(2 * NET):
                        hbm = io["hsr_h"] if et < NET else io["gate_h"]
                        row = et * 128 if et < NET else (et - NET) * 128
                        for tcc in range(NTC):
                            sy.wait_ge(c.sem("tE"), tE0 + k + 1)
                            c.stA.issue(sy.dma_start(
                                out=hbm[row:row + 128, tcc * TC:(tcc + 1) * TC],
                                in_=stage[:, (k % 4) * 512:(k % 4 + 1) * 512]), k % 4)
                            k += 1
                    c.stA.wait_all(sy)
                    sy.sem_inc(c.sem("gate"), 1)
                    c.bump("gate")

    if L1_STOP == "p5b":
        return l1_fin(c)
    gate_b = c.cnt["gate"]
    with contextlib.ExitStack() as ph:
        hbuf = sb(c, ph, "cv_h", [128, 2 * (T + 4)], BF16)
        cbuf = sb(c, ph, "cv_c", [128, 2 * T], F32)
        obuf = sb(c, ph, "cv_o", [128, 2 * T], BF16)

        with nc.Block("p5c") as blk:
            tV0, tS0, tX0 = c.snap("tV", "tS", "tX")
            ldA0 = c.ldA.snap()
            stA0 = c.stA.snap()

            @blk.sync
            def _(sy):
                sy.wait_ge(c.sem("gate"), gate_b)

                def store(et):
                    c.wait(sy, "tS", tS0 + et + 1)
                    c.stA.issue(sy.dma_start(out=io["hs2_h"][et * 128:(et + 1) * 128, :],
                                             in_=obuf[:, (et % 2) * T:(et % 2 + 1) * T]), et % 2)
                for et in range(NET):
                    if et >= 2:
                        c.wait(sy, "tV", tV0 + (et - 2) + 1)
                    c.ldA.issue(sy.dma_start(out=hbuf[:, (et % 2) * (T + 4) + 4:(et % 2) * (T + 4) + 4 + T],
                                             in_=io["hsr_h"][et * 128:(et + 1) * 128, :]), et)
                    if et >= 1:
                        store(et - 1)
                store(NET - 1)
                c.stA.wait_all(sy)
                sy.sem_inc(c.sem("gate"), 1)
                c.bump("gate")

            @blk.gpsimd
            def _(g):
                c.inc("tX", g.memset(hbuf[:, 1:4], 0.0))
                c.inc("tX", g.memset(hbuf[:, T + 4 + 1:T + 4 + 4], 0.0))

            @blk.vector
            def _(v):
                v.wait_ge(c.sem("tX"), max(tX0 + 2, c._l1_ax))
                for et in range(NET):
                    b = (et % 2) * (T + 4)
                    cb = cbuf[:, (et % 2) * T:(et % 2 + 1) * T]
                    c.ldA.wait_units(v, et, ldA0[et % 2] + 16 * (et // 2 + 1))
                    if et >= 2:
                        c.wait(v, "tS", tS0 + et - 1)
                    c.vchain(v, v.tensor_scalar(cb, hbuf[:, b + 1:b + 1 + T],
                                                cwb_sb[:, et * KC:et * KC + 1],
                                                cwb_sb[:, NET * KC + et:NET * KC + et + 1],
                                                ALU.mult, ALU.add))
                    for kk in range(1, KC - 1):
                        c.vchain(v, v.scalar_tensor_tensor(cb, hbuf[:, b + 1 + kk:b + 1 + kk + T],
                                                           cwb_sb[:, et * KC + kk:et * KC + kk + 1],
                                                           cb, ALU.mult, ALU.add))
                    c.inc("tV", v.scalar_tensor_tensor(cb, hbuf[:, b + KC:b + KC + T],
                                                       cwb_sb[:, et * KC + KC - 1:et * KC + KC],
                                                       cb, ALU.mult, ALU.add))
                    c.wait(v, "tV", c.cnt["tV"])

            @blk.scalar
            def _(s):
                for et in range(NET):
                    c.wait(s, "tV", tV0 + et + 1)
                    if et >= 2:
                        c.stA.wait_units(s, et % 2, stA0[et % 2] + 16 * (et // 2))
                    c.inc("tS", s.activation(obuf[:, (et % 2) * T:(et % 2 + 1) * T],
                                             cbuf[:, (et % 2) * T:(et % 2 + 1) * T], AF.Silu))

    if L1_STOP == "p5c":
        return l1_fin(c)
    scan_p5d(c, ones128f, Ssel, A_sb, dtb_sb, dvec_sb)
    if L1_STOP == "p5d":
        return l1_fin(c)

    gate_d = c.cnt["gate"]
    with contextlib.ExitStack() as ph:
        opw = sb(c, ph, "opw", [128, NET * D], BF16)
        y2b = sb(c, ph, "y2b", [128, 2 * NET * TC], BF16)
        mst = sb(c, ph, "mst", [128, 4 * 512], F32)
        psq = [psum(c, ph, f"p5e_ps{i}", [128, 512]) for i in range(4)]

        with nc.Block("p5e") as blk:
            tP0, tE0 = c.snap("tP", "tE")
            ldA0 = c.ldA.snap()
            stA0 = c.stA.snap()

            @blk.gpsimd
            def _(g):
                g.wait_ge(c.sem("gate"), gate_d)
                for et in range(NET):
                    c.grpA.issue(g.dma_start(out=opw[:, et * D:(et + 1) * D],
                                             in_=io["outproj_t"][et * 128:(et + 1) * 128, :]), 0)

            @blk.sync
            def _(sy):
                sy.wait_ge(c.sem("gate"), gate_d)

                def store(k):
                    sy.wait_ge(c.sem("tE"), tE0 + k + 1)
                    t128g = k // 2
                    dch = k % 2
                    c.stA.issue(sy.dma_start(
                        out=io["mam_h"][t128g * 128:(t128g + 1) * 128, dch * 512:(dch + 1) * 512],
                        in_=mst[:, (k % 4) * 512:(k % 4 + 1) * 512]), k % 4)
                for tcc in range(NTC):
                    if tcc >= 2:
                        c.wait(sy, "tP", tP0 + (tcc - 1) * 8)
                    for hh in range(2):
                        c.ldA.issue(sy.dma_start(
                            out=y2b[:].rearrange("p (b a f) -> p b a f", b=2, a=NET)[:, tcc % 2, hh * 8:(hh + 1) * 8],
                            in_=io["y2_h"].rearrange("(a p) t -> p a t", p=128)[:, hh * 8:(hh + 1) * 8,
                                                                               tcc * TC:(tcc + 1) * TC]), tcc)
                    if tcc >= 1:
                        for k in range((tcc - 1) * 8, tcc * 8):
                            store(k)
                for k in range((NTC - 1) * 8, NTC * 8):
                    store(k)
                c.stA.wait_all(sy)
                sy.sem_inc(c.sem("gate"), 1)
                c.bump("gate")

            @blk.tensor
            def _(t):
                c.grpA.wait_all(t)
                k = 0
                for tcc in range(NTC):
                    c.ldA.wait_units(t, tcc, ldA0[tcc % 2] + 32 * (tcc // 2 + 1))
                    for t128 in range(4):
                        for dch in range(2):
                            if k >= 4:
                                c.wait(t, "tE", tE0 + (k - 4) + 1)
                            for et in range(NET):
                                ins = t.matmul(
                                    psq[k % 4][:],
                                    y2b[:, (tcc % 2) * NET * TC + et * TC + t128 * 128:
                                           (tcc % 2) * NET * TC + et * TC + (t128 + 1) * 128],
                                    opw[:, et * D + dch * 512:et * D + (dch + 1) * 512],
                                    start=(et == 0), stop=(et == NET - 1))
                            c.inc("tP", ins)
                            k += 1

            @blk.vector
            def _(v):
                k = 0
                for tcc in range(NTC):
                    for t128 in range(4):
                        for dch in range(2):
                            c.wait(v, "tP", tP0 + k + 1)
                            if k >= 4:
                                c.stA.wait_units(v, k % 4, stA0[k % 4] + 16 * (k // 4))
                            c.inc("tE", v.tensor_copy(mst[:, (k % 4) * 512:(k % 4 + 1) * 512],
                                                      psq[k % 4][:]))
                            k += 1

    if L1_STOP == "p5e":
        return l1_fin(c)
    ffn_p6(c, idn)
    l1_fin(c)


def l1_fin(c):
    gfin = c.cnt["gate"]
    with c.nc.Block("l1_fin") as blk:
        @blk.sync
        def _(sy):
            sy.wait_ge(c.sem("gate"), gfin)


def _prep_weights_l0(inputs):
    f32 = np.float32
    asc = np.ascontiguousarray
    w = {}
    w["wq"] = asc(inputs["L0_attn_Wq"])
    w["wk"] = asc(inputs["L0_attn_Wk"])
    w["wv"] = asc(inputs["L0_attn_Wv"])
    w["wo"] = asc(inputs["L0_attn_Wo"])
    w["bq_t"] = asc(inputs["L0_attn_bq"].reshape(NHT, 128).T)
    w["bk_t"] = asc(inputs["L0_attn_bk"].reshape(NHT, 128).T)
    w["bv_r"] = asc(inputs["L0_attn_bv"].reshape(1, D))
    w["bo_r"] = asc(inputs["L0_attn_bo"].reshape(1, D))
    w["gamma_r"] = asc(inputs["L0_attn_gamma"].reshape(1, D))
    w["beta_r"] = asc(inputs["L0_attn_beta"].reshape(1, D))
    w["gate_w"] = asc(inputs["L0_ffn_gate_w"])
    w["gate_b_r"] = asc(inputs["L0_ffn_gate_b"].reshape(1, NE))
    w["w1"] = asc(inputs["L0_ffn_w1"].reshape(NE, NHT, 128, 2, FH)
                  .transpose(0, 3, 2, 1, 4).reshape(NE * 2 * 128, NHT * FH))
    w["b1r"] = asc(inputs["L0_ffn_b1"].reshape(NE, 32, 128)
                   .transpose(0, 2, 1).reshape(NE * 128, 32))
    w["w2"] = asc(inputs["L0_ffn_w2"].reshape(NE, 32, 128, D)
                  .transpose(0, 2, 1, 3).reshape(NE * 128, 32 * D))
    w["b2"] = asc(inputs["L0_ffn_b2"])
    return {k: v.astype(f32, copy=False) for k, v in w.items()}


def _prep_weights_l1(inputs):
    f32 = np.float32
    asc = np.ascontiguousarray
    w = {}
    w["inproj_t"] = asc(inputs["L1_attn_in_proj_w"].T)
    w["convw_t"] = asc(inputs["L1_attn_conv_w"].reshape(NET, 128, KC).transpose(1, 0, 2).reshape(128, NET * KC))
    w["convb_t"] = asc(inputs["L1_attn_conv_b"].reshape(NET, 128).T)
    xp = inputs["L1_attn_x_proj_w"]
    pad = np.zeros((ED, 128), dtype=f32)
    pad[:, 0:16] = xp[64:80, :].T
    pad[:, 32:48] = xp[80:96, :].T
    pad[:, 64:128] = xp[0:64, :].T
    w["xproj_pad"] = pad
    w["dtw_t"] = asc(inputs["L1_attn_dt_w"].T)
    w["dtb_t"] = asc(inputs["L1_attn_dt_b"].reshape(NET, 128).T)
    w["alog_t"] = asc(inputs["L1_attn_A_log"].reshape(NET, 128, NST).transpose(1, 0, 2).reshape(128, NET * NST))
    w["dvec_t"] = asc(inputs["L1_attn_D"].reshape(NET, 128).T)
    w["outproj_t"] = asc(inputs["L1_attn_out_proj_w"].T)
    w["wgate"] = asc(inputs["L1_ffn_w_gate"])
    w["wup"] = asc(inputs["L1_ffn_w_up"])
    w["wdown"] = asc(inputs["L1_ffn_w_down"])
    return {k: v.astype(f32, copy=False) for k, v in w.items()}


FUSED = True


def kernel(**inputs):
    x_full = np.ascontiguousarray(np.asarray(inputs["L0_attn_x"], dtype=np.float32))
    wl0 = _prep_weights_l0(inputs)
    wl1 = _prep_weights_l1(inputs)

    if FUSED:
        nc, _ = build(do_l0=True, do_l1=True)
        in_maps = []
        for b in range(8):
            m = {"x": np.ascontiguousarray(x_full[b])}
            m.update(wl0)
            m.update(wl1)
            in_maps.append(m)
        res = run_bass_kernel_spmd(nc, in_maps, core_ids=list(range(8)))
        out = np.stack([res.results[b]["out"] for b in range(8)], axis=0)
        return out.astype(np.float32)

    nc, _ = build(do_l0=True, do_l1=False)
    in_maps = []
    for b in range(8):
        m = {"x": np.ascontiguousarray(x_full[b])}
        m.update(wl0)
        in_maps.append(m)
    res = run_bass_kernel_spmd(nc, in_maps, core_ids=list(range(8)))
    x1 = np.stack([res.results[b]["x1"] for b in range(8)], axis=0)

    nc2, _ = build(do_l0=False, do_l1=True)
    in_maps2 = []
    for b in range(8):
        m = {"x1": np.ascontiguousarray(x1[b])}
        m.update(wl1)
        in_maps2.append(m)
    res2 = run_bass_kernel_spmd(nc2, in_maps2, core_ids=list(range(8)))
    out = np.stack([res2.results[b]["out"] for b in range(8)], axis=0)
    return out.astype(np.float32)
